# version 02
# baseline (speedup 1.0000x reference)
import contextlib
import numpy as np
import ml_dtypes

import concourse.bass as bass
import concourse.mybir as mybir
from concourse.bass_utils import run_bass_kernel_spmd

F32 = mybir.dt.float32
BF16 = mybir.dt.bfloat16
AF = mybir.ActivationFunctionType
ALU = mybir.AluOpType
AX = mybir.AxisListType.X
BF = ml_dtypes.bfloat16

B, S, D = 2, 4096, 2048
DH = 64
TOK = 1024
RG4 = [[0, 1, 2, 3], [4, 5, 6, 7]]
RG8 = [[0, 1, 2, 3, 4, 5, 6, 7]]
RND = 12582912.0
SWAP = [i ^ 1 for i in range(32)]


def host_tables():
    t = {}
    inv_freq = (np.float32(1.0) /
                (np.float32(10000.0) ** (np.arange(0, DH, 2, dtype=np.float32) / np.float32(DH)))).astype(np.float32)
    pos = np.arange(S, dtype=np.float32)
    freqs = (pos[:, None] * inv_freq[None, :]).astype(np.float32)
    cos = np.cos(freqs).astype(np.float32)
    sin = np.sin(freqs).astype(np.float32)
    cos_dup = np.repeat(cos, 2, axis=1)
    sin_sgn = np.empty((S, DH), np.float32)
    sin_sgn[:, 0::2] = -sin
    sin_sgn[:, 1::2] = sin
    t["cos_t"] = np.ascontiguousarray(np.tile(cos_dup.T, (2, 1)))
    t["sin_t"] = np.ascontiguousarray(np.tile(sin_sgn.T, (2, 1)))
    p = np.arange(128)
    t["tri"] = (p[None, :] >= p[:, None]).astype(np.float32)
    f = np.arange(512, dtype=np.float32)
    t["dd"] = (f[None, :] - np.arange(128, dtype=np.float32)[:, None]).astype(np.float32)
    t["ident_bf"] = np.eye(128, dtype=np.float32).astype(BF)
    t["ident_f"] = np.eye(128, dtype=np.float32)
    return t


_TABLES = host_tables()


def host_inputs(inp, core):
    b, r = core // 4, core % 4
    x = np.asarray(inp["L0_attn_hidden_states"])
    Wq = np.asarray(inp["L0_attn_Wq"]); Wk = np.asarray(inp["L0_attn_Wk"])
    Wv = np.asarray(inp["L0_attn_Wv"]); Wo = np.asarray(inp["L0_attn_Wo"])
    bo = np.asarray(inp["L0_attn_bo"])
    Wg = np.asarray(inp["L0_ffn_W_g"]); Wu = np.asarray(inp["L0_ffn_W_u"]); Wd = np.asarray(inp["L0_ffn_W_d"])
    Wgu1 = np.asarray(inp["L1_ffn_w_gate_up"]); Wp1 = np.asarray(inp["L1_ffn_w_proj"])

    d = dict(_TABLES)
    d["x_full"] = np.ascontiguousarray(x[b])
    d["x_own"] = np.ascontiguousarray(x[b, 1024 * r:1024 * (r + 1)])
    qcols = []
    for t in range(4):
        for hh in (t, t + 4):
            g = 8 * r + hh
            qcols.extend(range(64 * g, 64 * g + 64))
    d["wqs"] = np.ascontiguousarray(Wq[:, qcols])
    d["wks"] = np.ascontiguousarray(Wk[:, 128 * r:128 * (r + 1)])
    d["wvs"] = np.ascontiguousarray(Wv[:, 128 * r:128 * (r + 1)])
    d["wos"] = np.ascontiguousarray(Wo[512 * r:512 * (r + 1), :])
    d["bo"] = np.ascontiguousarray(np.broadcast_to(bo[None, :], (1, D)))
    hs = np.arange(1, 33, dtype=np.float32) * np.float32(8.0) / np.float32(32.0)
    slopes = (np.float32(2.0) ** (-hs)).astype(np.float32)
    sl = slopes[8 * r:8 * r + 8].astype(np.float64)
    d["slopes_neg"] = np.ascontiguousarray(
        np.tile(-sl[None, :].astype(np.float32), (128, 1)))
    tab = np.zeros((128, 8 * 32), np.float64)
    for h in range(8):
        for dlt in range(-3, 29):
            tab[:, h * 32 + dlt + 3] = -sl[h] * 128.0 * dlt / 240.0
    d["bias_tab"] = np.ascontiguousarray(tab.astype(np.float32))
    d["wg_sh"] = np.ascontiguousarray(
        np.concatenate([Wg.T[256 * core:256 * (core + 1), :],
                        Wu.T[256 * core:256 * (core + 1), :]], axis=1))
    d["wd_sh"] = np.ascontiguousarray(Wd.T[1024 * core:1024 * (core + 1), :])
    d["wgu_sh"] = np.ascontiguousarray(Wgu1[2048 * core:2048 * (core + 1), :])
    d["wp_sh"] = np.ascontiguousarray(Wp1[:, 1024 * core:1024 * (core + 1)])
    return d


class Ctr:
    def __init__(self, h):
        self.h = h
        self.n = 0

    def inc(self, k=1):
        self.n += k
        return self.n


def wait(e, c, n=None):
    n = c.n if n is None else n
    assert n <= c.n, f"wait on future inc: {n} > {c.n}"
    if n > 0:
        e.wait_ge(c.h, n)


class DmaStream:

    def __init__(self, nc, ex, name, nbuf):
        self.sems = [Ctr(ex.enter_context(nc.semaphore(f"{name}{i}"))) for i in range(nbuf)]
        self.nbuf = nbuf
        self.seq = 0

    def issue(self, e, out, in_, **kw):
        c = self.sems[self.seq % self.nbuf]
        e.dma_start(out=out, in_=in_, **kw).then_inc(c.h, 16)
        c.inc(16)
        self.seq += 1
        return self.seq - 1

    def wait_seq(self, e, seq):
        assert 0 <= seq < self.seq
        c = self.sems[seq % self.nbuf]
        e.wait_ge(c.h, 16 * (seq // self.nbuf + 1))


class DmaBatch:

    def __init__(self, nc, ex, name):
        self.c = Ctr(ex.enter_context(nc.semaphore(name)))

    def issue(self, e, out, in_, **kw):
        e.dma_start(out=out, in_=in_, **kw).then_inc(self.c.h, 16)
        self.c.inc(16)

    def wait_all(self, e):
        if self.c.n:
            e.wait_ge(self.c.h, self.c.n)


def build_nc(stage=99, taps=()):
    nc = bass.Bass("TRN2", target_bir_lowering=False, debug=False)
    ex = contextlib.ExitStack()
    taps = set(taps)

    def din(name, shape, dt=F32):
        return nc.dram_tensor(name, shape, dt, kind="ExternalInput").ap()

    def dout(name, shape, dt=F32):
        return nc.dram_tensor(name, shape, dt, kind="ExternalOutput").ap()

    def dint(name, shape, dt=F32):
        return nc.dram_tensor(name, shape, dt, kind="Internal").ap()

    def mk_sb(stk):
        def f(name, shape, dt, side="left"):
            return stk.enter_context(nc.sbuf_tensor(name, shape, dt, side=side))
        return f

    def mk_ps(stk):
        def f(name, shape, dt=F32):
            return stk.enter_context(nc.psum_tensor(name, shape, dt))
        return f

    sb = mk_sb(ex)

    def sem(name):
        return Ctr(ex.enter_context(nc.semaphore(name)))

    x_full = din("x_full", [S, D])
    x_own = din("x_own", [TOK, D])
    wqs_i = din("wqs", [D, 512]); wks_i = din("wks", [D, 128]); wvs_i = din("wvs", [D, 128])
    wos_i = din("wos", [512, D]); bo_i = din("bo", [1, D])
    wg_sh = din("wg_sh", [256, 16384]); wd_sh = din("wd_sh", [1024, 2048])
    wgu_sh = din("wgu_sh", [2048, 2048]); wp_sh = din("wp_sh", [2048, 1024])
    cos_i = din("cos_t", [128, S]); sin_i = din("sin_t", [128, S])
    tri_i = din("tri", [128, 128]); dd_i = din("dd", [128, 512])
    idb_i = din("ident_bf", [128, 128], BF16); idf_i = din("ident_f", [128, 128])
    sln_i = din("slopes_neg", [128, 8]); btab_i = din("bias_tab", [128, 256])

    out_own = dout("out_own", [TOK, D])
    tap = {}
    _tapspec = {
        "q": ([128, 4 * S], BF16), "k": ([128, S], BF16), "v": ([128, 32 * 130], BF16),
        "ctx": ([128, 4 * S], BF16), "x1": ([TOK, D], F32),
        "xn_own": ([128, 8 * D], BF16), "nq": ([128, 16 * TOK], BF16),
        "xn1": ([TOK, D], BF16), "gram": ([512, D], F32), "pown": ([128, 4 * D], BF16),
        "ffn0": ([TOK, D], F32),
        "rsout": ([TOK, D], F32), "bob": ([128, D], F32),
        "x1a": ([TOK, D], F32), "x1b": ([TOK, D], F32),
    }
    for t in taps:
        tap[t] = dout("tap_" + t, *_tapspec[t])

    sgu_in = dint("sgu_in", [256, 16384], BF16); sgu_out = dint("sgu_out", [2048, 16384], BF16)
    sd_in = dint("sd_in", [1024, 2048], BF16); sd_out = dint("sd_out", [8192, 2048], BF16)
    qgu_in = dint("qgu_in", [2048, 2048], BF16); qgu_out = dint("qgu_out", [16384, 2048], BF16)
    qp_in = dint("qp_in", [1024, 2048], BF16); qp_out = dint("qp_out", [8192, 2048], BF16)
    rs_in = dint("rs_in", [S, D]); rs_out = dint("rs_out", [TOK, D])
    ffn_sp = dint("ffn_sp", [TOK, D])
    xg_in = dint("xg_in", [TOK, D], BF16); xg_out = dint("xg_out", [S, D], BF16)
    a2a_in = dint("a2a_in", [S, 512], BF16); a2a_out = dint("a2a_out", [S, 512], BF16)
    pt_in = dint("pt_in", [8192, 128], BF16); pt_out = dint("pt_out", [32768, 128], BF16)

    sv = sem("sv")
    ss = sem("ss")
    stn = sem("stn")
    sgp = sem("sgp")

    ld_const = DmaBatch(nc, ex, "ldc")
    tbat = {t: DmaBatch(nc, ex, "tb_" + t) for t in taps}
    ld_wc = DmaBatch(nc, ex, "ldw")
    st_out = DmaBatch(nc, ex, "sto")
    cc = {nm: sem("cc_" + nm) for nm in
          ("sgu", "sd", "qgu", "qp", "rs", "xg", "a2a", "pt")}

    KT = sb("KT", [128, S], BF16)
    VE = sb("VE", [128, 32 * 130], BF16)
    idb = sb("idb", [128, 128], BF16)
    idf = sb("idf", [128, 128], F32)
    ones = sb("ones", [128, 128], F32)
    smalls = sb("smalls", [128, 64], F32)
    cbias = sb("cbias", [128, 8], F32)
    stQT = contextlib.ExitStack()
    QT = mk_sb(stQT)("QT", [128, 4 * S], BF16, side="right")

    stW = contextlib.ExitStack()
    sbW = mk_sb(stW)
    wq_sb = sbW("wq_sb", [128, 16 * 512], BF16, side="right")
    wk_sb = sbW("wk_sb", [128, 16 * 128], BF16, side="right")
    wv_sb = sbW("wv_sb", [128, 16 * 128], BF16, side="right")

    blk = ex.enter_context(nc.Block())

    def on(engine):
        def deco(fn):
            getattr(blk, engine)(fn)
        return deco

    @on("sync")
    def _(e):
        ld_const.issue(e, idb[:], idb_i[:, :])
        ld_const.issue(e, idf[:], idf_i[:, :])

    @on("gpsimd")
    def _(e):
        for dst, src in [
            (wq_sb[:].rearrange("p (a q) -> p a q", a=16), wqs_i[:, :].rearrange("(a p) q -> p a q", p=128)),
            (wk_sb[:].rearrange("p (a q) -> p a q", a=16), wks_i[:, :].rearrange("(a p) q -> p a q", p=128)),
            (wv_sb[:].rearrange("p (a q) -> p a q", a=16), wvs_i[:, :].rearrange("(a p) q -> p a q", p=128)),
        ]:
            ld_wc.issue(e, dst, src)

    @on("vector")
    def _(e):
        e.memset(ones[:], 1.0)
        e.memset(VE[:].rearrange("p (a b) -> p a b", b=65)[:, :, 64:65], 1.0)
        for c, val in [(0, 1e-5), (1, 1e-8), (2, -30.0), (3, 1.1920929e-07)]:
            e.memset(cbias[:, c:c + 1], val)
        e.engine_nop().then_inc(sv.h, 1); sv.inc()
    consts_v = sv.n

    if stage >= 3:
        st0 = contextlib.ExitStack()
        sb0 = mk_sb(st0)
        sgn = [sb0(f"sgn{i}", [128, 8192], BF16, side="right") for i in range(2)]
        ld_sgn = DmaStream(nc, ex, "lsgn", 2)
        st_sgn = DmaStream(nc, ex, "ssgn", 2)
        jobs = [("gu", tt, hh) for tt in range(2) for hh in range(2)] + \
               [("d", tt, 0) for tt in range(8)]
        sgn_done = []
        for j, (kind, tt, hh) in enumerate(jobs):
            @on("gpsimd")
            def _(e, j=j, kind=kind, tt=tt, hh=hh):
                if j >= 2:
                    wait(e, ss, sgn_done[j - 2])
                    st_sgn.wait_seq(e, j - 2)
                if kind == "gu":
                    ld_sgn.issue(e, sgn[j % 2][:], wg_sh[128 * tt:128 * (tt + 1), 8192 * hh:8192 * (hh + 1)])
                else:
                    ld_sgn.issue(e, sgn[j % 2][:, :2048], wd_sh[128 * tt:128 * (tt + 1), :])

            @on("scalar")
            def _(e, j=j, kind=kind):
                ld_sgn.wait_seq(e, j)
                if j >= 2:
                    st_sgn.wait_seq(e, j - 2)
                if kind == "gu":
                    e.activation(sgn[j % 2][:], sgn[j % 2][:], AF.Sign).then_inc(ss.h, 1)
                else:
                    e.activation(sgn[j % 2][:, :2048], sgn[j % 2][:, :2048], AF.Sign).then_inc(ss.h, 1)
                ss.inc()
            sgn_done.append(ss.n)

            @on("gpsimd")
            def _(e, j=j, kind=kind, tt=tt, hh=hh):
                wait(e, ss, sgn_done[j])
                if kind == "gu":
                    st_sgn.issue(e, sgu_in[128 * tt:128 * (tt + 1), 8192 * hh:8192 * (hh + 1)], sgn[j % 2][:])
                else:
                    st_sgn.issue(e, sd_in[128 * tt:128 * (tt + 1), :], sgn[j % 2][:, :2048])

        @on("gpsimd")
        def _(e):
            for j in range(len(jobs)):
                st_sgn.wait_seq(e, j)
            e.collective_compute("AllGather", ALU.bypass, replica_groups=RG8,
                                 ins=[sgu_in.tensor.ap().opt()],
                                 outs=[sgu_out.tensor.ap().opt()]).then_inc(cc["sgu"].h, 1)
            cc["sgu"].inc()
            e.collective_compute("AllGather", ALU.bypass, replica_groups=RG8,
                                 ins=[sd_in.tensor.ap().opt()],
                                 outs=[sd_out.tensor.ap().opt()]).then_inc(cc["sd"].h, 1)
            cc["sd"].inc()
        st0.close()

    st1 = contextlib.ExitStack()
    sb1 = mk_sb(st1)
    ps1 = mk_ps(st1)
    xt = [sb1(f"xt{i}", [128, D], F32, side="right") for i in range(2)]
    xnt = [sb1(f"xnt{i}", [128, D], BF16, side="right") for i in range(2)]
    xnf = sb1("xnf", [128, 16 * 512], BF16, side="right")
    cs_sl = sb1("cs_sl", [128, 512], F32, side="right")
    sn_sl = sb1("sn_sl", [128, 512], F32, side="right")
    qshu = sb1("qshu", [128, 512], F32, side="right")
    qtmp = sb1("qtmp", [128, 512], F32, side="right")
    pst = [ps1(f"pst{i}", [128, 2048], BF16) for i in range(2)]
    psq = ps1("psq", [128, 512], F32)
    psk = ps1("psk", [128, 512], F32)
    psv = ps1("psv", [128, 128], F32)

    ld_x = DmaStream(nc, ex, "lx", 2)
    ld_cs = DmaStream(nc, ex, "lcs", 2)

    if stage >= 3:
        @on("sync")
        def _(e):
            st_sgn.wait_seq(e, st_sgn.seq - 1)
            st_sgn.wait_seq(e, st_sgn.seq - 2)

    xn_hist = []
    var_hist = []
    evac_done = [0, 0]
    rope_done = 0
    v_last_evac = 0

    for ch in range(8):
        t0 = 512 * ch

        @on("sync")
        def _(e, t0=t0, rope_prev=rope_done):
            if rope_prev:
                wait(e, sv, rope_prev)
            ld_cs.issue(e, cs_sl[:], cos_i[:, t0:t0 + 512])
            ld_cs.issue(e, sn_sl[:], sin_i[:, t0:t0 + 512])
        cs_seq = ld_cs.seq - 1

        for tt in range(4):
            gi = 4 * ch + tt
            cb = 8 * (gi % 2)

            @on("sync")
            def _(e, tt=tt, t0=t0, gi=gi):
                if gi >= 2:
                    wait(e, sv, xn_hist[gi - 2])
                ld_x.issue(e, xt[tt % 2][:], x_full[t0 + 128 * tt:t0 + 128 * (tt + 1), :])

            @on("scalar")
            def _(e, tt=tt, gi=gi, cb=cb, evp=evac_done[tt % 2]):
                ld_x.wait_seq(e, gi)
                if gi >= 2:
                    wait(e, sv, var_hist[gi - 2])
                    wait(e, sv, evp)
                e.activation(xnt[tt % 2][:], xt[tt % 2][:], AF.Square,
                             accum_out=smalls[:, cb + 1:cb + 2]).then_inc(ss.h, 1)
                ss.inc()
            sq_n = ss.n

            @on("vector")
            def _(e, tt=tt, gi=gi, sq_n=sq_n, cb=cb):
                ld_x.wait_seq(e, gi)
                e.reduce_sum(smalls[:, cb:cb + 1], xt[tt % 2][:], AX)
                wait(e, ss, sq_n)
                e.drain()
                e.tensor_scalar(smalls[:, cb:cb + 1], smalls[:, cb:cb + 1], 1.0 / D, None, ALU.mult)
                e.drain()
                e.tensor_tensor(smalls[:, cb + 2:cb + 3], smalls[:, cb:cb + 1],
                                smalls[:, cb:cb + 1], ALU.mult)
                e.drain()
                e.scalar_tensor_tensor(smalls[:, cb + 3:cb + 4], smalls[:, cb + 1:cb + 2], 1.0 / D,
                                       smalls[:, cb + 2:cb + 3], ALU.mult, ALU.subtract).then_inc(sv.h, 1)
                sv.inc()
            var_hist.append(sv.n)

            @on("scalar")
            def _(e, var_n=sv.n, cb=cb, first=(gi == 0)):
                wait(e, sv, var_n)
                if first:
                    wait(e, sv, consts_v)
                e.activation(smalls[:, cb + 4:cb + 5], smalls[:, cb + 3:cb + 4],
                             AF.Sqrt, bias=cbias[:, 0:1]).then_inc(ss.h, 1)
                ss.inc()
            sqr_n = ss.n

            @on("vector")
            def _(e, tt=tt, sqr_n=sqr_n, cb=cb):
                wait(e, ss, sqr_n)
                e.reciprocal(smalls[:, cb + 5:cb + 6], smalls[:, cb + 4:cb + 5])
                e.drain()
                e.tensor_scalar(xnt[tt % 2][:], xt[tt % 2][:], smalls[:, cb:cb + 1],
                                smalls[:, cb + 5:cb + 6], ALU.subtract, ALU.mult).then_inc(sv.h, 1)
                sv.inc()
            xn_hist.append(sv.n)

            @on("tensor")
            def _(e, tt=tt, xn_n=sv.n, prev_ev=evac_done[tt % 2], first=(gi == 0)):
                wait(e, sv, xn_n)
                if first:
                    ld_const.wait_all(e)
                if prev_ev:
                    wait(e, sv, prev_ev)
                for dc in range(16):
                    ins = e.transpose(pst[tt % 2][:, 128 * dc:128 * (dc + 1)],
                                      xnt[tt % 2][:, 128 * dc:128 * (dc + 1)], idb[:])
                ins.then_inc(stn.h, 1)
                stn.inc()
            tr_n = stn.n

            @on("vector")
            def _(e, tt=tt, tr_n=tr_n):
                wait(e, stn, tr_n)
                e.tensor_copy(
                    xnf[:].rearrange("p (a b) -> p a b", b=512)[:, :, 128 * tt:128 * (tt + 1)],
                    pst[tt % 2][:].rearrange("p (a b) -> p a b", b=128)).then_inc(sv.h, 1)
                sv.inc()
            evac_done[tt % 2] = sv.n

        xnf_n = sv.n

        for pr in range(4):
            @on("tensor")
            def _(e, pr=pr, ch=ch, xnf_n=xnf_n, rope_prev=rope_done):
                wait(e, sv, xnf_n)
                if rope_prev:
                    wait(e, sv, rope_prev)
                if pr == 0 and ch == 0:
                    ld_wc.wait_all(e)
                for dc in range(16):
                    ins = e.matmul(psq[:, :], wq_sb[:, dc * 512 + pr * 128:dc * 512 + (pr + 1) * 128],
                                   xnf[:, dc * 512:(dc + 1) * 512], start=(dc == 0), stop=(dc == 15))
                ins.then_inc(stn.h, 1)
                stn.inc()
            qmm_n = stn.n

            @on("vector")
            def _(e, pr=pr, ch=ch, qmm_n=qmm_n, cs_seq=cs_seq):
                wait(e, stn, qmm_n)
                ld_cs.wait_seq(e, cs_seq - 1)
                ld_cs.wait_seq(e, cs_seq)
                dst = QT[:, pr * S + 512 * ch: pr * S + 512 * (ch + 1)]
                e.stream_shuffle(qshu[:], psq[:, :], SWAP)
                e.tensor_tensor(qtmp[:], psq[:, :], cs_sl[:], ALU.mult)
                e.drain()
                e.tensor_tensor(qshu[:], qshu[:], sn_sl[:], ALU.mult)
                e.drain()
                e.tensor_add(dst, qtmp[:], qshu[:]).then_inc(sv.h, 1)
                sv.inc()
            rope_done = sv.n

        @on("tensor")
        def _(e, xnf_n=xnf_n, rope_prev=rope_done):
            wait(e, sv, rope_prev)
            for dc in range(16):
                ins = e.matmul(psk[:, :], wk_sb[:, dc * 128:(dc + 1) * 128],
                               xnf[:, dc * 512:(dc + 1) * 512], start=(dc == 0), stop=(dc == 15))
            ins.then_inc(stn.h, 1)
            stn.inc()
        kmm_n = stn.n

        @on("vector")
        def _(e, ch=ch, kmm_n=kmm_n):
            wait(e, stn, kmm_n)
            dst = KT[:, 512 * ch:512 * (ch + 1)]
            e.stream_shuffle(qshu[:], psk[:, :], SWAP)
            e.tensor_tensor(qtmp[:], psk[:, :], cs_sl[:], ALU.mult)
            e.drain()
            e.tensor_tensor(qshu[:], qshu[:], sn_sl[:], ALU.mult)
            e.drain()
            e.tensor_add(dst, qtmp[:], qshu[:]).then_inc(sv.h, 1)
            sv.inc()
        rope_done = sv.n

        for tt in range(4):
            @on("tensor")
            def _(e, tt=tt, xnf_n=xnf_n, vprev=v_last_evac):
                if vprev:
                    wait(e, sv, vprev)
                wait(e, sv, xnf_n)
                for dc in range(16):
                    ins = e.matmul(psv[:, :], xnf[:, dc * 512 + 128 * tt:dc * 512 + 128 * (tt + 1)],
                                   wv_sb[:, dc * 128:(dc + 1) * 128], start=(dc == 0), stop=(dc == 15))
                ins.then_inc(stn.h, 1)
                stn.inc()
            vmm_n = stn.n

            @on("vector")
            def _(e, ch=ch, tt=tt, vmm_n=vmm_n):
                wait(e, stn, vmm_n)
                jt = 4 * ch + tt
                e.tensor_copy(VE[:, jt * 130:jt * 130 + 64], psv[:, 0:64])
                e.tensor_copy(VE[:, jt * 130 + 65:jt * 130 + 129], psv[:, 64:128]).then_inc(sv.h, 1)
                sv.inc()
            v_last_evac = sv.n

    st1.close()
    stW.close()

    def tap_store(name, src_ap):
        if name in tap:
            @on("sync")
            def _(e):
                wait(e, sv, sv.n)
                wait(e, ss, ss.n)
                wait(e, stn, stn.n)
                tbat[name].issue(e, tap[name][:, :], src_ap)

    tap_store("q", QT[:])
    tap_store("k", KT[:])
    tap_store("v", VE[:])

    if stage <= 1:
        z = sb("zz", [128, D], F32)

        @on("vector")
        def _(e):
            e.memset(z[:], 0.0).then_inc(sv.h, 1); sv.inc()
        zn = sv.n

        @on("sync")
        def _(e):
            wait(e, sv, zn)
            for tt in range(8):
                st_out.issue(e, out_own[128 * tt:128 * (tt + 1), :], z[:])
            st_out.wait_all(e)
        stQT.close()
        ex.close()
        return nc

    stC = contextlib.ExitStack()
    sbC = mk_sb(stC)
    psC = mk_ps(stC)
    CTX = sbC("CTX", [128, 4 * S], BF16, side="right")
    T_b = [sbC(f"T{i}", [128, 512], F32, side="right") for i in range(2)]
    E_b = [sbC(f"E{i}", [128, 512], BF16, side="right") for i in range(2)]
    den_sb = sbC("den", [128, 512], F32, side="right")
    rb_sb = sbC("rb_sb", [128, 512], F32, side="right")
    tmpdiv = [sbC(f"tdv{i}", [128, 512], BF16, side="right") for i in range(2)]
    tri = sbC("tri_s", [128, 128], F32, side="right")
    dd = sbC("dd_s", [128, 512], F32, side="right")
    sln = sbC("sln_s", [128, 8], F32, side="right")
    btab = sbC("btab_s", [128, 256], F32, side="right")
    wos_sb = sbC("wos_sb", [128, 4 * D], BF16, side="right")
    wo_out = [sbC(f"woo{i}", [128, 512], F32, side="right") for i in range(2)]
    ps_s = psC("ps_s", [128, 512], F32)
    ps_c = psC("ps_c", [128, 512], F32)
    ps_w = psC("ps_w", [128, 1024], F32)

    ld_attc = DmaBatch(nc, ex, "ldac")
    ctx_dma = DmaStream(nc, ex, "cdx", 2)
    rs_st = DmaStream(nc, ex, "rst", 2)

    qkv_done_v = sv.n

    ffn_gen = None
    otm_st = None
    l1q_state = {}
    if stage >= 3:
        stF = contextlib.ExitStack()
        sbF = mk_sb(stF)
        n_fm = sbF("n_fm", [128, 16 * TOK], BF16)
        otm_st = DmaStream(nc, ex, "sfo", 1)

        def ffn_emit():
            stQ = contextlib.ExitStack()
            sbQ = mk_sb(stQ)
            psQ = mk_ps(stQ)
            xo_t = [sbQ(f"xo{i}", [128, D], F32, side="right") for i in range(2)]
            yq = sbQ("yq", [128, D], F32, side="right")
            q_s = sbQ("q_s", [128, D], F32, side="right")
            nq_t = [sbQ(f"nq{i}", [128, D], BF16, side="right") for i in range(2)]
            psqt = psQ("psqt", [128, 2048], BF16)
            ld_xq = DmaStream(nc, ex, "lxq", 2)
            nq_hist = []
            qvar_hist = []
            qev_hist = [0, 0]
            for tt in range(8):
                cb = 8 * (tt % 2)

                @on("sync")
                def _(e, tt=tt, q0=qkv_done_v):
                    if tt == 0:
                        wait(e, sv, q0)
                        if "st_l1" in l1q_state:
                            s = l1q_state["st_l1"]
                            s.wait_seq(e, s.seq - 1)
                            s.wait_seq(e, s.seq - 2)
                    if tt >= 2:
                        wait(e, sv, nq_hist[tt - 2])
                    ld_xq.issue(e, xo_t[tt % 2][:], x_own[128 * tt:128 * (tt + 1), :])

                @on("scalar")
                def _(e, tt=tt, cb=cb, evp=qev_hist[tt % 2]):
                    ld_xq.wait_seq(e, tt)
                    if tt >= 2:
                        wait(e, sv, qvar_hist[tt - 2])
                        wait(e, sv, evp)
                    e.activation(nq_t[tt % 2][:], xo_t[tt % 2][:], AF.Square,
                                 accum_out=smalls[:, cb + 1:cb + 2]).then_inc(ss.h, 1)
                    ss.inc()
                sq_n = ss.n

                @on("vector")
                def _(e, tt=tt, sq_n=sq_n, cb=cb):
                    ld_xq.wait_seq(e, tt)
                    e.reduce_sum(smalls[:, cb:cb + 1], xo_t[tt % 2][:], AX)
                    wait(e, ss, sq_n)
                    e.drain()
                    e.tensor_scalar(smalls[:, cb:cb + 1], smalls[:, cb:cb + 1], 1.0 / D, None, ALU.mult)
                    e.drain()
                    e.tensor_tensor(smalls[:, cb + 2:cb + 3], smalls[:, cb:cb + 1],
                                    smalls[:, cb:cb + 1], ALU.mult)
                    e.drain()
                    e.scalar_tensor_tensor(smalls[:, cb + 3:cb + 4], smalls[:, cb + 1:cb + 2],
                                           1.0 / D, smalls[:, cb + 2:cb + 3],
                                           ALU.mult, ALU.subtract).then_inc(sv.h, 1)
                    sv.inc()
                qvar_hist.append(sv.n)

                @on("scalar")
                def _(e, n=sv.n, cb=cb):
                    wait(e, sv, n)
                    e.activation(smalls[:, cb + 4:cb + 5], smalls[:, cb + 3:cb + 4],
                                 AF.Sqrt, bias=cbias[:, 1:2]).then_inc(ss.h, 1)
                    ss.inc()
                sqr_n = ss.n

                @on("vector")
                def _(e, tt=tt, sqr_n=sqr_n, cb=cb):
                    wait(e, ss, sqr_n)
                    e.reciprocal(smalls[:, cb + 5:cb + 6], smalls[:, cb + 4:cb + 5])
                    e.drain()
                    e.tensor_scalar(yq[:], xo_t[tt % 2][:], smalls[:, cb:cb + 1],
                                    smalls[:, cb + 5:cb + 6], ALU.subtract, ALU.mult)
                    e.drain()
                    e.reduce_max(smalls[:, cb + 6:cb + 7], yq[:], AX, apply_absolute_value=True)
                    e.drain()
                    e.reciprocal(smalls[:, cb + 7:cb + 8], smalls[:, cb + 6:cb + 7])
                    e.drain()
                    e.tensor_scalar(smalls[:, cb + 7:cb + 8], smalls[:, cb + 7:cb + 8],
                                    127.0, None, ALU.mult)
                    e.drain()
                    e.reciprocal(smalls[:, cb + 6:cb + 7], smalls[:, cb + 7:cb + 8])
                    e.drain()
                    e.tensor_scalar(q_s[:], yq[:], smalls[:, cb + 7:cb + 8], RND,
                                    ALU.mult, ALU.add)
                    e.drain()
                    e.tensor_scalar(q_s[:], q_s[:], RND, -128.0, ALU.subtract, ALU.max)
                    e.drain()
                    e.tensor_scalar(nq_t[tt % 2][:], q_s[:], 127.0, smalls[:, cb + 6:cb + 7],
                                    ALU.min, ALU.mult).then_inc(sv.h, 1)
                    sv.inc()
                nq_hist.append(sv.n)

                @on("tensor")
                def _(e, tt=tt, n=sv.n, prev=qev_hist[tt % 2]):
                    wait(e, sv, n)
                    if prev:
                        wait(e, sv, prev)
                    for dc in range(16):
                        ins = e.transpose(psqt[:, 128 * dc:128 * (dc + 1)],
                                          nq_t[tt % 2][:, 128 * dc:128 * (dc + 1)], idb[:])
                    ins.then_inc(stn.h, 1)
                    stn.inc()
                tr_n = stn.n

                @on("vector")
                def _(e, tt=tt, tr_n=tr_n):
                    wait(e, stn, tr_n)
                    e.tensor_copy(
                        n_fm[:].rearrange("p (a b) -> p a b", b=TOK)[:, :, 128 * tt:128 * (tt + 1)],
                        psqt[:].rearrange("p (a b) -> p a b", b=128)).then_inc(sv.h, 1)
                    sv.inc()
                qev_hist[tt % 2] = sv.n
                if tt % 2 == 1:
                    yield
            nq_all = sv.n
            if "nq" in tap:
                @on("sync")
                def _(e, n=nq_all):
                    wait(e, sv, n)
                    tbat["nq"].issue(e, tap["nq"][:, :], n_fm[:])
            stQ.close()

            sbF2 = mk_sb(stF)
            psF2 = mk_ps(stF)
            ffn_acc = sbF2("ffn_acc", [128, 16 * 512], F32)
            p_grp = sbF2("p_grp", [128, 8 * 512], BF16)
            gsil = sbF2("gsil", [128, 1024], BF16)
            wstr = [sbF2(f"wstr{i}", [128, 256], BF16) for i in range(2)]
            wdt = [sbF2(f"wdt{i}", [128, 256], BF16) for i in range(2)]
            otm = [sbF2("otm0", [128, 2048], F32)]
            ps_gu = psF2("ps_gu", [128, 1024], F32)
            ps_dn = psF2("ps_dn", [128, 1024], F32)
            ld_w = DmaStream(nc, ex, "lw", 2)
            ld_wd = DmaStream(nc, ex, "lwd", 2)
            ld_w_cons = []
            ld_wd_cons = []
            silu_hist = 0
            p_hist = 0
            dn_mm_hist = 0
            acc_ev_hist = 0
            spill_v = 0

            for th in range(2):
                for grp in range(8):
                    for s2 in range(4):
                        hc0 = 128 * (8 * grp + 2 * s2)
                        for gu in range(2):
                            for dc in range(16):
                                @on("sync")
                                def _(e, dc=dc, gu=gu, hc0=hc0,
                                      first=(th == 0 and grp == 0 and s2 == 0 and gu == 0 and dc == 0)):
                                    if first:
                                        wait(e, sv, nq_all)
                                        wait(e, cc["sgu"], cc["sgu"].n)
                                    if ld_w.seq >= 2:
                                        wait(e, stn, ld_w_cons[ld_w.seq - 2])
                                    ld_w.issue(e, wstr[ld_w.seq % 2][:],
                                               sgu_out[128 * dc:128 * (dc + 1),
                                                       8192 * gu + hc0:8192 * gu + hc0 + 256])

                                @on("tensor")
                                def _(e, th=th, gu=gu, dc=dc, seq=ld_w.seq - 1,
                                      sil=silu_hist, pn=p_hist):
                                    ld_w.wait_seq(e, seq)
                                    if dc == 0:
                                        if gu == 0 and pn:
                                            wait(e, sv, pn)
                                        if gu == 1 and sil:
                                            wait(e, ss, sil)
                                    for i in range(2):
                                        ins = e.matmul(ps_gu[:, 512 * i:512 * (i + 1)],
                                                       wstr[seq % 2][:, 128 * i:128 * (i + 1)],
                                                       n_fm[:, dc * TOK + 512 * th:dc * TOK + 512 * (th + 1)],
                                                       start=(dc == 0), stop=(dc == 15))
                                    ins.then_inc(stn.h, 1)
                                    stn.inc()
                                ld_w_cons.append(stn.n)
                            gu_mm_n = stn.n
                            if gu == 0:
                                @on("scalar")
                                def _(e, n=gu_mm_n, pn=p_hist):
                                    wait(e, stn, n)
                                    if pn:
                                        wait(e, sv, pn)
                                    e.activation(gsil[:], ps_gu[:, :], AF.Silu).then_inc(ss.h, 1)
                                    ss.inc()
                                silu_hist = ss.n
                            else:
                                @on("vector")
                                def _(e, s2=s2, n=gu_mm_n, sil=silu_hist, dmm=dn_mm_hist):
                                    wait(e, stn, n)
                                    wait(e, ss, sil)
                                    if dmm:
                                        wait(e, stn, dmm)
                                    e.tensor_mul(p_grp[:, 1024 * s2:1024 * (s2 + 1)],
                                                 gsil[:], ps_gu[:, :]).then_inc(sv.h, 1)
                                    sv.inc()
                                p_hist = sv.n
                    p_all_n = p_hist

                    for dmp in range(8):
                        for i in range(8):
                            @on("sync")
                            def _(e, grp=grp, dmp=dmp, i=i, first=(ld_wd.seq == 0)):
                                if first:
                                    wait(e, cc["sd"], cc["sd"].n)
                                if ld_wd.seq >= 2:
                                    wait(e, stn, ld_wd_cons[ld_wd.seq - 2])
                                ht = 8 * grp + i
                                ld_wd.issue(e, wdt[ld_wd.seq % 2][:],
                                            sd_out[128 * ht:128 * (ht + 1), 256 * dmp:256 * (dmp + 1)])

                            @on("tensor")
                            def _(e, dmp=dmp, i=i, seq=ld_wd.seq - 1,
                                  pn=p_all_n, accp=acc_ev_hist, spv=spill_v):
                                ld_wd.wait_seq(e, seq)
                                if i == 0:
                                    wait(e, sv, pn)
                                    if accp:
                                        wait(e, sv, accp)
                                    if dmp == 0 and spv:
                                        wait(e, sv, spv)
                                for kk in range(2):
                                    ins = e.matmul(ps_dn[:, 512 * kk:512 * (kk + 1)],
                                                   wdt[seq % 2][:, 128 * kk:128 * (kk + 1)],
                                                   p_grp[:, 512 * i:512 * (i + 1)],
                                                   start=(i == 0), stop=(i == 7))
                                ins.then_inc(stn.h, 1)
                                stn.inc()
                            ld_wd_cons.append(stn.n)
                        dn_mm_hist = stn.n

                        @on("vector")
                        def _(e, grp=grp, dmp=dmp, n=dn_mm_hist, spv=spill_v):
                            wait(e, stn, n)
                            if grp == 0 and spv:
                                wait(e, sv, spv)
                            for kk in range(2):
                                dmc = 2 * dmp + kk
                                if grp == 0:
                                    ins = e.tensor_copy(ffn_acc[:, 512 * dmc:512 * (dmc + 1)],
                                                        ps_dn[:, 512 * kk:512 * (kk + 1)])
                                else:
                                    ins = e.tensor_add(ffn_acc[:, 512 * dmc:512 * (dmc + 1)],
                                                       ffn_acc[:, 512 * dmc:512 * (dmc + 1)],
                                                       ps_dn[:, 512 * kk:512 * (kk + 1)])
                            ins.then_inc(sv.h, 1)
                            sv.inc()
                        acc_ev_hist = sv.n
                    yield
                acc_done = sv.n

                spill_ev = 0
                for tq in range(4):
                    for dmg in range(4):
                        @on("tensor")
                        def _(e, tq=tq, dmg=dmg, n=acc_done, sev=spill_ev):
                            wait(e, sv, n)
                            if sev:
                                wait(e, sv, sev)
                            for k in range(4):
                                dmc = 4 * dmg + k
                                ins = e.transpose(ps_dn[:, 128 * k:128 * (k + 1)],
                                                  ffn_acc[:, 512 * dmc + 128 * tq:512 * dmc + 128 * (tq + 1)],
                                                  idf[:])
                            ins.then_inc(stn.h, 1)
                            stn.inc()
                        tr_n = stn.n

                        @on("vector")
                        def _(e, tq=tq, dmg=dmg, tr_n=tr_n, otp=(otm_st.seq - 1)):
                            wait(e, stn, tr_n)
                            if otp >= 0:
                                otm_st.wait_seq(e, otp)
                            e.tensor_copy(
                                otm[0][:].rearrange("p (a b) -> p a b", b=128)[:, 4 * dmg:4 * dmg + 4, :],
                                ps_dn[:, 0:512].rearrange("p (a b) -> p a b", b=128)).then_inc(sv.h, 1)
                            sv.inc()
                        spill_ev = sv.n
                    spill_v = sv.n

                    @on("sync")
                    def _(e, th=th, tq=tq, n=spill_v):
                        wait(e, sv, n)
                        otm_st.issue(e, ffn_sp[512 * th + 128 * tq:512 * th + 128 * (tq + 1), :],
                                     otm[0][:])
                    yield
            while True:
                yield

        ffn_gen = ffn_emit()

    l1q_gen = None
    if stage >= 4:
        def l1q_emit():
            stQ1 = contextlib.ExitStack()
            sbQ1 = mk_sb(stQ1)
            psQ1 = mk_ps(stQ1)
            wq_t = [sbQ1(f"l1w{i}", [128, 2048], BF16, side="right") for i in range(2)]
            qf = sbQ1("l1qf", [128, 2048], F32, side="right")
            wh_t = sbQ1("l1wh", [128, 2048], BF16, side="right")
            qtr = sbQ1("l1tr", [128, 2048], BF16, side="right")
            gsc = sbQ1("l1gs", [128, 32], F32, side="right")
            sclb = sbQ1("l1sc", [128, 32], BF16, side="right")
            rsf = sbQ1("l1rs", [128, 32], F32, side="right")
            psq1 = psQ1("psq1", [128, 2048], BF16)
            ld_l1 = DmaStream(nc, ex, "l1l", 2)
            st_l1 = DmaStream(nc, ex, "l1s", 2)
            jobs = [("gu", t) for t in range(16)] + [("p", t) for t in range(16)]
            qdone = []
            trdone = []
            evdone = []
            for j, (kind, t) in enumerate(jobs):
                W = 2048 if kind == "gu" else 1024
                G = W // 64

                @on("gpsimd")
                def _(e, j=j, kind=kind, t=t, W=W):
                    if j >= 2:
                        wait(e, sv, qdone[j - 2])
                    src_t = wgu_sh if kind == "gu" else wp_sh
                    ld_l1.issue(e, wq_t[j % 2][:, :W], src_t[128 * t:128 * (t + 1), :])

                @on("vector")
                def _(e, j=j, W=W, G=G):
                    ld_l1.wait_seq(e, j)
                    wg = wq_t[j % 2][:, :W].rearrange("p (g k) -> p g k", k=64)
                    e.reduce_sum(gsc[:, :G].unsqueeze(2), wg, AX, apply_absolute_value=True)
                    e.drain()
                    e.tensor_scalar(gsc[:, :G], gsc[:, :G], 1.0 / 64.0, None, ALU.mult)
                    e.drain()
                    e.tensor_copy(sclb[:, :G], gsc[:, :G])
                    e.drain()
                    e.tensor_scalar(sclb[:, :G], sclb[:, :G], 1e-8, None, ALU.max)
                    e.drain()
                    e.reciprocal(rsf[:, :G], sclb[:, :G])
                    e.drain()
                    e.tensor_tensor(qf[:, :W].rearrange("p (g k) -> p g k", k=64), wg,
                                    rsf[:, :G].unsqueeze(2).broadcast_to([128, G, 64]), ALU.mult)
                    e.drain()
                    e.tensor_scalar(qf[:, :W], qf[:, :W], RND, RND, ALU.add, ALU.subtract)
                    e.drain()
                    e.tensor_scalar(qf[:, :W], qf[:, :W], -1.0, 1.0, ALU.max, ALU.min)
                    e.drain()
                    e.tensor_tensor(wh_t[:, :W].rearrange("p (g k) -> p g k", k=64),
                                    qf[:, :W].rearrange("p (g k) -> p g k", k=64),
                                    sclb[:, :G].unsqueeze(2).broadcast_to([128, G, 64]),
                                    ALU.mult).then_inc(sv.h, 1)
                    sv.inc()
                qdone.append(sv.n)

                @on("tensor")
                def _(e, j=j, W=W, n=qdone[j], prev=(evdone[j - 2] if j >= 2 else 0)):
                    wait(e, sv, n)
                    if prev:
                        wait(e, sv, prev)
                    for bb in range(W // 128):
                        ins = e.transpose(psq1[:, 128 * bb:128 * (bb + 1)],
                                          wh_t[:, 128 * bb:128 * (bb + 1)], idb[:])
                    ins.then_inc(stn.h, 1)
                    stn.inc()
                trdone.append(stn.n)

                @on("vector")
                def _(e, j=j, W=W, n=trdone[j], stp=(j - 1)):
                    wait(e, stn, n)
                    if stp >= 0:
                        st_l1.wait_seq(e, stp)
                    e.tensor_copy(qtr[:, :W], psq1[:, :W]).then_inc(sv.h, 1)
                    sv.inc()
                evdone.append(sv.n)

                @on("sync")
                def _(e, j=j, kind=kind, t=t, W=W, n=evdone[j]):
                    wait(e, sv, n)
                    if kind == "gu":
                        dst = qgu_in[:, 128 * t:128 * (t + 1)].rearrange("(a p) q -> p a q", p=128)
                    else:
                        dst = qp_in[:, 128 * t:128 * (t + 1)].rearrange("(a p) q -> p a q", p=128)
                    st_l1.issue(e, dst, qtr[:, :W].rearrange("p (a q) -> p a q", q=128))
                if j % 2 == 1:
                    yield
            @on("gpsimd")
            def _(e, nj=len(jobs)):
                st_l1.wait_seq(e, nj - 1)
                st_l1.wait_seq(e, nj - 2)
                e.collective_compute("AllGather", ALU.bypass, replica_groups=RG8,
                                     ins=[qgu_in.tensor.ap().opt()],
                                     outs=[qgu_out.tensor.ap().opt()]).then_inc(cc["qgu"].h, 1)
                cc["qgu"].inc()
                e.collective_compute("AllGather", ALU.bypass, replica_groups=RG8,
                                     ins=[qp_in.tensor.ap().opt()],
                                     outs=[qp_out.tensor.ap().opt()]).then_inc(cc["qp"].h, 1)
                cc["qp"].inc()
            l1q_state["st_l1"] = st_l1
            stQ1.close()
            while True:
                yield

        l1q_gen = l1q_emit()

    @on("sync")
    def _(e):
        wait(e, sv, qkv_done_v)
        for dst, src in [(tri[:], tri_i[:, :]), (dd[:], dd_i[:, :]),
                         (sln[:], sln_i[:, :]), (btab[:], btab_i[:, :])]:
            ld_attc.issue(e, dst, src)

    @on("gpsimd")
    def _(e):
        wait(e, sv, qkv_done_v)
        ld_wc.issue(e, wos_sb[:].rearrange("p (a q) -> p a q", a=4),
                    wos_i[:, :].rearrange("(a p) q -> p a q", p=128))

    stt_hist = 0
    exp_hist = [0, 0]
    ctx_hist = [0, 0]
    div_hist = 0
    den_bcast_hist = 0
    odd_dma = []
    wo_evac_hist = [0, 0]
    att_const_waited = [False]

    gens = [g for g in (ffn_gen,) if g is not None]
    if stage >= 4:
        gens = [ffn_gen, l1q_gen]
        import itertools
        def _seq():
            for _ in range(17):
                next(l1q_gen); yield
            while True:
                next(ffn_gen); yield
        _sq = _seq()
        def drain_ffn(k=1):
            for _ in range(k):
                next(_sq)
    else:
        def drain_ffn(k=1):
            if ffn_gen is not None:
                for _ in range(k):
                    next(ffn_gen)

    for qt in range(8):
        odd_seqs = []
        for h in range(8):
            drain_ffn(1 if (qt * 8 + h) % 2 == 0 else 0)
            kh = h // 4
            nblk = 4 * (qt + 1)
            for jt in range(nblk):
                sl_par = jt % 2
                dlt = 4 * qt - jt

                @on("tensor")
                def _(e, h=h, qt=qt, jt=jt, kh=kh,
                      sttp=stt_hist, divp=div_hist):
                    if jt == 0 and divp:
                        wait(e, sv, divp)
                    if sttp:
                        wait(e, sv, sttp)
                    if h == 0 and qt == 0 and jt == 0:
                        wait(e, sv, qkv_done_v)
                    e.matmul(ps_s[:, :],
                             KT[64 * kh:64 * (kh + 1), 128 * jt:128 * (jt + 1)],
                             QT[64 * kh:64 * (kh + 1), (h % 4) * S + 512 * qt:(h % 4) * S + 512 * (qt + 1)],
                             start=True, stop=True,
                             tile_position=(64 * kh, 0)).then_inc(stn.h, 1)
                    stn.inc()
                mm_n = stn.n

                @on("vector")
                def _(e, h=h, jt=jt, sl_par=sl_par, mm_n=mm_n, expp=exp_hist[sl_par]):
                    wait(e, stn, mm_n)
                    if expp:
                        wait(e, ss, expp)
                    if not att_const_waited[0]:
                        ld_attc.wait_all(e)
                        att_const_waited[0] = True
                    e.scalar_tensor_tensor(T_b[sl_par][:], dd[:], sln[:, h:h + 1],
                                           ps_s[:, :],
                                           ALU.mult, ALU.add).then_inc(sv.h, 1)
                    sv.inc()
                stt_hist = sv.n

                @on("scalar")
                def _(e, h=h, jt=jt, dlt=dlt, sl_par=sl_par, sttn=stt_hist,
                      ctxp=ctx_hist[sl_par]):
                    wait(e, sv, sttn)
                    if ctxp:
                        wait(e, stn, ctxp)
                    col = h * 32 + dlt + 3
                    e.activation(T_b[sl_par][:], T_b[sl_par][:], AF.Tanh,
                                 bias=btab[:, col:col + 1], scale=1.0 / 240.0)
                    e.drain()
                    e.activation(E_b[sl_par][:], T_b[sl_par][:], AF.Exp,
                                 bias=cbias[:, 2:3], scale=30.0).then_inc(ss.h, 1)
                    ss.inc()
                exp_hist[sl_par] = ss.n

                sub = jt - 4 * qt
                masked = sub >= 0
                if masked:
                    @on("vector")
                    def _(e, sl_par=sl_par, sub=sub, expn=exp_hist[sl_par]):
                        wait(e, ss, expn)
                        if sub > 0:
                            e.memset(E_b[sl_par][:, 0:128 * sub], 0.0)
                        e.tensor_mul(E_b[sl_par][:, 128 * sub:128 * (sub + 1)],
                                     E_b[sl_par][:, 128 * sub:128 * (sub + 1)],
                                     tri[:]).then_inc(sv.h, 1)
                        sv.inc()
                    e_ready = ("sv", sv.n)
                else:
                    e_ready = ("ss", exp_hist[sl_par])

                @on("tensor")
                def _(e, h=h, jt=jt, kh=kh, sl_par=sl_par, nblk=nblk,
                      e_ready=e_ready, divp=div_hist):
                    if e_ready[0] == "sv":
                        wait(e, sv, e_ready[1])
                    else:
                        wait(e, ss, e_ready[1])
                    if jt == 0 and divp:
                        wait(e, sv, divp)
                    e.matmul(ps_c[0:65, :], VE[:, 130 * jt + 65 * kh:130 * jt + 65 * kh + 65],
                             E_b[sl_par][:], start=(jt == 0),
                             stop=(jt == nblk - 1)).then_inc(stn.h, 1)
                    stn.inc()
                ctx_hist[sl_par] = stn.n
            ctx_done = stn.n

            @on("vector")
            def _(e, ctx_done=ctx_done, bcp=den_bcast_hist):
                wait(e, stn, ctx_done)
                if bcp:
                    wait(e, stn, bcp)
                e.tensor_copy(den_sb[64:65, 0:512], ps_c[64:65, :])
                e.drain()
                e.reciprocal(den_sb[64:65, 0:512], den_sb[64:65, 0:512]).then_inc(sv.h, 1)
                sv.inc()
            den_n = sv.n

            @on("tensor")
            def _(e, den_n=den_n):
                wait(e, sv, den_n)
                e.matmul(ps_s[0:64, 0:512], ones[64:65, 0:64], den_sb[64:65, 0:512],
                         start=True, stop=True, tile_position=(64, 0)).then_inc(stn.h, 1)
                stn.inc()
            den_bcast_hist = stn.n

            @on("vector")
            def _(e, h=h, qt=qt, bc_n=den_bcast_hist, cdp=(ctx_dma.seq - 2)):
                wait(e, stn, bc_n)
                e.tensor_copy(rb_sb[0:64, :], ps_s[0:64, 0:512])
                e.drain()
                if h % 2 == 0:
                    dst = CTX[0:64, (h // 2) * S + 512 * qt:(h // 2) * S + 512 * (qt + 1)]
                else:
                    if cdp >= 0:
                        ctx_dma.wait_seq(e, cdp)
                    dst = tmpdiv[(ctx_dma.seq) % 2][0:64, :]
                e.tensor_mul(dst, ps_c[0:64, :], rb_sb[0:64, :]).then_inc(sv.h, 1)
                sv.inc()
            div_hist = sv.n

            if h % 2 == 1:
                @on("sync")
                def _(e, h=h, qt=qt, divn=div_hist):
                    wait(e, sv, divn)
                    seq = ctx_dma.issue(
                        e, CTX[64:128, (h // 2) * S + 512 * qt:(h // 2) * S + 512 * (qt + 1)],
                        tmpdiv[ctx_dma.seq % 2][0:64, :])
                    odd_seqs.append(seq)

        wo_dep_v = div_hist
        for tt in range(4):
            for dmc in range(4):
                wi = 4 * tt + dmc
                wp = wi % 2

                @on("tensor")
                def _(e, qt=qt, tt=tt, dmc=dmc, wp=wp, wo_dep_v=wo_dep_v,
                      odd_seqs=list(odd_seqs), evp=wo_evac_hist[wp],
                      first=(qt == 0 and tt == 0 and dmc == 0)):
                    wait(e, sv, wo_dep_v)
                    if first:
                        ld_wc.wait_all(e)
                    for sq in odd_seqs:
                        ctx_dma.wait_seq(e, sq)
                    if evp:
                        wait(e, sv, evp)
                    for ac in range(4):
                        ins = e.matmul(ps_w[:, 512 * wp:512 * (wp + 1)],
                                       CTX[:, ac * S + 512 * qt + 128 * tt:ac * S + 512 * qt + 128 * (tt + 1)],
                                       wos_sb[:, ac * D + 512 * dmc:ac * D + 512 * (dmc + 1)],
                                       start=(ac == 0), stop=(ac == 3))
                    ins.then_inc(stn.h, 1)
                    stn.inc()
                wo_n = stn.n

                @on("vector")
                def _(e, wp=wp, wo_n=wo_n, stp=(rs_st.seq - 2)):
                    wait(e, stn, wo_n)
                    if stp >= 0:
                        rs_st.wait_seq(e, stp)
                    e.tensor_copy(wo_out[wp][:], ps_w[:, 512 * wp:512 * (wp + 1)]).then_inc(sv.h, 1)
                    sv.inc()
                wo_evac_hist[wp] = sv.n

                @on("sync")
                def _(e, qt=qt, tt=tt, dmc=dmc, wp=wp, evn=wo_evac_hist[wp]):
                    wait(e, sv, evn)
                    rs_st.issue(e, rs_in[512 * qt + 128 * tt:512 * qt + 128 * (tt + 1),
                                         512 * dmc:512 * (dmc + 1)], wo_out[wp][:])

    drain_ffn(40)

    tap_store("ctx", CTX[:])

    @on("gpsimd")
    def _(e):
        rs_st.wait_seq(e, rs_st.seq - 1)
        rs_st.wait_seq(e, rs_st.seq - 2)
        e.collective_compute("ReduceScatter", ALU.add, replica_groups=RG4,
                             ins=[rs_in.tensor.ap().opt()],
                             outs=[rs_out.tensor.ap().opt()]).then_inc(cc["rs"].h, 1)
        cc["rs"].inc()

    if stage >= 3 and "nq" in tap:
        @on("sync")
        def _(e):
            tbat["nq"].wait_all(e)
    if stage >= 3:
        stF.close()
    stC_right_users_v = sv.n
    stC_users_stn = stn.n
    stC.close()
    stQT.close()
    stD = contextlib.ExitStack()
    sbD = mk_sb(stD)
    psD = mk_ps(stD)
    x1 = sbD("x1", [128, 8 * D], F32)
    bob = sbD("bob", [128, D], F32, side="right")
    stB2 = contextlib.ExitStack()
    ps_b2 = mk_ps(stB2)("ps_b2", [128, 512], F32)
    ld_boc = DmaBatch(nc, ex, "lboc")
    ld_xo = DmaBatch(nc, ex, "lxo")

    @on("sync")
    def _(e, n=stC_right_users_v):
        wait(e, sv, n)
        for t in ("ctx", "q", "k", "v"):
            if t in tbat:
                tbat[t].wait_all(e)
        ld_boc.issue(e, bob[0:1, :], bo_i[:, :])
        for tt in range(8):
            ld_xo.issue(e, x1[:, tt * D:(tt + 1) * D], x_own[128 * tt:128 * (tt + 1), :])

    if "x1a" in tap:
        @on("sync")
        def _(e):
            ld_xo.wait_all(e)
            for tt in range(8):
                tbat["x1a"].issue(e, tap["x1a"][128 * tt:128 * (tt + 1), :], x1[:, tt * D:(tt + 1) * D])

    bo_prev_ev = 0
    for c4 in range(4):
        @on("tensor")
        def _(e, c4=c4, prev=bo_prev_ev, nn=stC_right_users_v):
            if c4 == 0:
                ld_boc.wait_all(e)
                wait(e, sv, nn)
            if prev:
                wait(e, sv, prev)
            e.matmul(ps_b2[:, :], ones[0:1, 0:128], bob[0:1, 512 * c4:512 * (c4 + 1)],
                     start=True, stop=True).then_inc(stn.h, 1)
            stn.inc()
        bo_mm_n = stn.n

        @on("vector")
        def _(e, c4=c4, n=bo_mm_n):
            wait(e, stn, n)
            e.tensor_copy(bob[:, 512 * c4:512 * (c4 + 1)], ps_b2[:, :]).then_inc(sv.h, 1)
            sv.inc()
        bo_prev_ev = sv.n
    bo_ev_n = sv.n

    @on("vector")
    def _(e, n=bo_ev_n):
        wait(e, sv, n)
        ld_xo.wait_all(e)
        if "x1a" in tap:
            tbat["x1a"].wait_all(e)
        for tt in range(8):
            e.tensor_add(x1[:, tt * D:(tt + 1) * D], x1[:, tt * D:(tt + 1) * D], bob[:])
        e.engine_nop().then_inc(sv.h, 1)
        sv.inc()
    x1_bo_n = sv.n
    stB2.close()

    if "x1b" in tap:
        @on("sync")
        def _(e, n=x1_bo_n):
            wait(e, sv, n)
            for tt in range(8):
                tbat["x1b"].issue(e, tap["x1b"][128 * tt:128 * (tt + 1), :], x1[:, tt * D:(tt + 1) * D])

    if stage >= 3:
        @on("gpsimd")
        def _(e, n=x1_bo_n):
            wait(e, sv, n)
            if "x1b" in tap:
                tbat["x1b"].wait_all(e)
            otm_st.wait_seq(e, otm_st.seq - 1)
            for tt in range(8):
                e.dma_start(out=x1[:, tt * D:(tt + 1) * D], in_=ffn_sp[128 * tt:128 * (tt + 1), :],
                            accum_op=ALU.add).then_inc(sgp.h, 16)
                sgp.inc(16)
        x1_f_n = sgp.n
    else:
        x1_f_n = 0

    @on("gpsimd")
    def _(e, n=x1_bo_n, fn=x1_f_n):
        wait(e, sv, n)
        if fn:
            wait(e, sgp, fn)
        wait(e, cc["rs"], cc["rs"].n)
        for tt in range(8):
            e.dma_start(out=x1[:, tt * D:(tt + 1) * D], in_=rs_out[128 * tt:128 * (tt + 1), :],
                        accum_op=ALU.add).then_inc(sgp.h, 16)
            sgp.inc(16)
    x1_acc_n = sgp.n

    if "rsout" in tap:
        @on("sync")
        def _(e):
            wait(e, cc["rs"], cc["rs"].n)
            for tt in range(8):
                tbat["rsout"].issue(e, tap["rsout"][128 * tt:128 * (tt + 1), :],
                                    rs_out[128 * tt:128 * (tt + 1), :])

    if "bob" in tap:
        @on("sync")
        def _(e, n=bo_ev_n):
            wait(e, sv, n)
            tbat["bob"].issue(e, tap["bob"][:, :], bob[:])

    if "ffn0" in tap:
        @on("sync")
        def _(e, n=x1_f_n):
            if n:
                wait(e, sgp, n)
            for tt in range(8):
                tbat["ffn0"].issue(e, tap["ffn0"][128 * tt:128 * (tt + 1), :],
                                   ffn_sp[128 * tt:128 * (tt + 1), :])

    if "x1" in tap:
        @on("sync")
        def _(e, n=x1_acc_n):
            wait(e, sgp, n)
            for tt in range(8):
                tbat["x1"].issue(e, tap["x1"][128 * tt:128 * (tt + 1), :], x1[:, tt * D:(tt + 1) * D])

    if stage >= 4:
        import dataclasses
        xsl = dint("xsl", [S, 512], BF16)
        stL = contextlib.ExitStack()
        sbL = mk_sb(stL)
        xn1_fm = sbL("xn1_fm", [128, 16 * TOK], BF16)
        ones_bf = sbL("ones_bf", [128, 1], BF16)

        @on("vector")
        def _(e, n=x1_acc_n):
            wait(e, sgp, n)
            rs_st.wait_seq(e, rs_st.seq - 1)
            rs_st.wait_seq(e, rs_st.seq - 2)
            e.memset(ones_bf[:], 1.0).then_inc(sv.h, 1); sv.inc()

        stLN = contextlib.ExitStack()
        sbLN = mk_sb(stLN)
        psLN = mk_ps(stLN)
        xnt1 = [sbLN(f"xnt1_{i}", [128, D], BF16, side="right") for i in range(2)]
        pstL = [psLN(f"pstL{i}", [128, 2048], BF16) for i in range(2)]
        st_xg = DmaStream(nc, ex, "sxg", 2)
        ln1_nq = []
        ln1_ev = [0, 0]
        for tt in range(8):
            cb = 8 * (tt % 2)

            @on("scalar")
            def _(e, tt=tt, cb=cb, n=x1_acc_n, evp=ln1_ev[tt % 2]):
                if tt == 0:
                    wait(e, sgp, n)
                if tt >= 2:
                    wait(e, sv, ln1_nq[tt - 2])
                    wait(e, sv, evp)
                    st_xg.wait_seq(e, tt - 2)
                e.activation(xnt1[tt % 2][:], x1[:, tt * D:(tt + 1) * D], AF.Square,
                             accum_out=smalls[:, cb + 1:cb + 2]).then_inc(ss.h, 1)
                ss.inc()
            sq_n = ss.n

            @on("vector")
            def _(e, tt=tt, sq_n=sq_n, cb=cb):
                wait(e, ss, sq_n)
                e.reduce_sum(smalls[:, cb:cb + 1], x1[:, tt * D:(tt + 1) * D], AX)
                e.drain()
                e.tensor_scalar(smalls[:, cb:cb + 1], smalls[:, cb:cb + 1], 1.0 / D, None, ALU.mult)
                e.drain()
                e.tensor_tensor(smalls[:, cb + 2:cb + 3], smalls[:, cb:cb + 1],
                                smalls[:, cb:cb + 1], ALU.mult)
                e.drain()
                e.scalar_tensor_tensor(smalls[:, cb + 3:cb + 4], smalls[:, cb + 1:cb + 2],
                                       1.0 / D, smalls[:, cb + 2:cb + 3],
                                       ALU.mult, ALU.subtract).then_inc(sv.h, 1)
                sv.inc()
            var_n = sv.n

            @on("scalar")
            def _(e, var_n=var_n, cb=cb):
                wait(e, sv, var_n)
                e.activation(smalls[:, cb + 4:cb + 5], smalls[:, cb + 3:cb + 4],
                             AF.Sqrt, bias=cbias[:, 0:1]).then_inc(ss.h, 1)
                ss.inc()
            sqr_n = ss.n

            @on("vector")
            def _(e, tt=tt, sqr_n=sqr_n, cb=cb):
                wait(e, ss, sqr_n)
                e.reciprocal(smalls[:, cb + 5:cb + 6], smalls[:, cb + 4:cb + 5])
                e.drain()
                e.tensor_scalar(xnt1[tt % 2][:], x1[:, tt * D:(tt + 1) * D],
                                smalls[:, cb:cb + 1], smalls[:, cb + 5:cb + 6],
                                ALU.subtract, ALU.mult).then_inc(sv.h, 1)
                sv.inc()
            ln1_nq.append(sv.n)

            @on("tensor")
            def _(e, tt=tt, n=sv.n, prev=ln1_ev[tt % 2]):
                wait(e, sv, n)
                if prev:
                    wait(e, sv, prev)
                for dc in range(16):
                    ins = e.transpose(pstL[tt % 2][:, 128 * dc:128 * (dc + 1)],
                                      xnt1[tt % 2][:, 128 * dc:128 * (dc + 1)], idb[:])
                ins.then_inc(stn.h, 1)
                stn.inc()
            tr_n = stn.n

            @on("vector")
            def _(e, tt=tt, tr_n=tr_n):
                wait(e, stn, tr_n)
                e.tensor_copy(
                    xn1_fm[:].rearrange("p (a b) -> p a b", b=TOK)[:, :, 128 * tt:128 * (tt + 1)],
                    pstL[tt % 2][:].rearrange("p (a b) -> p a b", b=128)).then_inc(sv.h, 1)
                sv.inc()
            ln1_ev[tt % 2] = sv.n

            @on("sync")
            def _(e, tt=tt, n=ln1_nq[tt]):
                wait(e, sv, n)
                st_xg.issue(e, xg_in[128 * tt:128 * (tt + 1), :], xnt1[tt % 2][:])
        xn1_fm_done = sv.n

        if "xn1" in tap:
            @on("sync")
            def _(e):
                for j in range(8):
                    st_xg.wait_seq(e, j)
                tbat["xn1"].issue(e, tap["xn1"][:, :], xg_in[:, :])

        @on("gpsimd")
        def _(e):
            for j in range(8):
                st_xg.wait_seq(e, j)
            e.collective_compute("AllGather", ALU.bypass, replica_groups=RG4,
                                 ins=[xg_in.tensor.ap().opt()],
                                 outs=[xg_out.tensor.ap().opt()]).then_inc(cc["xg"].h, 1)
            cc["xg"].inc()

        ld_xsl = DmaBatch(nc, ex, "lxsl")

        @on("sync")
        def _(e):
            wait(e, cc["xg"], cc["xg"].n)
            pid = nc.partition_id(engines=[mybir.EngineType.SP])
            r512 = (pid % 4) * 512
            for k in range(4):
                ap = xg_out[1024 * k:1024 * (k + 1), 0:512]
                ap2 = dataclasses.replace(ap, offset=ap.offset + r512)
                ld_xsl.issue(e, xsl[1024 * k:1024 * (k + 1), :], ap2)
        stLN.close()

        stF1 = contextlib.ExitStack()
        sbF1 = mk_sb(stF1)
        psF1 = mk_ps(stF1)

        def ffn1_emit():
            ffn1_acc = sbF1("ffn1_acc", [128, 16 * 512], F32)
            p1_grp = sbF1("p1_grp", [128, 8 * 512], BF16)
            gsil1 = sbF1("gsil1", [128, 1024], BF16)
            sq_bf = sbF1("sq_bf", [128, 1024], BF16)
            w1str = [sbF1(f"w1s{i}", [128, 256], BF16) for i in range(2)]
            wp1t = [sbF1(f"wp1{i}", [128, 256], BF16) for i in range(2)]
            otm1 = sbF1("otm1", [128, 2048], F32)
            stg = sbF1("stg", [128, 512], F32)
            invc = sbF1("invc", [128, 4], F32)
            ps_gu1 = psF1("ps_gu1", [128, 1024], F32)
            ps_dn1 = psF1("ps_dn1", [128, 1024], F32)
            ps_h1 = psF1("ps_h1", [1, 512], F32)
            ld_w1 = DmaStream(nc, ex, "lw1", 2)
            ld_wp1 = DmaStream(nc, ex, "lwp1", 2)
            ot1_st = DmaStream(nc, ex, "so1", 1)
            ld_w1_cons = []
            ld_wp1_cons = []
            silu_h = 0
            p_h = 0
            dn_h = 0
            acc_h = 0
            spl_v = 0
            hsq_mm = 0

            @on("vector")
            def _(e):
                e.memset(stg[:], 0.0).then_inc(sv.h, 1); sv.inc()

            hsq_read_prev = 0
            for th in range(2):
                first_hsq = True
                for grp in range(8):
                    for s2 in range(4):
                        H = 8 * grp + 2 * s2
                        for gu in range(2):
                            HH = H + 64 * gu
                            cblk = 2048 * (HH // 16) + 0
                            for dc in range(16):
                                @on("sync")
                                def _(e, HH=HH, dc=dc,
                                      first=(th == 0 and grp == 0 and s2 == 0 and gu == 0 and dc == 0)):
                                    if first:
                                        wait(e, sv, xn1_fm_done)
                                        wait(e, cc["qgu"], cc["qgu"].n)
                                    if ld_w1.seq >= 2:
                                        wait(e, stn, ld_w1_cons[ld_w1.seq - 2])
                                    ld_w1.issue(e, w1str[ld_w1.seq % 2][:],
                                                qgu_out[2048 * (HH // 16) + 128 * dc:
                                                        2048 * (HH // 16) + 128 * (dc + 1),
                                                        128 * (HH % 16):128 * (HH % 16) + 256])

                                @on("tensor")
                                def _(e, th=th, gu=gu, dc=dc, seq=ld_w1.seq - 1,
                                      sil=silu_h, pn=p_h):
                                    ld_w1.wait_seq(e, seq)
                                    if dc == 0:
                                        if gu == 0 and pn:
                                            wait(e, sv, pn)
                                        if gu == 1 and sil:
                                            wait(e, ss, sil)
                                    for i in range(2):
                                        ins = e.matmul(ps_gu1[:, 512 * i:512 * (i + 1)],
                                                       w1str[seq % 2][:, 128 * i:128 * (i + 1)],
                                                       xn1_fm[:, dc * TOK + 512 * th:dc * TOK + 512 * (th + 1)],
                                                       start=(dc == 0), stop=(dc == 15))
                                    ins.then_inc(stn.h, 1)
                                    stn.inc()
                                ld_w1_cons.append(stn.n)
                            gu_n = stn.n
                            if gu == 0:
                                @on("scalar")
                                def _(e, n=gu_n, pn=p_h):
                                    wait(e, stn, n)
                                    if pn:
                                        wait(e, sv, pn)
                                    e.activation(gsil1[:], ps_gu1[:, :], AF.Silu).then_inc(ss.h, 1)
                                    ss.inc()
                                silu_h = ss.n
                            else:
                                @on("vector")
                                def _(e, s2=s2, n=gu_n, sil=silu_h, dmm=dn_h):
                                    wait(e, stn, n)
                                    wait(e, ss, sil)
                                    if dmm:
                                        wait(e, stn, dmm)
                                    e.tensor_mul(p1_grp[:, 1024 * s2:1024 * (s2 + 1)],
                                                 gsil1[:], ps_gu1[:, :]).then_inc(sv.h, 1)
                                    sv.inc()
                                p_h = sv.n

                                @on("scalar")
                                def _(e, s2=s2, n=p_h, hs=hsq_mm):
                                    wait(e, sv, n)
                                    if hs:
                                        wait(e, stn, hs)
                                    e.activation(sq_bf[:], p1_grp[:, 1024 * s2:1024 * (s2 + 1)],
                                                 AF.Square).then_inc(ss.h, 1)
                                    ss.inc()
                                sq_n2 = ss.n

                                @on("tensor")
                                def _(e, n=sq_n2, first=first_hsq, last=(grp == 7 and s2 == 3),
                                      hrp=hsq_read_prev):
                                    wait(e, ss, n)
                                    if first and hrp:
                                        wait(e, sv, hrp)
                                    for i in range(2):
                                        ins = e.matmul(ps_h1[0:1, :], ones_bf[:],
                                                       sq_bf[:, 512 * i:512 * (i + 1)],
                                                       start=(first and i == 0),
                                                       stop=(last and i == 1)).then_inc(stn.h, 1)
                                        stn.inc()
                                hsq_mm = stn.n
                                first_hsq = False
                    p1_all = p_h

                    for dmp in range(8):
                        for i in range(8):
                            @on("sync")
                            def _(e, grp=grp, dmp=dmp, i=i, first=(ld_wp1.seq == 0)):
                                if first:
                                    wait(e, cc["qp"], cc["qp"].n)
                                if ld_wp1.seq >= 2:
                                    wait(e, stn, ld_wp1_cons[ld_wp1.seq - 2])
                                ht = 8 * grp + i
                                ld_wp1.issue(e, wp1t[ld_wp1.seq % 2][:],
                                             qp_out[128 * ht:128 * (ht + 1), 256 * dmp:256 * (dmp + 1)])

                            @on("tensor")
                            def _(e, dmp=dmp, i=i, seq=ld_wp1.seq - 1,
                                  pn=p1_all, accp=acc_h, spv=spl_v):
                                ld_wp1.wait_seq(e, seq)
                                if i == 0:
                                    wait(e, sv, pn)
                                    if accp:
                                        wait(e, sv, accp)
                                    if dmp == 0 and spv:
                                        wait(e, sv, spv)
                                for kk in range(2):
                                    ins = e.matmul(ps_dn1[:, 512 * kk:512 * (kk + 1)],
                                                   wp1t[seq % 2][:, 128 * kk:128 * (kk + 1)],
                                                   p1_grp[:, 512 * i:512 * (i + 1)],
                                                   start=(i == 0), stop=(i == 7))
                                ins.then_inc(stn.h, 1)
                                stn.inc()
                            ld_wp1_cons.append(stn.n)
                        dn_h = stn.n

                        @on("vector")
                        def _(e, grp=grp, dmp=dmp, n=dn_h, spv=spl_v):
                            wait(e, stn, n)
                            if grp == 0 and spv:
                                wait(e, sv, spv)
                            for kk in range(2):
                                dmc = 2 * dmp + kk
                                if grp == 0:
                                    ins = e.tensor_copy(ffn1_acc[:, 512 * dmc:512 * (dmc + 1)],
                                                        ps_dn1[:, 512 * kk:512 * (kk + 1)])
                                else:
                                    ins = e.tensor_add(ffn1_acc[:, 512 * dmc:512 * (dmc + 1)],
                                                       ffn1_acc[:, 512 * dmc:512 * (dmc + 1)],
                                                       ps_dn1[:, 512 * kk:512 * (kk + 1)])
                            ins.then_inc(sv.h, 1)
                            sv.inc()
                        acc_h = sv.n
                    yield
                acc_done = sv.n

                @on("vector")
                def _(e, n=hsq_mm):
                    wait(e, stn, n)
                    e.tensor_copy(stg[0:1, :], ps_h1[0:1, :])
                    e.drain()
                    e.tensor_scalar(stg[0:1, :], stg[0:1, :], 1.0 / 8192.0, None, ALU.mult).then_inc(sv.h, 1)
                    sv.inc()
                mh = sv.n
                hsq_read_prev = mh

                @on("scalar")
                def _(e, n=mh):
                    wait(e, sv, n)
                    e.activation(stg[0:1, :], stg[0:1, :], AF.Sqrt,
                                 bias=cbias[0:1, 3:4]).then_inc(ss.h, 1)
                    ss.inc()
                sh = ss.n

                @on("vector")
                def _(e, n=sh):
                    wait(e, ss, n)
                    e.reciprocal(stg[0:1, :], stg[0:1, :]).then_inc(sv.h, 1)
                    sv.inc()
                ih = sv.n

                @on("tensor")
                def _(e, n=ih, dnn=dn_h):
                    wait(e, sv, n)
                    wait(e, stn, dnn)
                    for k in range(4):
                        ins = e.transpose(ps_dn1[:, 128 * k:128 * (k + 1)],
                                          stg[:, 128 * k:128 * (k + 1)], idf[:])
                    ins.then_inc(stn.h, 1)
                    stn.inc()
                itr = stn.n

                @on("vector")
                def _(e, n=itr):
                    wait(e, stn, n)
                    for k in range(4):
                        ins = e.tensor_copy(invc[:, k:k + 1], ps_dn1[:, 128 * k:128 * k + 1])
                    ins.then_inc(sv.h, 1)
                    sv.inc()
                inv_done = sv.n

                spill_ev = 0
                for tq in range(4):
                    for dmg in range(4):
                        @on("tensor")
                        def _(e, tq=tq, dmg=dmg, n=acc_done, iv=inv_done, sev=spill_ev):
                            wait(e, sv, n)
                            wait(e, sv, iv)
                            if sev:
                                wait(e, sv, sev)
                            for k in range(4):
                                dmc = 4 * dmg + k
                                ins = e.transpose(ps_dn1[:, 128 * k:128 * (k + 1)],
                                                  ffn1_acc[:, 512 * dmc + 128 * tq:512 * dmc + 128 * (tq + 1)],
                                                  idf[:])
                            ins.then_inc(stn.h, 1)
                            stn.inc()
                        tr_n2 = stn.n

                        @on("vector")
                        def _(e, th=th, tq=tq, dmg=dmg, tr_n2=tr_n2, otp=(ot1_st.seq - 1)):
                            wait(e, stn, tr_n2)
                            if otp >= 0:
                                ot1_st.wait_seq(e, otp)
                            e.tensor_scalar(
                                otm1[:].rearrange("p (a b) -> p a b", b=128)[:, 4 * dmg:4 * dmg + 4, :],
                                ps_dn1[:, 0:512].rearrange("p (a b) -> p a b", b=128),
                                invc[:, tq:tq + 1], None, ALU.mult).then_inc(sv.h, 1)
                            sv.inc()
                        spill_ev = sv.n
                    spl_v = sv.n

                    @on("sync")
                    def _(e, th=th, tq=tq, n=spl_v, fwv=x1_f_n):
                        wait(e, sv, n)
                        wait(e, sgp, fwv)
                        ot1_st.issue(e, ffn_sp[512 * th + 128 * tq:512 * th + 128 * (tq + 1), :],
                                     otm1[:])
                    yield
            ffn1_state["ot1_st"] = ot1_st
            while True:
                yield

        ffn1_state = {}

        def gram_emit():
            stG = contextlib.ExitStack()
            sbG = mk_sb(stG)
            psG = mk_ps(stG)
            G_sb = sbG("G_sb", [128, 2048], F32, side="right")
            E1 = sbG("Eg", [128, 2048], BF16, side="right")
            qtrP = sbG("qtrP", [128, 2048], BF16, side="right")
            xslt = [sbG(f"xsl{i}", [128, 512], BF16, side="right") for i in range(2)]
            xgt = [sbG(f"xgt{i}", [128, 1024], BF16, side="right") for i in range(2)]
            pG = psG("pG", [128, 1024], F32)
            ptb = psG("ptb", [128, 1024], BF16)
            ld_xg2 = DmaStream(nc, ex, "lxg2", 2)
            ld_xs2 = DmaStream(nc, ex, "lxs2", 2)
            st_pt = DmaStream(nc, ex, "spt", 1)
            gmm_hist = []
            ev_h = 0
            exp_h = 0
            ptr_h = 0
            ptev_h = 0
            for ib in range(4):
                for jh in range(2):
                    for ct in range(32):
                        @on("sync")
                        def _(e, ib=ib, jh=jh, ct=ct, first=(ld_xs2.seq == 0)):
                            if first:
                                ld_xsl.wait_all(e)
                            if ld_xs2.seq >= 2:
                                wait(e, stn, gmm_hist[ld_xs2.seq - 2])
                            ld_xs2.issue(e, xslt[ld_xs2.seq % 2][:], xsl[128 * ct:128 * (ct + 1), :])
                            ld_xg2.issue(e, xgt[ld_xg2.seq % 2][:],
                                         xg_out[128 * ct:128 * (ct + 1), 1024 * jh:1024 * (jh + 1)])

                        @on("tensor")
                        def _(e, ib=ib, ct=ct, seq=ld_xs2.seq - 1, evh=ev_h):
                            ld_xs2.wait_seq(e, seq)
                            ld_xg2.wait_seq(e, seq)
                            if ct == 0 and evh:
                                wait(e, sv, evh)
                            for k in range(2):
                                ins = e.matmul(pG[:, 512 * k:512 * (k + 1)],
                                               xslt[seq % 2][:, 128 * ib:128 * (ib + 1)],
                                               xgt[seq % 2][:, 512 * k:512 * (k + 1)],
                                               start=(ct == 0), stop=(ct == 31))
                            ins.then_inc(stn.h, 1)
                            stn.inc()
                        gmm_hist.append(stn.n)
                    g_n = stn.n

                    @on("vector")
                    def _(e, jh=jh, n=g_n, exph=exp_h):
                        wait(e, stn, n)
                        if jh == 0 and exph:
                            wait(e, ss, exph)
                        e.tensor_copy(G_sb[:, 1024 * jh:1024 * (jh + 1)], pG[:, :]).then_inc(sv.h, 1)
                        sv.inc()
                    ev_h = sv.n
                    yield

                @on("vector")
                def _(e, n=ev_h):
                    wait(e, sv, n)
                    e.reduce_max(smalls[:, 32:33], G_sb[:], AX)
                    e.drain()
                    e.tensor_scalar(smalls[:, 33:34], smalls[:, 32:33], -1.0, None,
                                    ALU.mult).then_inc(sv.h, 1)
                    sv.inc()
                m_n = sv.n

                @on("scalar")
                def _(e, n=m_n, ptrh=ptr_h):
                    wait(e, sv, n)
                    if ptrh:
                        wait(e, stn, ptrh)
                    e.activation(E1[:], G_sb[:], AF.Exp, bias=smalls[:, 33:34],
                                 accum_out=smalls[:, 34:35]).then_inc(ss.h, 1)
                    ss.inc()
                exp_h = ss.n

                @on("vector")
                def _(e, n=exp_h):
                    wait(e, ss, n)
                    e.reciprocal(smalls[:, 35:36], smalls[:, 34:35])
                    e.drain()
                    e.tensor_scalar(E1[:], E1[:], smalls[:, 35:36], None, ALU.mult).then_inc(sv.h, 1)
                    sv.inc()
                p_n = sv.n

                for half in range(2):
                    @on("tensor")
                    def _(e, half=half, n=p_n, ptevh=ptev_h):
                        wait(e, sv, n)
                        if ptevh:
                            wait(e, sv, ptevh)
                        for k in range(8):
                            jc = 8 * half + k
                            ins = e.transpose(ptb[:, 128 * k:128 * (k + 1)],
                                              E1[:, 128 * jc:128 * (jc + 1)], idb[:])
                        ins.then_inc(stn.h, 1)
                        stn.inc()
                    ptr_h = stn.n

                    @on("vector")
                    def _(e, half=half, n=ptr_h, stp=(st_pt.seq - 1)):
                        wait(e, stn, n)
                        if stp >= 0 and half == 0:
                            st_pt.wait_seq(e, stp)
                        e.tensor_copy(qtrP[:, 1024 * half:1024 * (half + 1)], ptb[:]).then_inc(sv.h, 1)
                        sv.inc()
                    ptev_h = sv.n

                @on("sync")
                def _(e, ib=ib, n=ptev_h):
                    wait(e, sv, n)
                    st_pt.issue(e, pt_in[2048 * ib:2048 * (ib + 1), :].rearrange("(a p) q -> p a q", p=128),
                                qtrP[:].rearrange("p (a q) -> p a q", q=128))
                yield

            @on("gpsimd")
            def _(e, ns=st_pt.seq):
                st_pt.wait_seq(e, ns - 1)
                e.collective_compute("AllGather", ALU.bypass, replica_groups=RG4,
                                     ins=[pt_in.tensor.ap().opt()],
                                     outs=[pt_out.tensor.ap().opt()]).then_inc(cc["pt"].h, 1)
                cc["pt"].inc()
            stG.close()

            sbX = mk_sb(stF1)
            psX = mk_ps(stF1)
            ptt = [sbX(f"ptt{i}", [128, 256], BF16) for i in range(2)]
            cxe = sbX("cxe", [128, 512], F32)
            ps_cx = psX("ps_cx", [128, 512], F32)
            ps_tr1 = psX("ps_tr1", [128, 512], F32)
            ld_pt2 = DmaStream(nc, ex, "lpt2", 2)
            ld_pt2_cons = []
            cx_ev = 0
            cx_tr = 0
            cx_add = 0
            for I in range(16):
                for cc2 in range(2):
                    for jp in range(8):
                        @on("sync")
                        def _(e, I=I, jp=jp, first=(ld_pt2.seq == 0)):
                            if first:
                                wait(e, cc["pt"], cc["pt"].n)
                            if ld_pt2.seq >= 2:
                                wait(e, stn, ld_pt2_cons[ld_pt2.seq - 2])
                            base = 8192 * (I // 4) + (16 * (I % 4) + 2 * jp) * 128
                            ld_pt2.issue(e, ptt[ld_pt2.seq % 2][:].rearrange("p (a q) -> p a q", q=128),
                                         pt_out[base:base + 256, :].rearrange("(a p) q -> p a q", p=128))

                        @on("tensor")
                        def _(e, I=I, cc2=cc2, jp=jp, seq=ld_pt2.seq - 1, cxv=cx_ev):
                            ld_pt2.wait_seq(e, seq)
                            if jp == 0 and cxv:
                                wait(e, sv, cxv)
                            for k in range(2):
                                jc = 2 * jp + k
                                ins = e.matmul(ps_cx[:, :],
                                               ptt[seq % 2][:, 128 * k:128 * (k + 1)],
                                               xn1_fm[:, jc * TOK + 512 * cc2:jc * TOK + 512 * (cc2 + 1)],
                                               start=(jp == 0 and k == 0), stop=(jp == 7 and k == 1))
                            ins.then_inc(stn.h, 1)
                            stn.inc()
                        ld_pt2_cons.append(stn.n)
                    cmm = stn.n

                    @on("vector")
                    def _(e, n=cmm, ctr=cx_tr):
                        wait(e, stn, n)
                        if ctr:
                            wait(e, stn, ctr)
                        e.tensor_copy(cxe[:], ps_cx[:, :]).then_inc(sv.h, 1)
                        sv.inc()
                    cx_ev = sv.n

                    @on("tensor")
                    def _(e, n=cx_ev, ca=cx_add):
                        wait(e, sv, n)
                        if ca:
                            wait(e, sv, ca)
                        for k in range(4):
                            ins = e.transpose(ps_tr1[:, 128 * k:128 * (k + 1)],
                                              cxe[:, 128 * k:128 * (k + 1)], idf[:])
                        ins.then_inc(stn.h, 1)
                        stn.inc()
                    cx_tr = stn.n

                    @on("vector")
                    def _(e, I=I, cc2=cc2, n=cx_tr):
                        wait(e, stn, n)
                        for k in range(4):
                            tt = 4 * cc2 + k
                            ins = e.tensor_add(x1[:, tt * D + 128 * I:tt * D + 128 * (I + 1)],
                                               x1[:, tt * D + 128 * I:tt * D + 128 * (I + 1)],
                                               ps_tr1[:, 128 * k:128 * (k + 1)])
                        ins.then_inc(sv.h, 1)
                        sv.inc()
                    cx_add = sv.n
                    yield
            gram_state["cx_done"] = cx_add
            while True:
                yield

        gram_state = {}
        g1 = ffn1_emit()
        g2 = gram_emit()
        next(g1)
        for _ in range(80):
            next(g2)
            next(g1)
        for _ in range(40):
            next(g1)
        for _ in range(40):
            next(g2)

        ffn1_last_v = sv.n

        @on("gpsimd")
        def _(e, n=ffn1_last_v):
            wait(e, sv, n)
            ot1 = ffn1_state["ot1_st"]
            ot1.wait_seq(e, ot1.seq - 1)
            for tt in range(8):
                e.dma_start(out=x1[:, tt * D:(tt + 1) * D], in_=ffn_sp[128 * tt:128 * (tt + 1), :],
                            accum_op=ALU.add).then_inc(sgp.h, 16)
                sgp.inc(16)
        fin_n = sgp.n

        @on("sync")
        def _(e, n=fin_n):
            wait(e, sgp, n)
            for tt in range(8):
                st_out.issue(e, out_own[128 * tt:128 * (tt + 1), :], x1[:, tt * D:(tt + 1) * D])
            st_out.wait_all(e)
            for tb in tbat.values():
                tb.wait_all(e)
        stF1.close()
        stL.close()
        stD.close()
        ex.close()
        return nc

    if stage <= 3:
        @on("sync")
        def _(e, n=x1_acc_n):
            wait(e, sgp, n)
            for tt in range(8):
                st_out.issue(e, out_own[128 * tt:128 * (tt + 1), :], x1[:, tt * D:(tt + 1) * D])
            st_out.wait_all(e)
            for tb in tbat.values():
                tb.wait_all(e)
        stD.close()
        ex.close()
        return nc

    stD.close()
    ex.close()
    return nc


def kernel_stage(inputs, stage=1, taps=()):
    nc = build_nc(stage=stage, taps=taps)
    in_maps = [host_inputs(inputs, c) for c in range(8)]
    res = run_bass_kernel_spmd(nc, in_maps, core_ids=list(range(8)))
    return res


def kernel(**inputs):
    nc = build_nc(stage=4)
    in_maps = [host_inputs(inputs, c) for c in range(8)]
    res = run_bass_kernel_spmd(nc, in_maps, core_ids=list(range(8)))
    out = np.empty((2, 4096, 2048), np.float32)
    for c in range(8):
        b, r = c // 4, c % 4
        out[b, 1024 * r:1024 * (r + 1)] = res.results[c]["out_own"]
    return out


# version 04
# speedup vs baseline: 1.4651x; 1.4651x over previous
import contextlib
import numpy as np
import ml_dtypes

import concourse.bass as bass
import concourse.mybir as mybir
from concourse.bass_utils import run_bass_kernel_spmd

F32 = mybir.dt.float32
BF16 = mybir.dt.bfloat16
AF = mybir.ActivationFunctionType
ALU = mybir.AluOpType
AX = mybir.AxisListType.X
BF = ml_dtypes.bfloat16

B, S, D = 2, 4096, 2048
DH = 64
TOK = 1024
RG4 = [[0, 1, 2, 3], [4, 5, 6, 7]]
RG8 = [[0, 1, 2, 3, 4, 5, 6, 7]]
RND = 12582912.0
SWAP = [i ^ 1 for i in range(32)]


def host_tables():
    t = {}
    inv_freq = (np.float32(1.0) /
                (np.float32(10000.0) ** (np.arange(0, DH, 2, dtype=np.float32) / np.float32(DH)))).astype(np.float32)
    pos = np.arange(S, dtype=np.float32)
    freqs = (pos[:, None] * inv_freq[None, :]).astype(np.float32)
    cos = np.cos(freqs).astype(np.float32)
    sin = np.sin(freqs).astype(np.float32)
    cos_dup = np.repeat(cos, 2, axis=1)
    sin_sgn = np.empty((S, DH), np.float32)
    sin_sgn[:, 0::2] = -sin
    sin_sgn[:, 1::2] = sin
    t["cos_t"] = np.ascontiguousarray(np.tile(cos_dup.T, (2, 1)))
    t["sin_t"] = np.ascontiguousarray(np.tile(sin_sgn.T, (2, 1)))
    p = np.arange(128)
    t["tri"] = (p[None, :] >= p[:, None]).astype(np.float32)
    f = np.arange(512, dtype=np.float32)
    ddb = f[None, :] - np.arange(128, dtype=np.float32)[:, None]
    t["dd"] = np.concatenate([ddb - 128.0 * jl for jl in range(4)], axis=1).astype(np.float32)
    t["ident_bf"] = np.eye(128, dtype=np.float32).astype(BF)
    t["ident_f"] = np.eye(128, dtype=np.float32)
    return t


_TABLES = host_tables()


def host_inputs(inp, core):
    b, r = core // 4, core % 4
    x = np.asarray(inp["L0_attn_hidden_states"])
    Wq = np.asarray(inp["L0_attn_Wq"]); Wk = np.asarray(inp["L0_attn_Wk"])
    Wv = np.asarray(inp["L0_attn_Wv"]); Wo = np.asarray(inp["L0_attn_Wo"])
    bo = np.asarray(inp["L0_attn_bo"])
    Wg = np.asarray(inp["L0_ffn_W_g"]); Wu = np.asarray(inp["L0_ffn_W_u"]); Wd = np.asarray(inp["L0_ffn_W_d"])
    Wgu1 = np.asarray(inp["L1_ffn_w_gate_up"]); Wp1 = np.asarray(inp["L1_ffn_w_proj"])

    d = dict(_TABLES)
    d["x_full"] = np.ascontiguousarray(x[b])
    d["x_own"] = np.ascontiguousarray(x[b, 1024 * r:1024 * (r + 1)])
    qcols = []
    for t in range(4):
        for hh in (t, t + 4):
            g = 8 * r + hh
            qcols.extend(range(64 * g, 64 * g + 64))
    d["wqs"] = np.ascontiguousarray(Wq[:, qcols])
    d["wks"] = np.ascontiguousarray(Wk[:, 128 * r:128 * (r + 1)])
    d["wvs"] = np.ascontiguousarray(Wv[:, 128 * r:128 * (r + 1)])
    d["wos"] = np.ascontiguousarray(Wo[512 * r:512 * (r + 1), :])
    d["bo"] = np.ascontiguousarray(np.broadcast_to(bo[None, :], (1, D)))
    hs = np.arange(1, 33, dtype=np.float32) * np.float32(8.0) / np.float32(32.0)
    slopes = (np.float32(2.0) ** (-hs)).astype(np.float32)
    sl = slopes[8 * r:8 * r + 8].astype(np.float64)
    d["slopes_neg"] = np.ascontiguousarray(
        np.tile(-sl[None, :].astype(np.float32), (128, 1)))
    tab = np.zeros((128, 64), np.float64)
    for h in range(8):
        for dq in range(8):
            tab[:, 8 * h + dq] = -sl[h] * 512.0 * dq / 240.0
    d["bias_tab"] = np.ascontiguousarray(tab.astype(np.float32))
    d["wg_sh"] = np.ascontiguousarray(
        np.concatenate([Wg.T[256 * core:256 * (core + 1), :],
                        Wu.T[256 * core:256 * (core + 1), :]], axis=1))
    d["wd_sh"] = np.ascontiguousarray(Wd.T[1024 * core:1024 * (core + 1), :])
    d["wgu_sh"] = np.ascontiguousarray(Wgu1[2048 * core:2048 * (core + 1), :])
    d["wp_sh"] = np.ascontiguousarray(Wp1[:, 1024 * core:1024 * (core + 1)])
    return d


class Ctr:
    def __init__(self, h):
        self.h = h
        self.n = 0

    def inc(self, k=1):
        self.n += k
        return self.n


def wait(e, c, n=None):
    n = c.n if n is None else n
    assert n <= c.n, f"wait on future inc: {n} > {c.n}"
    if n > 0:
        e.wait_ge(c.h, n)


class DmaStream:

    def __init__(self, nc, ex, name, nbuf):
        self.sems = [Ctr(ex.enter_context(nc.semaphore(f"{name}{i}"))) for i in range(nbuf)]
        self.nbuf = nbuf
        self.seq = 0

    def issue(self, e, out, in_, **kw):
        c = self.sems[self.seq % self.nbuf]
        e.dma_start(out=out, in_=in_, **kw).then_inc(c.h, 16)
        c.inc(16)
        self.seq += 1
        return self.seq - 1

    def wait_seq(self, e, seq):
        assert 0 <= seq < self.seq
        c = self.sems[seq % self.nbuf]
        e.wait_ge(c.h, 16 * (seq // self.nbuf + 1))


class DmaBatch:

    def __init__(self, nc, ex, name):
        self.c = Ctr(ex.enter_context(nc.semaphore(name)))

    def issue(self, e, out, in_, **kw):
        e.dma_start(out=out, in_=in_, **kw).then_inc(self.c.h, 16)
        self.c.inc(16)

    def wait_all(self, e):
        if self.c.n:
            e.wait_ge(self.c.h, self.c.n)


def build_nc(stage=99, taps=()):
    nc = bass.Bass("TRN2", target_bir_lowering=False, debug=False)
    ex = contextlib.ExitStack()
    taps = set(taps)

    def din(name, shape, dt=F32):
        return nc.dram_tensor(name, shape, dt, kind="ExternalInput").ap()

    def dout(name, shape, dt=F32):
        return nc.dram_tensor(name, shape, dt, kind="ExternalOutput").ap()

    def dint(name, shape, dt=F32):
        return nc.dram_tensor(name, shape, dt, kind="Internal").ap()

    def mk_sb(stk):
        def f(name, shape, dt, side="left"):
            return stk.enter_context(nc.sbuf_tensor(name, shape, dt, side=side))
        return f

    def mk_ps(stk):
        def f(name, shape, dt=F32):
            return stk.enter_context(nc.psum_tensor(name, shape, dt))
        return f

    sb = mk_sb(ex)

    def sem(name):
        return Ctr(ex.enter_context(nc.semaphore(name)))

    x_full = din("x_full", [S, D])
    x_own = din("x_own", [TOK, D])
    wqs_i = din("wqs", [D, 512]); wks_i = din("wks", [D, 128]); wvs_i = din("wvs", [D, 128])
    wos_i = din("wos", [512, D]); bo_i = din("bo", [1, D])
    wg_sh = din("wg_sh", [256, 16384]); wd_sh = din("wd_sh", [1024, 2048])
    wgu_sh = din("wgu_sh", [2048, 2048]); wp_sh = din("wp_sh", [2048, 1024])
    cos_i = din("cos_t", [128, S]); sin_i = din("sin_t", [128, S])
    tri_i = din("tri", [128, 128]); dd_i = din("dd", [128, 2048])
    idb_i = din("ident_bf", [128, 128], BF16); idf_i = din("ident_f", [128, 128])
    sln_i = din("slopes_neg", [128, 8]); btab_i = din("bias_tab", [128, 64])

    out_own = dout("out_own", [TOK, D])
    tap = {}
    _tapspec = {
        "q": ([128, 4 * S], BF16), "k": ([128, S], BF16), "v": ([128, 32 * 130], BF16),
        "ctx": ([128, 4 * S], BF16), "x1": ([TOK, D], F32),
        "xn_own": ([128, 8 * D], BF16), "nq": ([128, 16 * TOK], BF16),
        "xn1": ([TOK, D], BF16), "gram": ([512, D], F32), "pown": ([128, 4 * D], BF16),
        "ffn0": ([TOK, D], F32),
        "rsout": ([TOK, D], F32), "bob": ([128, D], F32),
        "x1a": ([TOK, D], F32), "x1b": ([TOK, D], F32),
    }
    for t in taps:
        tap[t] = dout("tap_" + t, *_tapspec[t])

    sgu_in = dint("sgu_in", [256, 16384], BF16); sgu_out = dint("sgu_out", [2048, 16384], BF16)
    sd_in = dint("sd_in", [1024, 2048], BF16); sd_out = dint("sd_out", [8192, 2048], BF16)
    qgu_in = dint("qgu_in", [2048, 2048], BF16); qgu_out = dint("qgu_out", [16384, 2048], BF16)
    qp_in = dint("qp_in", [1024, 2048], BF16); qp_out = dint("qp_out", [8192, 2048], BF16)
    rs_in = dint("rs_in", [S, D]); rs_out = dint("rs_out", [TOK, D])
    ffn_sp = dint("ffn_sp", [TOK, D])
    xg_in = dint("xg_in", [TOK, D], BF16); xg_out = dint("xg_out", [S, D], BF16)
    a2a_in = dint("a2a_in", [S, 512], BF16); a2a_out = dint("a2a_out", [S, 512], BF16)
    pt_in = dint("pt_in", [8192, 128], BF16); pt_out = dint("pt_out", [32768, 128], BF16)

    sv = sem("sv")
    ss = sem("ss")
    stn = sem("stn")
    sgp = sem("sgp")

    ld_const = DmaBatch(nc, ex, "ldc")
    tbat = {t: DmaBatch(nc, ex, "tb_" + t) for t in taps}
    ld_wc = DmaBatch(nc, ex, "ldw")
    st_out = DmaBatch(nc, ex, "sto")
    cc = {nm: sem("cc_" + nm) for nm in
          ("sgu", "sd", "qgu", "qp", "rs", "xg", "a2a", "pt")}

    KT = sb("KT", [128, S], BF16)
    VE = sb("VE", [128, 32 * 130], BF16)
    idb = sb("idb", [128, 128], BF16)
    idf = sb("idf", [128, 128], F32)
    ones = sb("ones", [128, 128], F32)
    smalls = sb("smalls", [128, 64], F32)
    cbias = sb("cbias", [128, 8], F32)
    stQT = contextlib.ExitStack()
    QT = mk_sb(stQT)("QT", [128, 4 * S], BF16, side="right")

    stW = contextlib.ExitStack()
    sbW = mk_sb(stW)
    wq_sb = sbW("wq_sb", [128, 16 * 512], BF16, side="right")
    wk_sb = sbW("wk_sb", [128, 16 * 128], BF16, side="right")
    wv_sb = sbW("wv_sb", [128, 16 * 128], BF16, side="right")

    blk = ex.enter_context(nc.Block())

    def on(engine):
        def deco(fn):
            getattr(blk, engine)(fn)
        return deco

    @on("sync")
    def _(e):
        ld_const.issue(e, idb[:], idb_i[:, :])
        ld_const.issue(e, idf[:], idf_i[:, :])

    @on("gpsimd")
    def _(e):
        for dst, src in [
            (wq_sb[:].rearrange("p (a q) -> p a q", a=16), wqs_i[:, :].rearrange("(a p) q -> p a q", p=128)),
            (wk_sb[:].rearrange("p (a q) -> p a q", a=16), wks_i[:, :].rearrange("(a p) q -> p a q", p=128)),
            (wv_sb[:].rearrange("p (a q) -> p a q", a=16), wvs_i[:, :].rearrange("(a p) q -> p a q", p=128)),
        ]:
            ld_wc.issue(e, dst, src)

    @on("vector")
    def _(e):
        e.memset(ones[:], 1.0)
        e.memset(VE[:].rearrange("p (a b) -> p a b", b=65)[:, :, 64:65], 1.0)
        for c, val in [(0, 1e-5), (1, 1e-8), (2, -30.0), (3, 1.1920929e-07)]:
            e.memset(cbias[:, c:c + 1], val)
        e.engine_nop().then_inc(sv.h, 1); sv.inc()
    consts_v = sv.n

    if stage >= 3:
        st0 = contextlib.ExitStack()
        sb0 = mk_sb(st0)
        sgn = [sb0(f"sgn{i}", [128, 8192], BF16, side="right") for i in range(2)]
        ld_sgn = DmaStream(nc, ex, "lsgn", 2)
        st_sgn = DmaStream(nc, ex, "ssgn", 2)
        jobs = [("gu", tt, hh) for tt in range(2) for hh in range(2)] + \
               [("d", tt, 0) for tt in range(8)]
        sgn_done = []
        for j, (kind, tt, hh) in enumerate(jobs):
            @on("gpsimd")
            def _(e, j=j, kind=kind, tt=tt, hh=hh):
                if j >= 2:
                    wait(e, ss, sgn_done[j - 2])
                    st_sgn.wait_seq(e, j - 2)
                if kind == "gu":
                    ld_sgn.issue(e, sgn[j % 2][:], wg_sh[128 * tt:128 * (tt + 1), 8192 * hh:8192 * (hh + 1)])
                else:
                    ld_sgn.issue(e, sgn[j % 2][:, :2048], wd_sh[128 * tt:128 * (tt + 1), :])

            @on("scalar")
            def _(e, j=j, kind=kind):
                ld_sgn.wait_seq(e, j)
                if j >= 2:
                    st_sgn.wait_seq(e, j - 2)
                if kind == "gu":
                    e.activation(sgn[j % 2][:], sgn[j % 2][:], AF.Sign).then_inc(ss.h, 1)
                else:
                    e.activation(sgn[j % 2][:, :2048], sgn[j % 2][:, :2048], AF.Sign).then_inc(ss.h, 1)
                ss.inc()
            sgn_done.append(ss.n)

            @on("gpsimd")
            def _(e, j=j, kind=kind, tt=tt, hh=hh):
                wait(e, ss, sgn_done[j])
                if kind == "gu":
                    st_sgn.issue(e, sgu_in[128 * tt:128 * (tt + 1), 8192 * hh:8192 * (hh + 1)], sgn[j % 2][:])
                else:
                    st_sgn.issue(e, sd_in[128 * tt:128 * (tt + 1), :], sgn[j % 2][:, :2048])

        @on("gpsimd")
        def _(e):
            for j in range(len(jobs)):
                st_sgn.wait_seq(e, j)
            e.collective_compute("AllGather", ALU.bypass, replica_groups=RG8,
                                 ins=[sgu_in.tensor.ap().opt()],
                                 outs=[sgu_out.tensor.ap().opt()]).then_inc(cc["sgu"].h, 1)
            cc["sgu"].inc()
            e.collective_compute("AllGather", ALU.bypass, replica_groups=RG8,
                                 ins=[sd_in.tensor.ap().opt()],
                                 outs=[sd_out.tensor.ap().opt()]).then_inc(cc["sd"].h, 1)
            cc["sd"].inc()
        st0.close()

    st1 = contextlib.ExitStack()
    sb1 = mk_sb(st1)
    ps1 = mk_ps(st1)
    xt = [sb1(f"xt{i}", [128, D], F32, side="right") for i in range(2)]
    xnt = [sb1(f"xnt{i}", [128, D], BF16, side="right") for i in range(2)]
    xnf = sb1("xnf", [128, 16 * 512], BF16, side="right")
    cs_sl = sb1("cs_sl", [128, 512], F32, side="right")
    sn_sl = sb1("sn_sl", [128, 512], F32, side="right")
    qshu = sb1("qshu", [128, 512], F32, side="right")
    qtmp = sb1("qtmp", [128, 512], F32, side="right")
    pst = [ps1(f"pst{i}", [128, 2048], BF16) for i in range(2)]
    psq = ps1("psq", [128, 512], F32)
    psk = ps1("psk", [128, 512], F32)
    psv = ps1("psv", [128, 128], F32)

    ld_x = DmaStream(nc, ex, "lx", 2)
    ld_cs = DmaStream(nc, ex, "lcs", 2)

    if stage >= 3:
        @on("sync")
        def _(e):
            st_sgn.wait_seq(e, st_sgn.seq - 1)
            st_sgn.wait_seq(e, st_sgn.seq - 2)

    xn_hist = []
    var_hist = []
    evac_done = [0, 0]
    rope_done = 0
    v_last_evac = 0

    for ch in range(8):
        t0 = 512 * ch

        @on("sync")
        def _(e, t0=t0, rope_prev=rope_done):
            if rope_prev:
                wait(e, sv, rope_prev)
            ld_cs.issue(e, cs_sl[:], cos_i[:, t0:t0 + 512])
            ld_cs.issue(e, sn_sl[:], sin_i[:, t0:t0 + 512])
        cs_seq = ld_cs.seq - 1

        for tt in range(4):
            gi = 4 * ch + tt
            cb = 8 * (gi % 2)

            @on("sync")
            def _(e, tt=tt, t0=t0, gi=gi):
                if gi >= 2:
                    wait(e, sv, xn_hist[gi - 2])
                ld_x.issue(e, xt[tt % 2][:], x_full[t0 + 128 * tt:t0 + 128 * (tt + 1), :])

            @on("scalar")
            def _(e, tt=tt, gi=gi, cb=cb, evp=evac_done[tt % 2]):
                ld_x.wait_seq(e, gi)
                if gi >= 2:
                    wait(e, sv, var_hist[gi - 2])
                    wait(e, sv, evp)
                e.activation(xnt[tt % 2][:], xt[tt % 2][:], AF.Square,
                             accum_out=smalls[:, cb + 1:cb + 2]).then_inc(ss.h, 1)
                ss.inc()
            sq_n = ss.n

            @on("vector")
            def _(e, tt=tt, gi=gi, sq_n=sq_n, cb=cb):
                ld_x.wait_seq(e, gi)
                e.reduce_sum(smalls[:, cb:cb + 1], xt[tt % 2][:], AX)
                wait(e, ss, sq_n)
                e.drain()
                e.tensor_scalar(smalls[:, cb:cb + 1], smalls[:, cb:cb + 1], 1.0 / D, None, ALU.mult)
                e.drain()
                e.tensor_tensor(smalls[:, cb + 2:cb + 3], smalls[:, cb:cb + 1],
                                smalls[:, cb:cb + 1], ALU.mult)
                e.drain()
                e.scalar_tensor_tensor(smalls[:, cb + 3:cb + 4], smalls[:, cb + 1:cb + 2], 1.0 / D,
                                       smalls[:, cb + 2:cb + 3], ALU.mult, ALU.subtract).then_inc(sv.h, 1)
                sv.inc()
            var_hist.append(sv.n)

            @on("scalar")
            def _(e, var_n=sv.n, cb=cb, first=(gi == 0)):
                wait(e, sv, var_n)
                if first:
                    wait(e, sv, consts_v)
                e.activation(smalls[:, cb + 4:cb + 5], smalls[:, cb + 3:cb + 4],
                             AF.Sqrt, bias=cbias[:, 0:1]).then_inc(ss.h, 1)
                ss.inc()
            sqr_n = ss.n

            @on("vector")
            def _(e, tt=tt, sqr_n=sqr_n, cb=cb):
                wait(e, ss, sqr_n)
                e.reciprocal(smalls[:, cb + 5:cb + 6], smalls[:, cb + 4:cb + 5])
                e.drain()
                e.tensor_scalar(xnt[tt % 2][:], xt[tt % 2][:], smalls[:, cb:cb + 1],
                                smalls[:, cb + 5:cb + 6], ALU.subtract, ALU.mult).then_inc(sv.h, 1)
                sv.inc()
            xn_hist.append(sv.n)

            @on("tensor")
            def _(e, tt=tt, xn_n=sv.n, prev_ev=evac_done[tt % 2], first=(gi == 0)):
                wait(e, sv, xn_n)
                if first:
                    ld_const.wait_all(e)
                if prev_ev:
                    wait(e, sv, prev_ev)
                for dc in range(16):
                    ins = e.transpose(pst[tt % 2][:, 128 * dc:128 * (dc + 1)],
                                      xnt[tt % 2][:, 128 * dc:128 * (dc + 1)], idb[:])
                ins.then_inc(stn.h, 1)
                stn.inc()
            tr_n = stn.n

            @on("vector")
            def _(e, tt=tt, tr_n=tr_n):
                wait(e, stn, tr_n)
                e.tensor_copy(
                    xnf[:].rearrange("p (a b) -> p a b", b=512)[:, :, 128 * tt:128 * (tt + 1)],
                    pst[tt % 2][:].rearrange("p (a b) -> p a b", b=128)).then_inc(sv.h, 1)
                sv.inc()
            evac_done[tt % 2] = sv.n

        xnf_n = sv.n

        for pr in range(4):
            @on("tensor")
            def _(e, pr=pr, ch=ch, xnf_n=xnf_n, rope_prev=rope_done):
                wait(e, sv, xnf_n)
                if rope_prev:
                    wait(e, sv, rope_prev)
                if pr == 0 and ch == 0:
                    ld_wc.wait_all(e)
                for dc in range(16):
                    ins = e.matmul(psq[:, :], wq_sb[:, dc * 512 + pr * 128:dc * 512 + (pr + 1) * 128],
                                   xnf[:, dc * 512:(dc + 1) * 512], start=(dc == 0), stop=(dc == 15))
                ins.then_inc(stn.h, 1)
                stn.inc()
            qmm_n = stn.n

            @on("vector")
            def _(e, pr=pr, ch=ch, qmm_n=qmm_n, cs_seq=cs_seq):
                wait(e, stn, qmm_n)
                ld_cs.wait_seq(e, cs_seq - 1)
                ld_cs.wait_seq(e, cs_seq)
                dst = QT[:, pr * S + 512 * ch: pr * S + 512 * (ch + 1)]
                e.stream_shuffle(qshu[:], psq[:, :], SWAP)
                e.tensor_tensor(qtmp[:], psq[:, :], cs_sl[:], ALU.mult)
                e.drain()
                e.tensor_tensor(qshu[:], qshu[:], sn_sl[:], ALU.mult)
                e.drain()
                e.tensor_add(dst, qtmp[:], qshu[:]).then_inc(sv.h, 1)
                sv.inc()
            rope_done = sv.n

        @on("tensor")
        def _(e, xnf_n=xnf_n, rope_prev=rope_done):
            wait(e, sv, rope_prev)
            for dc in range(16):
                ins = e.matmul(psk[:, :], wk_sb[:, dc * 128:(dc + 1) * 128],
                               xnf[:, dc * 512:(dc + 1) * 512], start=(dc == 0), stop=(dc == 15))
            ins.then_inc(stn.h, 1)
            stn.inc()
        kmm_n = stn.n

        @on("vector")
        def _(e, ch=ch, kmm_n=kmm_n):
            wait(e, stn, kmm_n)
            dst = KT[:, 512 * ch:512 * (ch + 1)]
            e.stream_shuffle(qshu[:], psk[:, :], SWAP)
            e.tensor_tensor(qtmp[:], psk[:, :], cs_sl[:], ALU.mult)
            e.drain()
            e.tensor_tensor(qshu[:], qshu[:], sn_sl[:], ALU.mult)
            e.drain()
            e.tensor_add(dst, qtmp[:], qshu[:]).then_inc(sv.h, 1)
            sv.inc()
        rope_done = sv.n

        for tt in range(4):
            @on("tensor")
            def _(e, tt=tt, xnf_n=xnf_n, vprev=v_last_evac):
                if vprev:
                    wait(e, sv, vprev)
                wait(e, sv, xnf_n)
                for dc in range(16):
                    ins = e.matmul(psv[:, :], xnf[:, dc * 512 + 128 * tt:dc * 512 + 128 * (tt + 1)],
                                   wv_sb[:, dc * 128:(dc + 1) * 128], start=(dc == 0), stop=(dc == 15))
                ins.then_inc(stn.h, 1)
                stn.inc()
            vmm_n = stn.n

            @on("vector")
            def _(e, ch=ch, tt=tt, vmm_n=vmm_n):
                wait(e, stn, vmm_n)
                jt = 4 * ch + tt
                e.tensor_copy(VE[:, jt * 130:jt * 130 + 64], psv[:, 0:64])
                e.tensor_copy(VE[:, jt * 130 + 65:jt * 130 + 129], psv[:, 64:128]).then_inc(sv.h, 1)
                sv.inc()
            v_last_evac = sv.n

    st1.close()
    stW.close()

    def tap_store(name, src_ap):
        if name in tap:
            @on("sync")
            def _(e):
                wait(e, sv, sv.n)
                wait(e, ss, ss.n)
                wait(e, stn, stn.n)
                tbat[name].issue(e, tap[name][:, :], src_ap)

    tap_store("q", QT[:])
    tap_store("k", KT[:])
    tap_store("v", VE[:])

    if stage <= 1:
        z = sb("zz", [128, D], F32)

        @on("vector")
        def _(e):
            e.memset(z[:], 0.0).then_inc(sv.h, 1); sv.inc()
        zn = sv.n

        @on("sync")
        def _(e):
            wait(e, sv, zn)
            for tt in range(8):
                st_out.issue(e, out_own[128 * tt:128 * (tt + 1), :], z[:])
            st_out.wait_all(e)
        stQT.close()
        ex.close()
        return nc

    stC = contextlib.ExitStack()
    sbC = mk_sb(stC)
    psC = mk_ps(stC)
    CTX = sbC("CTX", [128, 4 * S], BF16, side="right")
    T4 = sbC("T4", [128, 2048], F32)
    E4 = sbC("E4", [128, 2048], BF16, side="right")
    dnrb = sbC("dnrb", [128, 512], F32, side="right")
    tmpdiv = [sbC("tdv0", [128, 512], BF16, side="right")]
    tri = sbC("tri_s", [128, 128], F32, side="right")
    dd = sbC("dd_s", [128, 2048], F32, side="right")
    sln = sbC("sln_s", [128, 8], F32, side="right")
    btab = sbC("btab_s", [128, 64], F32, side="right")
    wos_sb = sbC("wos_sb", [128, 4 * D], BF16, side="right")
    wo_out = [sbC(f"woo{i}", [128, 512], F32, side="right") for i in range(2)]
    ps_s = psC("ps_s", [128, 512], F32)
    ps_c = psC("ps_c", [128, 512], F32)
    ps_w = psC("ps_w", [128, 1024], F32)

    ld_attc = DmaBatch(nc, ex, "ldac")
    ctx_dma = DmaStream(nc, ex, "cdx", 2)
    rs_st = DmaStream(nc, ex, "rst", 2)

    qkv_done_v = sv.n

    ffn_gen = None
    otm_st = None
    l1q_state = {}
    if stage >= 3:
        stF = contextlib.ExitStack()
        sbF = mk_sb(stF)
        n_fm = sbF("n_fm", [128, 16 * TOK], BF16)
        otm_st = DmaStream(nc, ex, "sfo", 1)

        def ffn_emit():
            stQ = contextlib.ExitStack()
            sbQ = mk_sb(stQ)
            psQ = mk_ps(stQ)
            xo_t = [sbQ(f"xo{i}", [128, D], F32, side="right") for i in range(2)]
            yq = sbQ("yq", [128, D], F32, side="right")
            q_s = sbQ("q_s", [128, D], F32, side="right")
            nq_t = [sbQ(f"nq{i}", [128, D], BF16, side="right") for i in range(2)]
            psqt = psQ("psqt", [128, 2048], BF16)
            ld_xq = DmaStream(nc, ex, "lxq", 2)
            nq_hist = []
            qvar_hist = []
            qev_hist = [0, 0]
            for tt in range(8):
                cb = 8 * (tt % 2)

                @on("sync")
                def _(e, tt=tt, q0=qkv_done_v):
                    if tt == 0:
                        wait(e, sv, q0)
                        if "st_l1" in l1q_state:
                            s = l1q_state["st_l1"]
                            s.wait_seq(e, s.seq - 1)
                            s.wait_seq(e, s.seq - 2)
                    if tt >= 2:
                        wait(e, sv, nq_hist[tt - 2])
                    ld_xq.issue(e, xo_t[tt % 2][:], x_own[128 * tt:128 * (tt + 1), :])

                @on("scalar")
                def _(e, tt=tt, cb=cb, evp=qev_hist[tt % 2]):
                    ld_xq.wait_seq(e, tt)
                    if tt >= 2:
                        wait(e, sv, qvar_hist[tt - 2])
                        wait(e, sv, evp)
                    e.activation(nq_t[tt % 2][:], xo_t[tt % 2][:], AF.Square,
                                 accum_out=smalls[:, cb + 1:cb + 2]).then_inc(ss.h, 1)
                    ss.inc()
                sq_n = ss.n

                @on("vector")
                def _(e, tt=tt, sq_n=sq_n, cb=cb):
                    ld_xq.wait_seq(e, tt)
                    e.reduce_sum(smalls[:, cb:cb + 1], xo_t[tt % 2][:], AX)
                    wait(e, ss, sq_n)
                    e.drain()
                    e.tensor_scalar(smalls[:, cb:cb + 1], smalls[:, cb:cb + 1], 1.0 / D, None, ALU.mult)
                    e.drain()
                    e.tensor_tensor(smalls[:, cb + 2:cb + 3], smalls[:, cb:cb + 1],
                                    smalls[:, cb:cb + 1], ALU.mult)
                    e.drain()
                    e.scalar_tensor_tensor(smalls[:, cb + 3:cb + 4], smalls[:, cb + 1:cb + 2],
                                           1.0 / D, smalls[:, cb + 2:cb + 3],
                                           ALU.mult, ALU.subtract).then_inc(sv.h, 1)
                    sv.inc()
                qvar_hist.append(sv.n)

                @on("scalar")
                def _(e, n=sv.n, cb=cb):
                    wait(e, sv, n)
                    e.activation(smalls[:, cb + 4:cb + 5], smalls[:, cb + 3:cb + 4],
                                 AF.Sqrt, bias=cbias[:, 1:2]).then_inc(ss.h, 1)
                    ss.inc()
                sqr_n = ss.n

                @on("vector")
                def _(e, tt=tt, sqr_n=sqr_n, cb=cb):
                    wait(e, ss, sqr_n)
                    e.reciprocal(smalls[:, cb + 5:cb + 6], smalls[:, cb + 4:cb + 5])
                    e.drain()
                    e.tensor_scalar(yq[:], xo_t[tt % 2][:], smalls[:, cb:cb + 1],
                                    smalls[:, cb + 5:cb + 6], ALU.subtract, ALU.mult)
                    e.drain()
                    e.reduce_max(smalls[:, cb + 6:cb + 7], yq[:], AX, apply_absolute_value=True)
                    e.drain()
                    e.reciprocal(smalls[:, cb + 7:cb + 8], smalls[:, cb + 6:cb + 7])
                    e.drain()
                    e.tensor_scalar(smalls[:, cb + 7:cb + 8], smalls[:, cb + 7:cb + 8],
                                    127.0, None, ALU.mult)
                    e.drain()
                    e.reciprocal(smalls[:, cb + 6:cb + 7], smalls[:, cb + 7:cb + 8])
                    e.drain()
                    e.tensor_scalar(q_s[:], yq[:], smalls[:, cb + 7:cb + 8], RND,
                                    ALU.mult, ALU.add)
                    e.drain()
                    e.tensor_scalar(q_s[:], q_s[:], RND, -128.0, ALU.subtract, ALU.max)
                    e.drain()
                    e.tensor_scalar(nq_t[tt % 2][:], q_s[:], 127.0, smalls[:, cb + 6:cb + 7],
                                    ALU.min, ALU.mult).then_inc(sv.h, 1)
                    sv.inc()
                nq_hist.append(sv.n)

                @on("tensor")
                def _(e, tt=tt, n=sv.n, prev=qev_hist[tt % 2]):
                    wait(e, sv, n)
                    if prev:
                        wait(e, sv, prev)
                    for dc in range(16):
                        ins = e.transpose(psqt[:, 128 * dc:128 * (dc + 1)],
                                          nq_t[tt % 2][:, 128 * dc:128 * (dc + 1)], idb[:])
                    ins.then_inc(stn.h, 1)
                    stn.inc()
                tr_n = stn.n

                @on("vector")
                def _(e, tt=tt, tr_n=tr_n):
                    wait(e, stn, tr_n)
                    e.tensor_copy(
                        n_fm[:].rearrange("p (a b) -> p a b", b=TOK)[:, :, 128 * tt:128 * (tt + 1)],
                        psqt[:].rearrange("p (a b) -> p a b", b=128)).then_inc(sv.h, 1)
                    sv.inc()
                qev_hist[tt % 2] = sv.n
                if tt % 2 == 1:
                    yield
            nq_all = sv.n
            if "nq" in tap:
                @on("sync")
                def _(e, n=nq_all):
                    wait(e, sv, n)
                    tbat["nq"].issue(e, tap["nq"][:, :], n_fm[:])
            stQ.close()

            sbF2 = mk_sb(stF)
            psF2 = mk_ps(stF)
            ffn_acc = sbF2("ffn_acc", [128, 16 * 512], F32)
            p_grp = sbF2("p_grp", [128, 8 * 512], BF16)
            gsil = sbF2("gsil", [128, 1024], BF16)
            wstr = [sbF2(f"wstr{i}", [128, 2 * 256], BF16) for i in range(2)]
            wdt = [sbF2(f"wdt{i}", [128, 2 * 256], BF16) for i in range(2)]
            otm = [sbF2("otm0", [128, 1024], F32)]
            ps_gu = psF2("ps_gu", [128, 1024], F32)
            ps_dn = psF2("ps_dn", [128, 1024], F32)
            ld_w = DmaStream(nc, ex, "lw", 2)
            ld_wd = DmaStream(nc, ex, "lwd", 2)
            ld_w_cons = []
            ld_wd_cons = []
            silu_hist = 0
            p_hist = 0
            dn_mm_hist = 0
            acc_ev_hist = 0
            spill_v = 0

            for th in range(2):
                for grp in range(8):
                    for s2 in range(4):
                        hc0 = 128 * (8 * grp + 2 * s2)
                        for gu in range(2):
                            for dh in range(8):
                                @on("sync")
                                def _(e, dh=dh, gu=gu, hc0=hc0,
                                      first=(th == 0 and grp == 0 and s2 == 0 and gu == 0 and dh == 0)):
                                    if first:
                                        wait(e, sv, nq_all)
                                        wait(e, cc["sgu"], cc["sgu"].n)
                                    if ld_w.seq >= 2:
                                        wait(e, stn, ld_w_cons[ld_w.seq - 2])
                                    ld_w.issue(e, wstr[ld_w.seq % 2][:].rearrange("p (a q) -> p a q", q=256),
                                               sgu_out[256 * dh:256 * (dh + 1),
                                                       8192 * gu + hc0:8192 * gu + hc0 + 256]
                                               .rearrange("(a p) q -> p a q", p=128))

                                @on("tensor")
                                def _(e, th=th, gu=gu, dh=dh, seq=ld_w.seq - 1,
                                      sil=silu_hist, pn=p_hist):
                                    ld_w.wait_seq(e, seq)
                                    if dh == 0:
                                        if gu == 0 and pn:
                                            wait(e, sv, pn)
                                        if gu == 1 and sil:
                                            wait(e, ss, sil)
                                    for a in range(2):
                                        dc = 2 * dh + a
                                        for i in range(2):
                                            ins = e.matmul(ps_gu[:, 512 * i:512 * (i + 1)],
                                                           wstr[seq % 2][:, 256 * a + 128 * i:256 * a + 128 * (i + 1)],
                                                           n_fm[:, dc * TOK + 512 * th:dc * TOK + 512 * (th + 1)],
                                                           start=(dc == 0), stop=(dc == 15))
                                    ins.then_inc(stn.h, 1)
                                    stn.inc()
                                ld_w_cons.append(stn.n)
                            gu_mm_n = stn.n
                            if gu == 0:
                                @on("scalar")
                                def _(e, n=gu_mm_n, pn=p_hist):
                                    wait(e, stn, n)
                                    if pn:
                                        wait(e, sv, pn)
                                    e.activation(gsil[:], ps_gu[:, :], AF.Silu).then_inc(ss.h, 1)
                                    ss.inc()
                                silu_hist = ss.n
                            else:
                                @on("vector")
                                def _(e, s2=s2, n=gu_mm_n, sil=silu_hist, dmm=dn_mm_hist):
                                    wait(e, stn, n)
                                    wait(e, ss, sil)
                                    if dmm:
                                        wait(e, stn, dmm)
                                    e.tensor_mul(p_grp[:, 1024 * s2:1024 * (s2 + 1)],
                                                 gsil[:], ps_gu[:, :]).then_inc(sv.h, 1)
                                    sv.inc()
                                p_hist = sv.n
                    p_all_n = p_hist

                    for dmp in range(8):
                        for ih in range(4):
                            @on("sync")
                            def _(e, grp=grp, dmp=dmp, ih=ih, first=(ld_wd.seq == 0)):
                                if first:
                                    wait(e, cc["sd"], cc["sd"].n)
                                if ld_wd.seq >= 2:
                                    wait(e, stn, ld_wd_cons[ld_wd.seq - 2])
                                ld_wd.issue(e, wdt[ld_wd.seq % 2][:].rearrange("p (a q) -> p a q", q=256),
                                            sd_out[1024 * grp + 256 * ih:1024 * grp + 256 * (ih + 1),
                                                   256 * dmp:256 * (dmp + 1)]
                                            .rearrange("(a p) q -> p a q", p=128))

                            @on("tensor")
                            def _(e, dmp=dmp, ih=ih, seq=ld_wd.seq - 1,
                                  pn=p_all_n, accp=acc_ev_hist, spv=spill_v):
                                ld_wd.wait_seq(e, seq)
                                if ih == 0:
                                    wait(e, sv, pn)
                                    if accp:
                                        wait(e, sv, accp)
                                    if dmp == 0 and spv:
                                        wait(e, sv, spv)
                                for a in range(2):
                                    i = 2 * ih + a
                                    for kk in range(2):
                                        ins = e.matmul(ps_dn[:, 512 * kk:512 * (kk + 1)],
                                                       wdt[seq % 2][:, 256 * a + 128 * kk:256 * a + 128 * (kk + 1)],
                                                       p_grp[:, 512 * i:512 * (i + 1)],
                                                       start=(i == 0), stop=(i == 7))
                                ins.then_inc(stn.h, 1)
                                stn.inc()
                            ld_wd_cons.append(stn.n)
                        dn_mm_hist = stn.n

                        @on("vector")
                        def _(e, grp=grp, dmp=dmp, n=dn_mm_hist, spv=spill_v):
                            wait(e, stn, n)
                            if grp == 0 and spv:
                                wait(e, sv, spv)
                            for kk in range(2):
                                dmc = 2 * dmp + kk
                                if grp == 0:
                                    ins = e.tensor_copy(ffn_acc[:, 512 * dmc:512 * (dmc + 1)],
                                                        ps_dn[:, 512 * kk:512 * (kk + 1)])
                                else:
                                    ins = e.tensor_add(ffn_acc[:, 512 * dmc:512 * (dmc + 1)],
                                                       ffn_acc[:, 512 * dmc:512 * (dmc + 1)],
                                                       ps_dn[:, 512 * kk:512 * (kk + 1)])
                            ins.then_inc(sv.h, 1)
                            sv.inc()
                        acc_ev_hist = sv.n
                    yield
                acc_done = sv.n

                spill_ev = 0
                for tq in range(4):
                    for dmg in range(4):
                        @on("tensor")
                        def _(e, tq=tq, dmg=dmg, n=acc_done, sev=spill_ev):
                            wait(e, sv, n)
                            if sev:
                                wait(e, sv, sev)
                            for k in range(4):
                                dmc = 4 * dmg + k
                                ins = e.transpose(ps_dn[:, 128 * k:128 * (k + 1)],
                                                  ffn_acc[:, 512 * dmc + 128 * tq:512 * dmc + 128 * (tq + 1)],
                                                  idf[:])
                            ins.then_inc(stn.h, 1)
                            stn.inc()
                        tr_n = stn.n

                        @on("vector")
                        def _(e, tq=tq, dmg=dmg, tr_n=tr_n, otp=(otm_st.seq - 1)):
                            wait(e, stn, tr_n)
                            if otp >= 0:
                                otm_st.wait_seq(e, otp)
                            e.tensor_copy(
                                otm[0][:].rearrange("p (a b) -> p a b", b=128)[:, 4 * (dmg % 2):4 * (dmg % 2) + 4, :],
                                ps_dn[:, 0:512].rearrange("p (a b) -> p a b", b=128)).then_inc(sv.h, 1)
                            sv.inc()
                        spill_ev = sv.n

                        if dmg % 2 == 1:
                            @on("sync")
                            def _(e, th=th, tq=tq, dmg=dmg, n=spill_ev):
                                wait(e, sv, n)
                                otm_st.issue(e, ffn_sp[512 * th + 128 * tq:512 * th + 128 * (tq + 1),
                                                       1024 * (dmg // 2):1024 * (dmg // 2 + 1)],
                                             otm[0][:])
                    spill_v = sv.n
                    yield
            while True:
                yield

        ffn_gen = ffn_emit()

    l1q_gen = None
    if stage >= 4:
        def l1q_emit():
            stQ1 = contextlib.ExitStack()
            sbQ1 = mk_sb(stQ1)
            psQ1 = mk_ps(stQ1)
            wq_t = [sbQ1(f"l1w{i}", [128, 2048], BF16, side="right") for i in range(2)]
            qf = sbQ1("l1qf", [128, 2048], F32, side="right")
            wh_t = sbQ1("l1wh", [128, 2048], BF16, side="right")
            qtr = sbQ1("l1tr", [128, 2048], BF16, side="right")
            gsc = sbQ1("l1gs", [128, 32], F32, side="right")
            sclb = sbQ1("l1sc", [128, 32], BF16, side="right")
            rsf = sbQ1("l1rs", [128, 32], F32, side="right")
            psq1 = psQ1("psq1", [128, 2048], BF16)
            ld_l1 = DmaStream(nc, ex, "l1l", 2)
            st_l1 = DmaStream(nc, ex, "l1s", 2)
            jobs = [("gu", t) for t in range(16)] + [("p", t) for t in range(16)]
            qdone = []
            trdone = []
            evdone = []
            for j, (kind, t) in enumerate(jobs):
                W = 2048 if kind == "gu" else 1024
                G = W // 64

                @on("gpsimd")
                def _(e, j=j, kind=kind, t=t, W=W):
                    if j >= 2:
                        wait(e, sv, qdone[j - 2])
                    src_t = wgu_sh if kind == "gu" else wp_sh
                    ld_l1.issue(e, wq_t[j % 2][:, :W], src_t[128 * t:128 * (t + 1), :])

                @on("vector")
                def _(e, j=j, W=W, G=G):
                    ld_l1.wait_seq(e, j)
                    wg = wq_t[j % 2][:, :W].rearrange("p (g k) -> p g k", k=64)
                    e.reduce_sum(gsc[:, :G].unsqueeze(2), wg, AX, apply_absolute_value=True)
                    e.drain()
                    e.tensor_scalar(gsc[:, :G], gsc[:, :G], 1.0 / 64.0, None, ALU.mult)
                    e.drain()
                    e.tensor_copy(sclb[:, :G], gsc[:, :G])
                    e.drain()
                    e.tensor_scalar(sclb[:, :G], sclb[:, :G], 1e-8, None, ALU.max)
                    e.drain()
                    e.reciprocal(rsf[:, :G], sclb[:, :G])
                    e.drain()
                    e.tensor_tensor(qf[:, :W].rearrange("p (g k) -> p g k", k=64), wg,
                                    rsf[:, :G].unsqueeze(2).broadcast_to([128, G, 64]), ALU.mult)
                    e.drain()
                    e.tensor_scalar(qf[:, :W], qf[:, :W], RND, RND, ALU.add, ALU.subtract)
                    e.drain()
                    e.tensor_scalar(qf[:, :W], qf[:, :W], -1.0, 1.0, ALU.max, ALU.min)
                    e.drain()
                    e.tensor_tensor(wh_t[:, :W].rearrange("p (g k) -> p g k", k=64),
                                    qf[:, :W].rearrange("p (g k) -> p g k", k=64),
                                    sclb[:, :G].unsqueeze(2).broadcast_to([128, G, 64]),
                                    ALU.mult).then_inc(sv.h, 1)
                    sv.inc()
                qdone.append(sv.n)

                @on("tensor")
                def _(e, j=j, W=W, n=qdone[j], prev=(evdone[j - 2] if j >= 2 else 0)):
                    wait(e, sv, n)
                    if prev:
                        wait(e, sv, prev)
                    for bb in range(W // 128):
                        ins = e.transpose(psq1[:, 128 * bb:128 * (bb + 1)],
                                          wh_t[:, 128 * bb:128 * (bb + 1)], idb[:])
                    ins.then_inc(stn.h, 1)
                    stn.inc()
                trdone.append(stn.n)

                @on("vector")
                def _(e, j=j, W=W, n=trdone[j], stp=(j - 1)):
                    wait(e, stn, n)
                    if stp >= 0:
                        st_l1.wait_seq(e, stp)
                    e.tensor_copy(qtr[:, :W], psq1[:, :W]).then_inc(sv.h, 1)
                    sv.inc()
                evdone.append(sv.n)

                @on("sync")
                def _(e, j=j, kind=kind, t=t, W=W, n=evdone[j]):
                    wait(e, sv, n)
                    if kind == "gu":
                        dst = qgu_in[:, 128 * t:128 * (t + 1)].rearrange("(a p) q -> p a q", p=128)
                    else:
                        dst = qp_in[:, 128 * t:128 * (t + 1)].rearrange("(a p) q -> p a q", p=128)
                    st_l1.issue(e, dst, qtr[:, :W].rearrange("p (a q) -> p a q", q=128))
                if j % 2 == 1:
                    yield
            @on("gpsimd")
            def _(e, nj=len(jobs)):
                st_l1.wait_seq(e, nj - 1)
                st_l1.wait_seq(e, nj - 2)
                e.collective_compute("AllGather", ALU.bypass, replica_groups=RG8,
                                     ins=[qgu_in.tensor.ap().opt()],
                                     outs=[qgu_out.tensor.ap().opt()]).then_inc(cc["qgu"].h, 1)
                cc["qgu"].inc()
                e.collective_compute("AllGather", ALU.bypass, replica_groups=RG8,
                                     ins=[qp_in.tensor.ap().opt()],
                                     outs=[qp_out.tensor.ap().opt()]).then_inc(cc["qp"].h, 1)
                cc["qp"].inc()
            l1q_state["st_l1"] = st_l1
            stQ1.close()
            while True:
                yield

        l1q_gen = l1q_emit()

    @on("sync")
    def _(e):
        wait(e, sv, qkv_done_v)
        for dst, src in [(tri[:], tri_i[:, :]), (dd[:], dd_i[:, :]),
                         (sln[:], sln_i[:, :]), (btab[:], btab_i[:, :])]:
            ld_attc.issue(e, dst, src)

    @on("gpsimd")
    def _(e):
        wait(e, sv, qkv_done_v)
        ld_wc.issue(e, wos_sb[:].rearrange("p (a q) -> p a q", a=4),
                    wos_i[:, :].rearrange("(a p) q -> p a q", p=128))

    stt_hist = 0
    exp_h = 0
    ctx_h = 0
    div_hist = 0
    den_bcast_hist = 0
    odd_dma = []
    wo_evac_hist = [0, 0]
    att_const_waited = [False]

    gens = [g for g in (ffn_gen,) if g is not None]
    if stage >= 4:
        gens = [ffn_gen, l1q_gen]
        import itertools
        def _seq():
            for _ in range(17):
                next(l1q_gen); yield
            while True:
                next(ffn_gen); yield
        _sq = _seq()
        def drain_ffn(k=1):
            for _ in range(k):
                next(_sq)
    else:
        def drain_ffn(k=1):
            if ffn_gen is not None:
                for _ in range(k):
                    next(ffn_gen)

    for qt in range(8):
        odd_seqs = []
        for h in range(8):
            drain_ffn(1)
            kh = h // 4
            nblk = 4 * (qt + 1)
            for jtq in range(qt + 1):
                for jl in range(4):
                    jt = 4 * jtq + jl

                    @on("tensor")
                    def _(e, h=h, qt=qt, jt=jt, kh=kh,
                          sttp=stt_hist, divp=div_hist):
                        if jt == 0 and divp:
                            wait(e, sv, divp)
                        if sttp:
                            wait(e, sv, sttp)
                        if h == 0 and qt == 0 and jt == 0:
                            wait(e, sv, qkv_done_v)
                        e.matmul(ps_s[:, :],
                                 KT[64 * kh:64 * (kh + 1), 128 * jt:128 * (jt + 1)],
                                 QT[64 * kh:64 * (kh + 1), (h % 4) * S + 512 * qt:(h % 4) * S + 512 * (qt + 1)],
                                 start=True, stop=True,
                                 tile_position=(64 * kh, 0)).then_inc(stn.h, 1)
                        stn.inc()
                    mm_n = stn.n

                    @on("vector")
                    def _(e, h=h, jl=jl, mm_n=mm_n, expp=exp_h):
                        wait(e, stn, mm_n)
                        if jl == 0 and expp:
                            wait(e, ss, expp)
                        if not att_const_waited[0]:
                            ld_attc.wait_all(e)
                            att_const_waited[0] = True
                        e.scalar_tensor_tensor(T4[:, 512 * jl:512 * (jl + 1)],
                                               dd[:, 512 * jl:512 * (jl + 1)],
                                               sln[:, h:h + 1], ps_s[:, :],
                                               ALU.mult, ALU.add).then_inc(sv.h, 1)
                        sv.inc()
                    stt_hist = sv.n

                @on("scalar")
                def _(e, h=h, dq=qt - jtq, sttn=stt_hist, ctxp=ctx_h):
                    wait(e, sv, sttn)
                    if ctxp:
                        wait(e, stn, ctxp)
                    col = 8 * h + dq
                    e.activation(T4[:], T4[:], AF.Tanh,
                                 bias=btab[:, col:col + 1], scale=1.0 / 240.0)
                    e.drain()
                    e.activation(E4[:], T4[:], AF.Exp,
                                 bias=cbias[:, 2:3], scale=30.0).then_inc(ss.h, 1)
                    ss.inc()
                exp_h = ss.n

                if jtq == qt:
                    @on("vector")
                    def _(e, expn=exp_h):
                        wait(e, ss, expn)
                        for jl in range(4):
                            if jl > 0:
                                e.memset(E4[:, 512 * jl:512 * jl + 128 * jl], 0.0)
                            ins = e.tensor_mul(E4[:, 512 * jl + 128 * jl:512 * jl + 128 * (jl + 1)],
                                               E4[:, 512 * jl + 128 * jl:512 * jl + 128 * (jl + 1)],
                                               tri[:])
                        ins.then_inc(sv.h, 1)
                        sv.inc()
                    e_ready = ("sv", sv.n)
                else:
                    e_ready = ("ss", exp_h)

                for jl in range(4):
                    jt = 4 * jtq + jl

                    @on("tensor")
                    def _(e, jt=jt, jl=jl, kh=kh, nblk=nblk,
                          e_ready=e_ready, divp=div_hist):
                        if jl == 0:
                            if e_ready[0] == "sv":
                                wait(e, sv, e_ready[1])
                            else:
                                wait(e, ss, e_ready[1])
                        if jt == 0 and divp:
                            wait(e, sv, divp)
                        e.matmul(ps_c[0:65, :], VE[:, 130 * jt + 65 * kh:130 * jt + 65 * kh + 65],
                                 E4[:, 512 * jl:512 * (jl + 1)], start=(jt == 0),
                                 stop=(jt == nblk - 1)).then_inc(stn.h, 1)
                        stn.inc()
                ctx_h = stn.n
            ctx_done = stn.n

            @on("vector")
            def _(e, ctx_done=ctx_done, bcp=den_bcast_hist):
                wait(e, stn, ctx_done)
                if bcp:
                    wait(e, stn, bcp)
                e.tensor_copy(dnrb[64:65, 0:512], ps_c[64:65, :])
                e.drain()
                e.reciprocal(dnrb[64:65, 0:512], dnrb[64:65, 0:512]).then_inc(sv.h, 1)
                sv.inc()
            den_n = sv.n

            @on("tensor")
            def _(e, den_n=den_n):
                wait(e, sv, den_n)
                e.matmul(ps_s[0:64, 0:512], ones[64:65, 0:64], dnrb[64:65, 0:512],
                         start=True, stop=True, tile_position=(64, 0)).then_inc(stn.h, 1)
                stn.inc()
            den_bcast_hist = stn.n

            @on("vector")
            def _(e, h=h, qt=qt, bc_n=den_bcast_hist, cdp=(ctx_dma.seq - 1)):
                wait(e, stn, bc_n)
                e.tensor_copy(dnrb[0:64, 0:512], ps_s[0:64, 0:512])
                e.drain()
                if h % 2 == 0:
                    dst = CTX[0:64, (h // 2) * S + 512 * qt:(h // 2) * S + 512 * (qt + 1)]
                else:
                    if cdp >= 0:
                        ctx_dma.wait_seq(e, cdp)
                    dst = tmpdiv[0][0:64, :]
                e.tensor_mul(dst, ps_c[0:64, :], dnrb[0:64, 0:512]).then_inc(sv.h, 1)
                sv.inc()
            div_hist = sv.n

            if h % 2 == 1:
                @on("sync")
                def _(e, h=h, qt=qt, divn=div_hist):
                    wait(e, sv, divn)
                    seq = ctx_dma.issue(
                        e, CTX[64:128, (h // 2) * S + 512 * qt:(h // 2) * S + 512 * (qt + 1)],
                        tmpdiv[0][0:64, :])
                    odd_seqs.append(seq)

        wo_dep_v = div_hist
        for tt in range(4):
            for dmc in range(4):
                wi = 4 * tt + dmc
                wp = wi % 2

                @on("tensor")
                def _(e, qt=qt, tt=tt, dmc=dmc, wp=wp, wo_dep_v=wo_dep_v,
                      odd_seqs=list(odd_seqs), evp=wo_evac_hist[wp],
                      first=(qt == 0 and tt == 0 and dmc == 0)):
                    wait(e, sv, wo_dep_v)
                    if first:
                        ld_wc.wait_all(e)
                    for sq in odd_seqs:
                        ctx_dma.wait_seq(e, sq)
                    if evp:
                        wait(e, sv, evp)
                    for ac in range(4):
                        ins = e.matmul(ps_w[:, 512 * wp:512 * (wp + 1)],
                                       CTX[:, ac * S + 512 * qt + 128 * tt:ac * S + 512 * qt + 128 * (tt + 1)],
                                       wos_sb[:, ac * D + 512 * dmc:ac * D + 512 * (dmc + 1)],
                                       start=(ac == 0), stop=(ac == 3))
                    ins.then_inc(stn.h, 1)
                    stn.inc()
                wo_n = stn.n

                @on("vector")
                def _(e, wp=wp, wo_n=wo_n, stp=(rs_st.seq - 2)):
                    wait(e, stn, wo_n)
                    if stp >= 0:
                        rs_st.wait_seq(e, stp)
                    e.tensor_copy(wo_out[wp][:], ps_w[:, 512 * wp:512 * (wp + 1)]).then_inc(sv.h, 1)
                    sv.inc()
                wo_evac_hist[wp] = sv.n

                @on("sync")
                def _(e, qt=qt, tt=tt, dmc=dmc, wp=wp, evn=wo_evac_hist[wp]):
                    wait(e, sv, evn)
                    rs_st.issue(e, rs_in[512 * qt + 128 * tt:512 * qt + 128 * (tt + 1),
                                         512 * dmc:512 * (dmc + 1)], wo_out[wp][:])

    drain_ffn(40)

    tap_store("ctx", CTX[:])

    @on("gpsimd")
    def _(e):
        rs_st.wait_seq(e, rs_st.seq - 1)
        rs_st.wait_seq(e, rs_st.seq - 2)
        e.collective_compute("ReduceScatter", ALU.add, replica_groups=RG4,
                             ins=[rs_in.tensor.ap().opt()],
                             outs=[rs_out.tensor.ap().opt()]).then_inc(cc["rs"].h, 1)
        cc["rs"].inc()

    if stage >= 3 and "nq" in tap:
        @on("sync")
        def _(e):
            tbat["nq"].wait_all(e)
    if stage >= 3:
        stF.close()
    stC_right_users_v = sv.n
    stC_users_stn = stn.n
    stC.close()
    stQT.close()
    stD = contextlib.ExitStack()
    sbD = mk_sb(stD)
    psD = mk_ps(stD)
    x1 = sbD("x1", [128, 8 * D], F32)
    bob = sbD("bob", [128, D], F32, side="right")
    stB2 = contextlib.ExitStack()
    ps_b2 = mk_ps(stB2)("ps_b2", [128, 512], F32)
    ld_boc = DmaBatch(nc, ex, "lboc")
    ld_xo = DmaBatch(nc, ex, "lxo")

    @on("sync")
    def _(e, n=stC_right_users_v):
        wait(e, sv, n)
        for t in ("ctx", "q", "k", "v"):
            if t in tbat:
                tbat[t].wait_all(e)
        ld_boc.issue(e, bob[0:1, :], bo_i[:, :])
        for tt in range(8):
            ld_xo.issue(e, x1[:, tt * D:(tt + 1) * D], x_own[128 * tt:128 * (tt + 1), :])

    if "x1a" in tap:
        @on("sync")
        def _(e):
            ld_xo.wait_all(e)
            for tt in range(8):
                tbat["x1a"].issue(e, tap["x1a"][128 * tt:128 * (tt + 1), :], x1[:, tt * D:(tt + 1) * D])

    bo_prev_ev = 0
    for c4 in range(4):
        @on("tensor")
        def _(e, c4=c4, prev=bo_prev_ev, nn=stC_right_users_v):
            if c4 == 0:
                ld_boc.wait_all(e)
                wait(e, sv, nn)
            if prev:
                wait(e, sv, prev)
            e.matmul(ps_b2[:, :], ones[0:1, 0:128], bob[0:1, 512 * c4:512 * (c4 + 1)],
                     start=True, stop=True).then_inc(stn.h, 1)
            stn.inc()
        bo_mm_n = stn.n

        @on("vector")
        def _(e, c4=c4, n=bo_mm_n):
            wait(e, stn, n)
            e.tensor_copy(bob[:, 512 * c4:512 * (c4 + 1)], ps_b2[:, :]).then_inc(sv.h, 1)
            sv.inc()
        bo_prev_ev = sv.n
    bo_ev_n = sv.n

    @on("vector")
    def _(e, n=bo_ev_n):
        wait(e, sv, n)
        ld_xo.wait_all(e)
        if "x1a" in tap:
            tbat["x1a"].wait_all(e)
        for tt in range(8):
            e.tensor_add(x1[:, tt * D:(tt + 1) * D], x1[:, tt * D:(tt + 1) * D], bob[:])
        e.engine_nop().then_inc(sv.h, 1)
        sv.inc()
    x1_bo_n = sv.n
    stB2.close()

    if "x1b" in tap:
        @on("sync")
        def _(e, n=x1_bo_n):
            wait(e, sv, n)
            for tt in range(8):
                tbat["x1b"].issue(e, tap["x1b"][128 * tt:128 * (tt + 1), :], x1[:, tt * D:(tt + 1) * D])

    if stage >= 3:
        @on("gpsimd")
        def _(e, n=x1_bo_n):
            wait(e, sv, n)
            if "x1b" in tap:
                tbat["x1b"].wait_all(e)
            otm_st.wait_seq(e, otm_st.seq - 1)
            for tt in range(8):
                e.dma_start(out=x1[:, tt * D:(tt + 1) * D], in_=ffn_sp[128 * tt:128 * (tt + 1), :],
                            accum_op=ALU.add).then_inc(sgp.h, 16)
                sgp.inc(16)
        x1_f_n = sgp.n
    else:
        x1_f_n = 0

    @on("gpsimd")
    def _(e, n=x1_bo_n, fn=x1_f_n):
        wait(e, sv, n)
        if fn:
            wait(e, sgp, fn)
        wait(e, cc["rs"], cc["rs"].n)
        for tt in range(8):
            e.dma_start(out=x1[:, tt * D:(tt + 1) * D], in_=rs_out[128 * tt:128 * (tt + 1), :],
                        accum_op=ALU.add).then_inc(sgp.h, 16)
            sgp.inc(16)
    x1_acc_n = sgp.n

    if "rsout" in tap:
        @on("sync")
        def _(e):
            wait(e, cc["rs"], cc["rs"].n)
            for tt in range(8):
                tbat["rsout"].issue(e, tap["rsout"][128 * tt:128 * (tt + 1), :],
                                    rs_out[128 * tt:128 * (tt + 1), :])

    if "bob" in tap:
        @on("sync")
        def _(e, n=bo_ev_n):
            wait(e, sv, n)
            tbat["bob"].issue(e, tap["bob"][:, :], bob[:])

    if "ffn0" in tap:
        @on("sync")
        def _(e, n=x1_f_n):
            if n:
                wait(e, sgp, n)
            for tt in range(8):
                tbat["ffn0"].issue(e, tap["ffn0"][128 * tt:128 * (tt + 1), :],
                                   ffn_sp[128 * tt:128 * (tt + 1), :])

    if "x1" in tap:
        @on("sync")
        def _(e, n=x1_acc_n):
            wait(e, sgp, n)
            for tt in range(8):
                tbat["x1"].issue(e, tap["x1"][128 * tt:128 * (tt + 1), :], x1[:, tt * D:(tt + 1) * D])

    if stage >= 4:
        import dataclasses
        xsl = dint("xsl", [S, 512], BF16)
        stL = contextlib.ExitStack()
        sbL = mk_sb(stL)
        xn1_fm = sbL("xn1_fm", [128, 16 * TOK], BF16)
        ones_bf = sbL("ones_bf", [128, 1], BF16)

        @on("vector")
        def _(e, n=x1_acc_n):
            wait(e, sgp, n)
            rs_st.wait_seq(e, rs_st.seq - 1)
            rs_st.wait_seq(e, rs_st.seq - 2)
            e.memset(ones_bf[:], 1.0).then_inc(sv.h, 1); sv.inc()

        stLN = contextlib.ExitStack()
        sbLN = mk_sb(stLN)
        psLN = mk_ps(stLN)
        xnt1 = [sbLN(f"xnt1_{i}", [128, D], BF16, side="right") for i in range(2)]
        pstL = [psLN(f"pstL{i}", [128, 2048], BF16) for i in range(2)]
        st_xg = DmaStream(nc, ex, "sxg", 2)
        ln1_nq = []
        ln1_ev = [0, 0]
        for tt in range(8):
            cb = 8 * (tt % 2)

            @on("scalar")
            def _(e, tt=tt, cb=cb, n=x1_acc_n, evp=ln1_ev[tt % 2]):
                if tt == 0:
                    wait(e, sgp, n)
                if tt >= 2:
                    wait(e, sv, ln1_nq[tt - 2])
                    wait(e, sv, evp)
                    st_xg.wait_seq(e, tt - 2)
                e.activation(xnt1[tt % 2][:], x1[:, tt * D:(tt + 1) * D], AF.Square,
                             accum_out=smalls[:, cb + 1:cb + 2]).then_inc(ss.h, 1)
                ss.inc()
            sq_n = ss.n

            @on("vector")
            def _(e, tt=tt, sq_n=sq_n, cb=cb):
                wait(e, ss, sq_n)
                e.reduce_sum(smalls[:, cb:cb + 1], x1[:, tt * D:(tt + 1) * D], AX)
                e.drain()
                e.tensor_scalar(smalls[:, cb:cb + 1], smalls[:, cb:cb + 1], 1.0 / D, None, ALU.mult)
                e.drain()
                e.tensor_tensor(smalls[:, cb + 2:cb + 3], smalls[:, cb:cb + 1],
                                smalls[:, cb:cb + 1], ALU.mult)
                e.drain()
                e.scalar_tensor_tensor(smalls[:, cb + 3:cb + 4], smalls[:, cb + 1:cb + 2],
                                       1.0 / D, smalls[:, cb + 2:cb + 3],
                                       ALU.mult, ALU.subtract).then_inc(sv.h, 1)
                sv.inc()
            var_n = sv.n

            @on("scalar")
            def _(e, var_n=var_n, cb=cb):
                wait(e, sv, var_n)
                e.activation(smalls[:, cb + 4:cb + 5], smalls[:, cb + 3:cb + 4],
                             AF.Sqrt, bias=cbias[:, 0:1]).then_inc(ss.h, 1)
                ss.inc()
            sqr_n = ss.n

            @on("vector")
            def _(e, tt=tt, sqr_n=sqr_n, cb=cb):
                wait(e, ss, sqr_n)
                e.reciprocal(smalls[:, cb + 5:cb + 6], smalls[:, cb + 4:cb + 5])
                e.drain()
                e.tensor_scalar(xnt1[tt % 2][:], x1[:, tt * D:(tt + 1) * D],
                                smalls[:, cb:cb + 1], smalls[:, cb + 5:cb + 6],
                                ALU.subtract, ALU.mult).then_inc(sv.h, 1)
                sv.inc()
            ln1_nq.append(sv.n)

            @on("tensor")
            def _(e, tt=tt, n=sv.n, prev=ln1_ev[tt % 2]):
                wait(e, sv, n)
                if prev:
                    wait(e, sv, prev)
                for dc in range(16):
                    ins = e.transpose(pstL[tt % 2][:, 128 * dc:128 * (dc + 1)],
                                      xnt1[tt % 2][:, 128 * dc:128 * (dc + 1)], idb[:])
                ins.then_inc(stn.h, 1)
                stn.inc()
            tr_n = stn.n

            @on("vector")
            def _(e, tt=tt, tr_n=tr_n):
                wait(e, stn, tr_n)
                e.tensor_copy(
                    xn1_fm[:].rearrange("p (a b) -> p a b", b=TOK)[:, :, 128 * tt:128 * (tt + 1)],
                    pstL[tt % 2][:].rearrange("p (a b) -> p a b", b=128)).then_inc(sv.h, 1)
                sv.inc()
            ln1_ev[tt % 2] = sv.n

            @on("sync")
            def _(e, tt=tt, n=ln1_nq[tt]):
                wait(e, sv, n)
                st_xg.issue(e, xg_in[128 * tt:128 * (tt + 1), :], xnt1[tt % 2][:])
        xn1_fm_done = sv.n

        if "xn1" in tap:
            @on("sync")
            def _(e):
                for j in range(8):
                    st_xg.wait_seq(e, j)
                tbat["xn1"].issue(e, tap["xn1"][:, :], xg_in[:, :])

        @on("gpsimd")
        def _(e):
            for j in range(8):
                st_xg.wait_seq(e, j)
            e.collective_compute("AllGather", ALU.bypass, replica_groups=RG4,
                                 ins=[xg_in.tensor.ap().opt()],
                                 outs=[xg_out.tensor.ap().opt()]).then_inc(cc["xg"].h, 1)
            cc["xg"].inc()

        ld_xsl = DmaBatch(nc, ex, "lxsl")

        @on("sync")
        def _(e):
            wait(e, cc["xg"], cc["xg"].n)
            pid = nc.partition_id(engines=[mybir.EngineType.SP])
            r512 = (pid % 4) * 512
            for k in range(4):
                ap = xg_out[1024 * k:1024 * (k + 1), 0:512]
                ap2 = dataclasses.replace(ap, offset=ap.offset + r512)
                ld_xsl.issue(e, xsl[1024 * k:1024 * (k + 1), :], ap2)
        stLN.close()

        stF1 = contextlib.ExitStack()
        sbF1 = mk_sb(stF1)
        psF1 = mk_ps(stF1)

        def ffn1_emit():
            ffn1_acc = sbF1("ffn1_acc", [128, 16 * 512], F32)
            p1_grp = sbF1("p1_grp", [128, 8 * 512], BF16)
            gsil1 = sbF1("gsil1", [128, 1024], BF16)
            sq_bf = sbF1("sq_bf", [128, 1024], BF16)
            w1str = [sbF1(f"w1s{i}", [128, 4 * 256], BF16) for i in range(2)]
            wp1t = [sbF1(f"wp1{i}", [128, 4 * 256], BF16) for i in range(2)]
            otm1 = sbF1("otm1", [128, 2048], F32)
            stg = sbF1("stg", [128, 512], F32)
            invc = sbF1("invc", [128, 4], F32)
            ps_gu1 = psF1("ps_gu1", [128, 1024], F32)
            ps_dn1 = psF1("ps_dn1", [128, 1024], F32)
            ps_h1 = psF1("ps_h1", [1, 512], F32)
            ld_w1 = DmaStream(nc, ex, "lw1", 2)
            ld_wp1 = DmaStream(nc, ex, "lwp1", 2)
            ot1_st = DmaStream(nc, ex, "so1", 1)
            ld_w1_cons = []
            ld_wp1_cons = []
            silu_h = 0
            p_h = 0
            dn_h = 0
            acc_h = 0
            spl_v = 0
            hsq_mm = 0

            @on("vector")
            def _(e):
                e.memset(stg[:], 0.0).then_inc(sv.h, 1); sv.inc()

            hsq_read_prev = 0
            for th in range(2):
                first_hsq = True
                for grp in range(8):
                    for s2 in range(4):
                        H = 8 * grp + 2 * s2
                        for gu in range(2):
                            HH = H + 64 * gu
                            for dh in range(4):
                                @on("sync")
                                def _(e, HH=HH, dh=dh,
                                      first=(th == 0 and grp == 0 and s2 == 0 and gu == 0 and dh == 0)):
                                    if first:
                                        wait(e, sv, xn1_fm_done)
                                        wait(e, cc["qgu"], cc["qgu"].n)
                                    if ld_w1.seq >= 2:
                                        wait(e, stn, ld_w1_cons[ld_w1.seq - 2])
                                    ld_w1.issue(e, w1str[ld_w1.seq % 2][:].rearrange("p (a q) -> p a q", q=256),
                                                qgu_out[2048 * (HH // 16) + 512 * dh:
                                                        2048 * (HH // 16) + 512 * (dh + 1),
                                                        128 * (HH % 16):128 * (HH % 16) + 256]
                                                .rearrange("(a p) q -> p a q", p=128))

                                @on("tensor")
                                def _(e, th=th, gu=gu, dh=dh, seq=ld_w1.seq - 1,
                                      sil=silu_h, pn=p_h):
                                    ld_w1.wait_seq(e, seq)
                                    if dh == 0:
                                        if gu == 0 and pn:
                                            wait(e, sv, pn)
                                        if gu == 1 and sil:
                                            wait(e, ss, sil)
                                    for a in range(4):
                                        dc = 4 * dh + a
                                        for i in range(2):
                                            ins = e.matmul(ps_gu1[:, 512 * i:512 * (i + 1)],
                                                           w1str[seq % 2][:, 256 * a + 128 * i:256 * a + 128 * (i + 1)],
                                                           xn1_fm[:, dc * TOK + 512 * th:dc * TOK + 512 * (th + 1)],
                                                           start=(dc == 0), stop=(dc == 15))
                                    ins.then_inc(stn.h, 1)
                                    stn.inc()
                                ld_w1_cons.append(stn.n)
                            gu_n = stn.n
                            if gu == 0:
                                @on("scalar")
                                def _(e, n=gu_n, pn=p_h):
                                    wait(e, stn, n)
                                    if pn:
                                        wait(e, sv, pn)
                                    e.activation(gsil1[:], ps_gu1[:, :], AF.Silu).then_inc(ss.h, 1)
                                    ss.inc()
                                silu_h = ss.n
                            else:
                                @on("vector")
                                def _(e, s2=s2, n=gu_n, sil=silu_h, dmm=dn_h):
                                    wait(e, stn, n)
                                    wait(e, ss, sil)
                                    if dmm:
                                        wait(e, stn, dmm)
                                    e.tensor_mul(p1_grp[:, 1024 * s2:1024 * (s2 + 1)],
                                                 gsil1[:], ps_gu1[:, :]).then_inc(sv.h, 1)
                                    sv.inc()
                                p_h = sv.n

                                @on("scalar")
                                def _(e, s2=s2, n=p_h, hs=hsq_mm):
                                    wait(e, sv, n)
                                    if hs:
                                        wait(e, stn, hs)
                                    e.activation(sq_bf[:], p1_grp[:, 1024 * s2:1024 * (s2 + 1)],
                                                 AF.Square).then_inc(ss.h, 1)
                                    ss.inc()
                                sq_n2 = ss.n

                                @on("tensor")
                                def _(e, n=sq_n2, first=first_hsq, last=(grp == 7 and s2 == 3),
                                      hrp=hsq_read_prev):
                                    wait(e, ss, n)
                                    if first and hrp:
                                        wait(e, sv, hrp)
                                    for i in range(2):
                                        ins = e.matmul(ps_h1[0:1, :], ones_bf[:],
                                                       sq_bf[:, 512 * i:512 * (i + 1)],
                                                       start=(first and i == 0),
                                                       stop=(last and i == 1)).then_inc(stn.h, 1)
                                        stn.inc()
                                hsq_mm = stn.n
                                first_hsq = False
                    p1_all = p_h

                    for dmp in range(8):
                        for ih in range(2):
                            @on("sync")
                            def _(e, grp=grp, dmp=dmp, ih=ih, first=(ld_wp1.seq == 0)):
                                if first:
                                    wait(e, cc["qp"], cc["qp"].n)
                                if ld_wp1.seq >= 2:
                                    wait(e, stn, ld_wp1_cons[ld_wp1.seq - 2])
                                ld_wp1.issue(e, wp1t[ld_wp1.seq % 2][:].rearrange("p (a q) -> p a q", q=256),
                                             qp_out[1024 * grp + 512 * ih:1024 * grp + 512 * (ih + 1),
                                                    256 * dmp:256 * (dmp + 1)]
                                             .rearrange("(a p) q -> p a q", p=128))

                            @on("tensor")
                            def _(e, dmp=dmp, ih=ih, seq=ld_wp1.seq - 1,
                                  pn=p1_all, accp=acc_h, spv=spl_v):
                                ld_wp1.wait_seq(e, seq)
                                if ih == 0:
                                    wait(e, sv, pn)
                                    if accp:
                                        wait(e, sv, accp)
                                    if dmp == 0 and spv:
                                        wait(e, sv, spv)
                                for a in range(4):
                                    i = 4 * ih + a
                                    for kk in range(2):
                                        ins = e.matmul(ps_dn1[:, 512 * kk:512 * (kk + 1)],
                                                       wp1t[seq % 2][:, 256 * a + 128 * kk:256 * a + 128 * (kk + 1)],
                                                       p1_grp[:, 512 * i:512 * (i + 1)],
                                                       start=(i == 0), stop=(i == 7))
                                ins.then_inc(stn.h, 1)
                                stn.inc()
                            ld_wp1_cons.append(stn.n)
                        dn_h = stn.n

                        @on("vector")
                        def _(e, grp=grp, dmp=dmp, n=dn_h, spv=spl_v):
                            wait(e, stn, n)
                            if grp == 0 and spv:
                                wait(e, sv, spv)
                            for kk in range(2):
                                dmc = 2 * dmp + kk
                                if grp == 0:
                                    ins = e.tensor_copy(ffn1_acc[:, 512 * dmc:512 * (dmc + 1)],
                                                        ps_dn1[:, 512 * kk:512 * (kk + 1)])
                                else:
                                    ins = e.tensor_add(ffn1_acc[:, 512 * dmc:512 * (dmc + 1)],
                                                       ffn1_acc[:, 512 * dmc:512 * (dmc + 1)],
                                                       ps_dn1[:, 512 * kk:512 * (kk + 1)])
                            ins.then_inc(sv.h, 1)
                            sv.inc()
                        acc_h = sv.n
                    yield
                acc_done = sv.n

                @on("vector")
                def _(e, n=hsq_mm):
                    wait(e, stn, n)
                    e.tensor_copy(stg[0:1, :], ps_h1[0:1, :])
                    e.drain()
                    e.tensor_scalar(stg[0:1, :], stg[0:1, :], 1.0 / 8192.0, None, ALU.mult).then_inc(sv.h, 1)
                    sv.inc()
                mh = sv.n
                hsq_read_prev = mh

                @on("scalar")
                def _(e, n=mh):
                    wait(e, sv, n)
                    e.activation(stg[0:1, :], stg[0:1, :], AF.Sqrt,
                                 bias=cbias[0:1, 3:4]).then_inc(ss.h, 1)
                    ss.inc()
                sh = ss.n

                @on("vector")
                def _(e, n=sh):
                    wait(e, ss, n)
                    e.reciprocal(stg[0:1, :], stg[0:1, :]).then_inc(sv.h, 1)
                    sv.inc()
                ih = sv.n

                @on("tensor")
                def _(e, n=ih, dnn=dn_h):
                    wait(e, sv, n)
                    wait(e, stn, dnn)
                    for k in range(4):
                        ins = e.transpose(ps_dn1[:, 128 * k:128 * (k + 1)],
                                          stg[:, 128 * k:128 * (k + 1)], idf[:])
                    ins.then_inc(stn.h, 1)
                    stn.inc()
                itr = stn.n

                @on("vector")
                def _(e, n=itr):
                    wait(e, stn, n)
                    for k in range(4):
                        ins = e.tensor_copy(invc[:, k:k + 1], ps_dn1[:, 128 * k:128 * k + 1])
                    ins.then_inc(sv.h, 1)
                    sv.inc()
                inv_done = sv.n

                spill_ev = 0
                for tq in range(4):
                    for dmg in range(4):
                        @on("tensor")
                        def _(e, tq=tq, dmg=dmg, n=acc_done, iv=inv_done, sev=spill_ev):
                            wait(e, sv, n)
                            wait(e, sv, iv)
                            if sev:
                                wait(e, sv, sev)
                            for k in range(4):
                                dmc = 4 * dmg + k
                                ins = e.transpose(ps_dn1[:, 128 * k:128 * (k + 1)],
                                                  ffn1_acc[:, 512 * dmc + 128 * tq:512 * dmc + 128 * (tq + 1)],
                                                  idf[:])
                            ins.then_inc(stn.h, 1)
                            stn.inc()
                        tr_n2 = stn.n

                        @on("vector")
                        def _(e, th=th, tq=tq, dmg=dmg, tr_n2=tr_n2, otp=(ot1_st.seq - 1)):
                            wait(e, stn, tr_n2)
                            if otp >= 0:
                                ot1_st.wait_seq(e, otp)
                            e.tensor_scalar(
                                otm1[:].rearrange("p (a b) -> p a b", b=128)[:, 4 * dmg:4 * dmg + 4, :],
                                ps_dn1[:, 0:512].rearrange("p (a b) -> p a b", b=128),
                                invc[:, tq:tq + 1], None, ALU.mult).then_inc(sv.h, 1)
                            sv.inc()
                        spill_ev = sv.n
                    spl_v = sv.n

                    @on("sync")
                    def _(e, th=th, tq=tq, n=spl_v, fwv=x1_f_n):
                        wait(e, sv, n)
                        wait(e, sgp, fwv)
                        ot1_st.issue(e, ffn_sp[512 * th + 128 * tq:512 * th + 128 * (tq + 1), :],
                                     otm1[:])
                    yield
            ffn1_state["ot1_st"] = ot1_st
            while True:
                yield

        ffn1_state = {}

        def gram_emit():
            stG = contextlib.ExitStack()
            sbG = mk_sb(stG)
            psG = mk_ps(stG)
            G_sb = sbG("G_sb", [128, 2048], F32, side="right")
            E1 = sbG("Eg", [128, 2048], BF16, side="right")
            qtrP = sbG("qtrP", [128, 2048], BF16, side="right")
            xslt = [sbG(f"xsl{i}", [128, 512], BF16, side="right") for i in range(2)]
            xgt = [sbG(f"xgt{i}", [128, 1024], BF16, side="right") for i in range(2)]
            pG = psG("pG", [128, 1024], F32)
            ptb = psG("ptb", [128, 1024], BF16)
            ld_xg2 = DmaStream(nc, ex, "lxg2", 2)
            ld_xs2 = DmaStream(nc, ex, "lxs2", 2)
            st_pt = DmaStream(nc, ex, "spt", 1)
            gmm_hist = []
            ev_h = 0
            exp_h = 0
            ptr_h = 0
            ptev_h = 0
            for ib in range(4):
                for jh in range(2):
                    for ct in range(32):
                        @on("sync")
                        def _(e, ib=ib, jh=jh, ct=ct, first=(ld_xs2.seq == 0)):
                            if first:
                                ld_xsl.wait_all(e)
                            if ld_xs2.seq >= 2:
                                wait(e, stn, gmm_hist[ld_xs2.seq - 2])
                            ld_xs2.issue(e, xslt[ld_xs2.seq % 2][:], xsl[128 * ct:128 * (ct + 1), :])
                            ld_xg2.issue(e, xgt[ld_xg2.seq % 2][:],
                                         xg_out[128 * ct:128 * (ct + 1), 1024 * jh:1024 * (jh + 1)])

                        @on("tensor")
                        def _(e, ib=ib, ct=ct, seq=ld_xs2.seq - 1, evh=ev_h):
                            ld_xs2.wait_seq(e, seq)
                            ld_xg2.wait_seq(e, seq)
                            if ct == 0 and evh:
                                wait(e, sv, evh)
                            for k in range(2):
                                ins = e.matmul(pG[:, 512 * k:512 * (k + 1)],
                                               xslt[seq % 2][:, 128 * ib:128 * (ib + 1)],
                                               xgt[seq % 2][:, 512 * k:512 * (k + 1)],
                                               start=(ct == 0), stop=(ct == 31))
                            ins.then_inc(stn.h, 1)
                            stn.inc()
                        gmm_hist.append(stn.n)
                    g_n = stn.n

                    @on("vector")
                    def _(e, jh=jh, n=g_n, exph=exp_h):
                        wait(e, stn, n)
                        if jh == 0 and exph:
                            wait(e, ss, exph)
                        e.tensor_copy(G_sb[:, 1024 * jh:1024 * (jh + 1)], pG[:, :]).then_inc(sv.h, 1)
                        sv.inc()
                    ev_h = sv.n
                    yield

                @on("vector")
                def _(e, n=ev_h):
                    wait(e, sv, n)
                    e.reduce_max(smalls[:, 32:33], G_sb[:], AX)
                    e.drain()
                    e.tensor_scalar(smalls[:, 33:34], smalls[:, 32:33], -1.0, None,
                                    ALU.mult).then_inc(sv.h, 1)
                    sv.inc()
                m_n = sv.n

                @on("scalar")
                def _(e, n=m_n, ptrh=ptr_h):
                    wait(e, sv, n)
                    if ptrh:
                        wait(e, stn, ptrh)
                    e.activation(E1[:], G_sb[:], AF.Exp, bias=smalls[:, 33:34],
                                 accum_out=smalls[:, 34:35]).then_inc(ss.h, 1)
                    ss.inc()
                exp_h = ss.n

                @on("vector")
                def _(e, n=exp_h):
                    wait(e, ss, n)
                    e.reciprocal(smalls[:, 35:36], smalls[:, 34:35])
                    e.drain()
                    e.tensor_scalar(E1[:], E1[:], smalls[:, 35:36], None, ALU.mult).then_inc(sv.h, 1)
                    sv.inc()
                p_n = sv.n

                for half in range(2):
                    @on("tensor")
                    def _(e, half=half, n=p_n, ptevh=ptev_h):
                        wait(e, sv, n)
                        if ptevh:
                            wait(e, sv, ptevh)
                        for k in range(8):
                            jc = 8 * half + k
                            ins = e.transpose(ptb[:, 128 * k:128 * (k + 1)],
                                              E1[:, 128 * jc:128 * (jc + 1)], idb[:])
                        ins.then_inc(stn.h, 1)
                        stn.inc()
                    ptr_h = stn.n

                    @on("vector")
                    def _(e, half=half, n=ptr_h, stp=(st_pt.seq - 1)):
                        wait(e, stn, n)
                        if stp >= 0 and half == 0:
                            st_pt.wait_seq(e, stp)
                        e.tensor_copy(qtrP[:, 1024 * half:1024 * (half + 1)], ptb[:]).then_inc(sv.h, 1)
                        sv.inc()
                    ptev_h = sv.n

                @on("sync")
                def _(e, ib=ib, n=ptev_h):
                    wait(e, sv, n)
                    st_pt.issue(e, pt_in[2048 * ib:2048 * (ib + 1), :].rearrange("(a p) q -> p a q", p=128),
                                qtrP[:].rearrange("p (a q) -> p a q", q=128))
                yield

            @on("gpsimd")
            def _(e, ns=st_pt.seq):
                st_pt.wait_seq(e, ns - 1)
                e.collective_compute("AllGather", ALU.bypass, replica_groups=RG4,
                                     ins=[pt_in.tensor.ap().opt()],
                                     outs=[pt_out.tensor.ap().opt()]).then_inc(cc["pt"].h, 1)
                cc["pt"].inc()
            stG.close()

            sbX = mk_sb(stF1)
            psX = mk_ps(stF1)
            ptt = [sbX(f"ptt{i}", [128, 256], BF16) for i in range(2)]
            cxe = sbX("cxe", [128, 512], F32)
            ps_cx = psX("ps_cx", [128, 512], F32)
            ps_tr1 = psX("ps_tr1", [128, 512], F32)
            ld_pt2 = DmaStream(nc, ex, "lpt2", 2)
            ld_pt2_cons = []
            cx_ev = 0
            cx_tr = 0
            cx_add = 0
            for I in range(16):
                for cc2 in range(2):
                    for jp in range(8):
                        @on("sync")
                        def _(e, I=I, jp=jp, first=(ld_pt2.seq == 0)):
                            if first:
                                wait(e, cc["pt"], cc["pt"].n)
                            if ld_pt2.seq >= 2:
                                wait(e, stn, ld_pt2_cons[ld_pt2.seq - 2])
                            base = 8192 * (I // 4) + (16 * (I % 4) + 2 * jp) * 128
                            ld_pt2.issue(e, ptt[ld_pt2.seq % 2][:].rearrange("p (a q) -> p a q", q=128),
                                         pt_out[base:base + 256, :].rearrange("(a p) q -> p a q", p=128))

                        @on("tensor")
                        def _(e, I=I, cc2=cc2, jp=jp, seq=ld_pt2.seq - 1, cxv=cx_ev):
                            ld_pt2.wait_seq(e, seq)
                            if jp == 0 and cxv:
                                wait(e, sv, cxv)
                            for k in range(2):
                                jc = 2 * jp + k
                                ins = e.matmul(ps_cx[:, :],
                                               ptt[seq % 2][:, 128 * k:128 * (k + 1)],
                                               xn1_fm[:, jc * TOK + 512 * cc2:jc * TOK + 512 * (cc2 + 1)],
                                               start=(jp == 0 and k == 0), stop=(jp == 7 and k == 1))
                            ins.then_inc(stn.h, 1)
                            stn.inc()
                        ld_pt2_cons.append(stn.n)
                    cmm = stn.n

                    @on("vector")
                    def _(e, n=cmm, ctr=cx_tr):
                        wait(e, stn, n)
                        if ctr:
                            wait(e, stn, ctr)
                        e.tensor_copy(cxe[:], ps_cx[:, :]).then_inc(sv.h, 1)
                        sv.inc()
                    cx_ev = sv.n

                    @on("tensor")
                    def _(e, n=cx_ev, ca=cx_add):
                        wait(e, sv, n)
                        if ca:
                            wait(e, sv, ca)
                        for k in range(4):
                            ins = e.transpose(ps_tr1[:, 128 * k:128 * (k + 1)],
                                              cxe[:, 128 * k:128 * (k + 1)], idf[:])
                        ins.then_inc(stn.h, 1)
                        stn.inc()
                    cx_tr = stn.n

                    @on("vector")
                    def _(e, I=I, cc2=cc2, n=cx_tr):
                        wait(e, stn, n)
                        for k in range(4):
                            tt = 4 * cc2 + k
                            ins = e.tensor_add(x1[:, tt * D + 128 * I:tt * D + 128 * (I + 1)],
                                               x1[:, tt * D + 128 * I:tt * D + 128 * (I + 1)],
                                               ps_tr1[:, 128 * k:128 * (k + 1)])
                        ins.then_inc(sv.h, 1)
                        sv.inc()
                    cx_add = sv.n
                    yield
            gram_state["cx_done"] = cx_add
            while True:
                yield

        gram_state = {}
        g1 = ffn1_emit()
        g2 = gram_emit()
        next(g1)
        for _ in range(80):
            next(g2)
            next(g1)
        for _ in range(40):
            next(g1)
        for _ in range(40):
            next(g2)

        ffn1_last_v = sv.n

        @on("gpsimd")
        def _(e, n=ffn1_last_v):
            wait(e, sv, n)
            ot1 = ffn1_state["ot1_st"]
            ot1.wait_seq(e, ot1.seq - 1)
            for tt in range(8):
                e.dma_start(out=x1[:, tt * D:(tt + 1) * D], in_=ffn_sp[128 * tt:128 * (tt + 1), :],
                            accum_op=ALU.add).then_inc(sgp.h, 16)
                sgp.inc(16)
        fin_n = sgp.n

        @on("sync")
        def _(e, n=fin_n):
            wait(e, sgp, n)
            for tt in range(8):
                st_out.issue(e, out_own[128 * tt:128 * (tt + 1), :], x1[:, tt * D:(tt + 1) * D])
            st_out.wait_all(e)
            for tb in tbat.values():
                tb.wait_all(e)
        stF1.close()
        stL.close()
        stD.close()
        ex.close()
        return nc

    if stage <= 3:
        @on("sync")
        def _(e, n=x1_acc_n):
            wait(e, sgp, n)
            for tt in range(8):
                st_out.issue(e, out_own[128 * tt:128 * (tt + 1), :], x1[:, tt * D:(tt + 1) * D])
            st_out.wait_all(e)
            for tb in tbat.values():
                tb.wait_all(e)
        stD.close()
        ex.close()
        return nc

    stD.close()
    ex.close()
    return nc


def kernel_stage(inputs, stage=1, taps=()):
    nc = build_nc(stage=stage, taps=taps)
    in_maps = [host_inputs(inputs, c) for c in range(8)]
    res = run_bass_kernel_spmd(nc, in_maps, core_ids=list(range(8)))
    return res


def kernel(**inputs):
    nc = build_nc(stage=4)
    in_maps = [host_inputs(inputs, c) for c in range(8)]
    res = run_bass_kernel_spmd(nc, in_maps, core_ids=list(range(8)))
    out = np.empty((2, 4096, 2048), np.float32)
    for c in range(8):
        b, r = c // 4, c % 4
        out[b, 1024 * r:1024 * (r + 1)] = res.results[c]["out_own"]
    return out


# version 05 (final)
# speedup vs baseline: 1.4656x; 1.0003x over previous
import contextlib
import numpy as np
import ml_dtypes

import concourse.bass as bass
import concourse.mybir as mybir
from concourse.bass_utils import run_bass_kernel_spmd

F32 = mybir.dt.float32
BF16 = mybir.dt.bfloat16
AF = mybir.ActivationFunctionType
ALU = mybir.AluOpType
AX = mybir.AxisListType.X
BF = ml_dtypes.bfloat16

B, S, D = 2, 4096, 2048
DH = 64
TOK = 1024
RG4 = [[0, 1, 2, 3], [4, 5, 6, 7]]
RG8 = [[0, 1, 2, 3, 4, 5, 6, 7]]
RND = 12582912.0
SWAP = [i ^ 1 for i in range(32)]


def host_tables():
    t = {}
    inv_freq = (np.float32(1.0) /
                (np.float32(10000.0) ** (np.arange(0, DH, 2, dtype=np.float32) / np.float32(DH)))).astype(np.float32)
    pos = np.arange(S, dtype=np.float32)
    freqs = (pos[:, None] * inv_freq[None, :]).astype(np.float32)
    cos = np.cos(freqs).astype(np.float32)
    sin = np.sin(freqs).astype(np.float32)
    cos_dup = np.repeat(cos, 2, axis=1)
    sin_sgn = np.empty((S, DH), np.float32)
    sin_sgn[:, 0::2] = -sin
    sin_sgn[:, 1::2] = sin
    t["cos_t"] = np.ascontiguousarray(np.tile(cos_dup.T, (2, 1)))
    t["sin_t"] = np.ascontiguousarray(np.tile(sin_sgn.T, (2, 1)))
    p = np.arange(128)
    t["tri"] = (p[None, :] >= p[:, None]).astype(np.float32)
    f = np.arange(512, dtype=np.float32)
    ddb = f[None, :] - np.arange(128, dtype=np.float32)[:, None]
    t["dd"] = np.concatenate([ddb - 128.0 * jl for jl in range(4)], axis=1).astype(np.float32)
    t["ident_bf"] = np.eye(128, dtype=np.float32).astype(BF)
    t["ident_f"] = np.eye(128, dtype=np.float32)
    return t


_TABLES = host_tables()


def host_inputs(inp, core):
    b, r = core // 4, core % 4
    x = np.asarray(inp["L0_attn_hidden_states"])
    Wq = np.asarray(inp["L0_attn_Wq"]); Wk = np.asarray(inp["L0_attn_Wk"])
    Wv = np.asarray(inp["L0_attn_Wv"]); Wo = np.asarray(inp["L0_attn_Wo"])
    bo = np.asarray(inp["L0_attn_bo"])
    Wg = np.asarray(inp["L0_ffn_W_g"]); Wu = np.asarray(inp["L0_ffn_W_u"]); Wd = np.asarray(inp["L0_ffn_W_d"])
    Wgu1 = np.asarray(inp["L1_ffn_w_gate_up"]); Wp1 = np.asarray(inp["L1_ffn_w_proj"])

    d = dict(_TABLES)
    d["x_full"] = np.ascontiguousarray(x[b])
    d["x_own"] = np.ascontiguousarray(x[b, 1024 * r:1024 * (r + 1)])
    qcols = []
    for t in range(4):
        for hh in (t, t + 4):
            g = 8 * r + hh
            qcols.extend(range(64 * g, 64 * g + 64))
    d["wqs"] = np.ascontiguousarray(Wq[:, qcols])
    d["wks"] = np.ascontiguousarray(Wk[:, 128 * r:128 * (r + 1)])
    d["wvs"] = np.ascontiguousarray(Wv[:, 128 * r:128 * (r + 1)])
    d["wos"] = np.ascontiguousarray(Wo[512 * r:512 * (r + 1), :])
    d["bo"] = np.ascontiguousarray(np.broadcast_to(bo[None, :], (1, D)))
    hs = np.arange(1, 33, dtype=np.float32) * np.float32(8.0) / np.float32(32.0)
    slopes = (np.float32(2.0) ** (-hs)).astype(np.float32)
    sl = slopes[8 * r:8 * r + 8].astype(np.float64)
    d["slopes_neg"] = np.ascontiguousarray(
        np.tile(-sl[None, :].astype(np.float32), (128, 1)))
    tab = np.zeros((128, 64), np.float64)
    for h in range(8):
        for dq in range(8):
            tab[:, 8 * h + dq] = -sl[h] * 512.0 * dq / 240.0
    d["bias_tab"] = np.ascontiguousarray(tab.astype(np.float32))
    d["wg_sh"] = np.ascontiguousarray(
        np.concatenate([Wg.T[256 * core:256 * (core + 1), :],
                        Wu.T[256 * core:256 * (core + 1), :]], axis=1))
    d["wd_sh"] = np.ascontiguousarray(Wd.T[1024 * core:1024 * (core + 1), :])
    d["wgu_sh"] = np.ascontiguousarray(Wgu1[2048 * core:2048 * (core + 1), :])
    d["wp_sh"] = np.ascontiguousarray(Wp1[:, 1024 * core:1024 * (core + 1)])
    return d


class Ctr:
    def __init__(self, h):
        self.h = h
        self.n = 0

    def inc(self, k=1):
        self.n += k
        return self.n


def wait(e, c, n=None):
    n = c.n if n is None else n
    assert n <= c.n, f"wait on future inc: {n} > {c.n}"
    if n > 0:
        e.wait_ge(c.h, n)


class DmaStream:

    def __init__(self, nc, ex, name, nbuf):
        self.sems = [Ctr(ex.enter_context(nc.semaphore(f"{name}{i}"))) for i in range(nbuf)]
        self.nbuf = nbuf
        self.seq = 0

    def issue(self, e, out, in_, **kw):
        c = self.sems[self.seq % self.nbuf]
        e.dma_start(out=out, in_=in_, **kw).then_inc(c.h, 16)
        c.inc(16)
        self.seq += 1
        return self.seq - 1

    def wait_seq(self, e, seq):
        assert 0 <= seq < self.seq
        c = self.sems[seq % self.nbuf]
        e.wait_ge(c.h, 16 * (seq // self.nbuf + 1))


class DmaBatch:

    def __init__(self, nc, ex, name):
        self.c = Ctr(ex.enter_context(nc.semaphore(name)))

    def issue(self, e, out, in_, **kw):
        e.dma_start(out=out, in_=in_, **kw).then_inc(self.c.h, 16)
        self.c.inc(16)

    def wait_all(self, e):
        if self.c.n:
            e.wait_ge(self.c.h, self.c.n)


def build_nc(stage=99, taps=()):
    nc = bass.Bass("TRN2", target_bir_lowering=False, debug=False)
    ex = contextlib.ExitStack()
    taps = set(taps)

    def din(name, shape, dt=F32):
        return nc.dram_tensor(name, shape, dt, kind="ExternalInput").ap()

    def dout(name, shape, dt=F32):
        return nc.dram_tensor(name, shape, dt, kind="ExternalOutput").ap()

    def dint(name, shape, dt=F32):
        return nc.dram_tensor(name, shape, dt, kind="Internal").ap()

    def mk_sb(stk):
        def f(name, shape, dt, side="left"):
            return stk.enter_context(nc.sbuf_tensor(name, shape, dt, side=side))
        return f

    def mk_ps(stk):
        def f(name, shape, dt=F32):
            return stk.enter_context(nc.psum_tensor(name, shape, dt))
        return f

    sb = mk_sb(ex)

    def sem(name):
        return Ctr(ex.enter_context(nc.semaphore(name)))

    x_full = din("x_full", [S, D])
    x_own = din("x_own", [TOK, D])
    wqs_i = din("wqs", [D, 512]); wks_i = din("wks", [D, 128]); wvs_i = din("wvs", [D, 128])
    wos_i = din("wos", [512, D]); bo_i = din("bo", [1, D])
    wg_sh = din("wg_sh", [256, 16384]); wd_sh = din("wd_sh", [1024, 2048])
    wgu_sh = din("wgu_sh", [2048, 2048]); wp_sh = din("wp_sh", [2048, 1024])
    cos_i = din("cos_t", [128, S]); sin_i = din("sin_t", [128, S])
    tri_i = din("tri", [128, 128]); dd_i = din("dd", [128, 2048])
    idb_i = din("ident_bf", [128, 128], BF16); idf_i = din("ident_f", [128, 128])
    sln_i = din("slopes_neg", [128, 8]); btab_i = din("bias_tab", [128, 64])

    out_own = dout("out_own", [TOK, D])
    tap = {}
    _tapspec = {
        "q": ([128, 4 * S], BF16), "k": ([128, S], BF16), "v": ([128, 32 * 130], BF16),
        "ctx": ([128, 4 * S], BF16), "x1": ([TOK, D], F32),
        "xn_own": ([128, 8 * D], BF16), "nq": ([128, 16 * TOK], BF16),
        "xn1": ([TOK, D], BF16), "gram": ([512, D], F32), "pown": ([128, 4 * D], BF16),
        "ffn0": ([TOK, D], F32),
        "rsout": ([TOK, D], F32), "bob": ([128, D], F32),
        "x1a": ([TOK, D], F32), "x1b": ([TOK, D], F32),
    }
    for t in taps:
        tap[t] = dout("tap_" + t, *_tapspec[t])

    sgu_in = dint("sgu_in", [256, 16384], BF16); sgu_out = dint("sgu_out", [2048, 16384], BF16)
    sd_in = dint("sd_in", [1024, 2048], BF16); sd_out = dint("sd_out", [8192, 2048], BF16)
    qgu_in = dint("qgu_in", [2048, 2048], BF16); qgu_out = dint("qgu_out", [16384, 2048], BF16)
    qp_in = dint("qp_in", [1024, 2048], BF16); qp_out = dint("qp_out", [8192, 2048], BF16)
    rs_in = dint("rs_in", [S, D]); rs_out = dint("rs_out", [TOK, D])
    ffn_sp = dint("ffn_sp", [TOK, D])
    xg_in = dint("xg_in", [TOK, D], BF16); xg_out = dint("xg_out", [S, D], BF16)
    a2a_in = dint("a2a_in", [S, 512], BF16); a2a_out = dint("a2a_out", [S, 512], BF16)
    pt_in = dint("pt_in", [8192, 128], BF16); pt_out = dint("pt_out", [32768, 128], BF16)

    sv = sem("sv")
    ss = sem("ss")
    stn = sem("stn")
    sgp = sem("sgp")

    ld_const = DmaBatch(nc, ex, "ldc")
    tbat = {t: DmaBatch(nc, ex, "tb_" + t) for t in taps}
    ld_wc = DmaBatch(nc, ex, "ldw")
    st_out = DmaBatch(nc, ex, "sto")
    cc = {nm: sem("cc_" + nm) for nm in
          ("sgu", "sd", "qgu", "qp", "rs", "xg", "a2a", "pt")}

    KT = sb("KT", [128, S], BF16)
    VE = sb("VE", [128, 32 * 130], BF16)
    idb = sb("idb", [128, 128], BF16)
    idf = sb("idf", [128, 128], F32)
    ones = sb("ones", [128, 128], F32)
    smalls = sb("smalls", [128, 64], F32)
    cbias = sb("cbias", [128, 8], F32)
    stQT = contextlib.ExitStack()
    QT = mk_sb(stQT)("QT", [128, 4 * S], BF16, side="right")

    stW = contextlib.ExitStack()
    sbW = mk_sb(stW)
    wq_sb = sbW("wq_sb", [128, 16 * 512], BF16, side="right")
    wk_sb = sbW("wk_sb", [128, 16 * 128], BF16, side="right")
    wv_sb = sbW("wv_sb", [128, 16 * 128], BF16, side="right")

    blk = ex.enter_context(nc.Block())

    def on(engine):
        def deco(fn):
            getattr(blk, engine)(fn)
        return deco

    @on("sync")
    def _(e):
        ld_const.issue(e, idb[:], idb_i[:, :])
        ld_const.issue(e, idf[:], idf_i[:, :])

    @on("gpsimd")
    def _(e):
        for dst, src in [
            (wq_sb[:].rearrange("p (a q) -> p a q", a=16), wqs_i[:, :].rearrange("(a p) q -> p a q", p=128)),
            (wk_sb[:].rearrange("p (a q) -> p a q", a=16), wks_i[:, :].rearrange("(a p) q -> p a q", p=128)),
            (wv_sb[:].rearrange("p (a q) -> p a q", a=16), wvs_i[:, :].rearrange("(a p) q -> p a q", p=128)),
        ]:
            ld_wc.issue(e, dst, src)

    @on("vector")
    def _(e):
        e.memset(ones[:], 1.0)
        e.memset(VE[:].rearrange("p (a b) -> p a b", b=65)[:, :, 64:65], 1.0)
        for c, val in [(0, 1e-5), (1, 1e-8), (2, -30.0), (3, 1.1920929e-07)]:
            e.memset(cbias[:, c:c + 1], val)
        e.engine_nop().then_inc(sv.h, 1); sv.inc()
    consts_v = sv.n

    if stage >= 3:
        st0 = contextlib.ExitStack()
        sb0 = mk_sb(st0)
        sgn = [sb0(f"sgn{i}", [128, 8192], BF16, side="right") for i in range(2)]
        ld_sgn = DmaStream(nc, ex, "lsgn", 2)
        st_sgn = DmaStream(nc, ex, "ssgn", 2)
        jobs = [("gu", tt, hh) for tt in range(2) for hh in range(2)] + \
               [("d", tt, 0) for tt in range(8)]
        sgn_done = []
        for j, (kind, tt, hh) in enumerate(jobs):
            @on("gpsimd")
            def _(e, j=j, kind=kind, tt=tt, hh=hh):
                if j >= 2:
                    wait(e, ss, sgn_done[j - 2])
                    st_sgn.wait_seq(e, j - 2)
                if kind == "gu":
                    ld_sgn.issue(e, sgn[j % 2][:], wg_sh[128 * tt:128 * (tt + 1), 8192 * hh:8192 * (hh + 1)])
                else:
                    ld_sgn.issue(e, sgn[j % 2][:, :2048], wd_sh[128 * tt:128 * (tt + 1), :])

            @on("scalar")
            def _(e, j=j, kind=kind):
                ld_sgn.wait_seq(e, j)
                if j >= 2:
                    st_sgn.wait_seq(e, j - 2)
                if kind == "gu":
                    e.activation(sgn[j % 2][:], sgn[j % 2][:], AF.Sign).then_inc(ss.h, 1)
                else:
                    e.activation(sgn[j % 2][:, :2048], sgn[j % 2][:, :2048], AF.Sign).then_inc(ss.h, 1)
                ss.inc()
            sgn_done.append(ss.n)

            @on("gpsimd")
            def _(e, j=j, kind=kind, tt=tt, hh=hh):
                wait(e, ss, sgn_done[j])
                if kind == "gu":
                    st_sgn.issue(e, sgu_in[128 * tt:128 * (tt + 1), 8192 * hh:8192 * (hh + 1)], sgn[j % 2][:])
                else:
                    st_sgn.issue(e, sd_in[128 * tt:128 * (tt + 1), :], sgn[j % 2][:, :2048])

        @on("gpsimd")
        def _(e):
            for j in range(len(jobs)):
                st_sgn.wait_seq(e, j)
            e.collective_compute("AllGather", ALU.bypass, replica_groups=RG8,
                                 ins=[sgu_in.tensor.ap().opt()],
                                 outs=[sgu_out.tensor.ap().opt()]).then_inc(cc["sgu"].h, 1)
            cc["sgu"].inc()
            e.collective_compute("AllGather", ALU.bypass, replica_groups=RG8,
                                 ins=[sd_in.tensor.ap().opt()],
                                 outs=[sd_out.tensor.ap().opt()]).then_inc(cc["sd"].h, 1)
            cc["sd"].inc()
        st0.close()

    st1 = contextlib.ExitStack()
    sb1 = mk_sb(st1)
    ps1 = mk_ps(st1)
    xt = [sb1(f"xt{i}", [128, D], F32, side="right") for i in range(2)]
    xnt = [sb1(f"xnt{i}", [128, D], BF16, side="right") for i in range(2)]
    xnf = sb1("xnf", [128, 16 * 512], BF16, side="right")
    cs_sl = sb1("cs_sl", [128, 512], F32, side="right")
    sn_sl = sb1("sn_sl", [128, 512], F32, side="right")
    qshu = sb1("qshu", [128, 512], F32, side="right")
    qtmp = sb1("qtmp", [128, 512], F32, side="right")
    pst = [ps1(f"pst{i}", [128, 2048], BF16) for i in range(2)]
    psq = ps1("psq", [128, 512], F32)
    psk = ps1("psk", [128, 512], F32)
    psv = ps1("psv", [128, 128], F32)

    ld_x = DmaStream(nc, ex, "lx", 2)
    ld_cs = DmaStream(nc, ex, "lcs", 2)

    if stage >= 3:
        @on("sync")
        def _(e):
            st_sgn.wait_seq(e, st_sgn.seq - 1)
            st_sgn.wait_seq(e, st_sgn.seq - 2)

    xn_hist = []
    var_hist = []
    evac_done = [0, 0]
    rope_done = 0
    v_last_evac = 0

    for ch in range(8):
        t0 = 512 * ch

        @on("sync")
        def _(e, t0=t0, rope_prev=rope_done):
            if rope_prev:
                wait(e, sv, rope_prev)
            ld_cs.issue(e, cs_sl[:], cos_i[:, t0:t0 + 512])
            ld_cs.issue(e, sn_sl[:], sin_i[:, t0:t0 + 512])
        cs_seq = ld_cs.seq - 1

        for tt in range(4):
            gi = 4 * ch + tt
            cb = 8 * (gi % 2)

            @on("sync")
            def _(e, tt=tt, t0=t0, gi=gi):
                if gi >= 2:
                    wait(e, sv, xn_hist[gi - 2])
                ld_x.issue(e, xt[tt % 2][:], x_full[t0 + 128 * tt:t0 + 128 * (tt + 1), :])

            @on("scalar")
            def _(e, tt=tt, gi=gi, cb=cb, evp=evac_done[tt % 2]):
                ld_x.wait_seq(e, gi)
                if gi >= 2:
                    wait(e, sv, var_hist[gi - 2])
                    wait(e, sv, evp)
                e.activation(xnt[tt % 2][:], xt[tt % 2][:], AF.Square,
                             accum_out=smalls[:, cb + 1:cb + 2]).then_inc(ss.h, 1)
                ss.inc()
            sq_n = ss.n

            @on("vector")
            def _(e, tt=tt, gi=gi, sq_n=sq_n, cb=cb):
                ld_x.wait_seq(e, gi)
                e.reduce_sum(smalls[:, cb:cb + 1], xt[tt % 2][:], AX)
                wait(e, ss, sq_n)
                e.drain()
                e.tensor_scalar(smalls[:, cb:cb + 1], smalls[:, cb:cb + 1], 1.0 / D, None, ALU.mult)
                e.drain()
                e.tensor_tensor(smalls[:, cb + 2:cb + 3], smalls[:, cb:cb + 1],
                                smalls[:, cb:cb + 1], ALU.mult)
                e.drain()
                e.scalar_tensor_tensor(smalls[:, cb + 3:cb + 4], smalls[:, cb + 1:cb + 2], 1.0 / D,
                                       smalls[:, cb + 2:cb + 3], ALU.mult, ALU.subtract).then_inc(sv.h, 1)
                sv.inc()
            var_hist.append(sv.n)

            @on("scalar")
            def _(e, var_n=sv.n, cb=cb, first=(gi == 0)):
                wait(e, sv, var_n)
                if first:
                    wait(e, sv, consts_v)
                e.activation(smalls[:, cb + 4:cb + 5], smalls[:, cb + 3:cb + 4],
                             AF.Sqrt, bias=cbias[:, 0:1]).then_inc(ss.h, 1)
                ss.inc()
            sqr_n = ss.n

            @on("vector")
            def _(e, tt=tt, sqr_n=sqr_n, cb=cb):
                wait(e, ss, sqr_n)
                e.reciprocal(smalls[:, cb + 5:cb + 6], smalls[:, cb + 4:cb + 5])
                e.drain()
                e.tensor_scalar(xnt[tt % 2][:], xt[tt % 2][:], smalls[:, cb:cb + 1],
                                smalls[:, cb + 5:cb + 6], ALU.subtract, ALU.mult).then_inc(sv.h, 1)
                sv.inc()
            xn_hist.append(sv.n)

            @on("tensor")
            def _(e, tt=tt, xn_n=sv.n, prev_ev=evac_done[tt % 2], first=(gi == 0)):
                wait(e, sv, xn_n)
                if first:
                    ld_const.wait_all(e)
                if prev_ev:
                    wait(e, sv, prev_ev)
                for dc in range(16):
                    ins = e.transpose(pst[tt % 2][:, 128 * dc:128 * (dc + 1)],
                                      xnt[tt % 2][:, 128 * dc:128 * (dc + 1)], idb[:])
                ins.then_inc(stn.h, 1)
                stn.inc()
            tr_n = stn.n

            @on("vector")
            def _(e, tt=tt, tr_n=tr_n):
                wait(e, stn, tr_n)
                e.tensor_copy(
                    xnf[:].rearrange("p (a b) -> p a b", b=512)[:, :, 128 * tt:128 * (tt + 1)],
                    pst[tt % 2][:].rearrange("p (a b) -> p a b", b=128)).then_inc(sv.h, 1)
                sv.inc()
            evac_done[tt % 2] = sv.n

        xnf_n = sv.n

        for pr in range(4):
            @on("tensor")
            def _(e, pr=pr, ch=ch, xnf_n=xnf_n, rope_prev=rope_done):
                wait(e, sv, xnf_n)
                if rope_prev:
                    wait(e, sv, rope_prev)
                if pr == 0 and ch == 0:
                    ld_wc.wait_all(e)
                for dc in range(16):
                    ins = e.matmul(psq[:, :], wq_sb[:, dc * 512 + pr * 128:dc * 512 + (pr + 1) * 128],
                                   xnf[:, dc * 512:(dc + 1) * 512], start=(dc == 0), stop=(dc == 15))
                ins.then_inc(stn.h, 1)
                stn.inc()
            qmm_n = stn.n

            @on("vector")
            def _(e, pr=pr, ch=ch, qmm_n=qmm_n, cs_seq=cs_seq):
                wait(e, stn, qmm_n)
                ld_cs.wait_seq(e, cs_seq - 1)
                ld_cs.wait_seq(e, cs_seq)
                dst = QT[:, pr * S + 512 * ch: pr * S + 512 * (ch + 1)]
                e.stream_shuffle(qshu[:], psq[:, :], SWAP)
                e.tensor_tensor(qtmp[:], psq[:, :], cs_sl[:], ALU.mult)
                e.drain()
                e.tensor_tensor(qshu[:], qshu[:], sn_sl[:], ALU.mult)
                e.drain()
                e.tensor_add(dst, qtmp[:], qshu[:]).then_inc(sv.h, 1)
                sv.inc()
            rope_done = sv.n

        @on("tensor")
        def _(e, xnf_n=xnf_n, rope_prev=rope_done):
            wait(e, sv, rope_prev)
            for dc in range(16):
                ins = e.matmul(psk[:, :], wk_sb[:, dc * 128:(dc + 1) * 128],
                               xnf[:, dc * 512:(dc + 1) * 512], start=(dc == 0), stop=(dc == 15))
            ins.then_inc(stn.h, 1)
            stn.inc()
        kmm_n = stn.n

        @on("vector")
        def _(e, ch=ch, kmm_n=kmm_n):
            wait(e, stn, kmm_n)
            dst = KT[:, 512 * ch:512 * (ch + 1)]
            e.stream_shuffle(qshu[:], psk[:, :], SWAP)
            e.tensor_tensor(qtmp[:], psk[:, :], cs_sl[:], ALU.mult)
            e.drain()
            e.tensor_tensor(qshu[:], qshu[:], sn_sl[:], ALU.mult)
            e.drain()
            e.tensor_add(dst, qtmp[:], qshu[:]).then_inc(sv.h, 1)
            sv.inc()
        rope_done = sv.n

        for tt in range(4):
            @on("tensor")
            def _(e, tt=tt, xnf_n=xnf_n, vprev=v_last_evac):
                if vprev:
                    wait(e, sv, vprev)
                wait(e, sv, xnf_n)
                for dc in range(16):
                    ins = e.matmul(psv[:, :], xnf[:, dc * 512 + 128 * tt:dc * 512 + 128 * (tt + 1)],
                                   wv_sb[:, dc * 128:(dc + 1) * 128], start=(dc == 0), stop=(dc == 15))
                ins.then_inc(stn.h, 1)
                stn.inc()
            vmm_n = stn.n

            @on("vector")
            def _(e, ch=ch, tt=tt, vmm_n=vmm_n):
                wait(e, stn, vmm_n)
                jt = 4 * ch + tt
                e.tensor_copy(VE[:, jt * 130:jt * 130 + 64], psv[:, 0:64])
                e.tensor_copy(VE[:, jt * 130 + 65:jt * 130 + 129], psv[:, 64:128]).then_inc(sv.h, 1)
                sv.inc()
            v_last_evac = sv.n

    st1.close()
    stW.close()

    def tap_store(name, src_ap):
        if name in tap:
            @on("sync")
            def _(e):
                wait(e, sv, sv.n)
                wait(e, ss, ss.n)
                wait(e, stn, stn.n)
                tbat[name].issue(e, tap[name][:, :], src_ap)

    tap_store("q", QT[:])
    tap_store("k", KT[:])
    tap_store("v", VE[:])

    if stage <= 1:
        z = sb("zz", [128, D], F32)

        @on("vector")
        def _(e):
            e.memset(z[:], 0.0).then_inc(sv.h, 1); sv.inc()
        zn = sv.n

        @on("sync")
        def _(e):
            wait(e, sv, zn)
            for tt in range(8):
                st_out.issue(e, out_own[128 * tt:128 * (tt + 1), :], z[:])
            st_out.wait_all(e)
        stQT.close()
        ex.close()
        return nc

    stC = contextlib.ExitStack()
    sbC = mk_sb(stC)
    psC = mk_ps(stC)
    CTX = sbC("CTX", [128, 4 * S], BF16, side="right")
    T4 = sbC("T4", [128, 2048], F32)
    E4 = sbC("E4", [128, 2048], BF16, side="right")
    dnrb = sbC("dnrb", [128, 512], F32, side="right")
    tmpdiv = [sbC("tdv0", [128, 512], BF16, side="right")]
    tri = sbC("tri_s", [128, 128], F32, side="right")
    dd = sbC("dd_s", [128, 2048], F32, side="right")
    sln = sbC("sln_s", [128, 8], F32, side="right")
    btab = sbC("btab_s", [128, 64], F32, side="right")
    wos_sb = sbC("wos_sb", [128, 4 * D], BF16, side="right")
    wo_out = [sbC(f"woo{i}", [128, 512], F32, side="right") for i in range(2)]
    ps_s = psC("ps_s", [128, 512], F32)
    ps_c = psC("ps_c", [128, 512], F32)
    ps_w = psC("ps_w", [128, 1024], F32)

    ld_attc = DmaBatch(nc, ex, "ldac")
    ctx_dma = DmaStream(nc, ex, "cdx", 2)
    rs_st = DmaStream(nc, ex, "rst", 2)

    qkv_done_v = sv.n

    ffn_gen = None
    otm_st = None
    l1q_state = {}
    if stage >= 3:
        stF = contextlib.ExitStack()
        sbF = mk_sb(stF)
        n_fm = sbF("n_fm", [128, 16 * TOK], BF16)
        otm_st = DmaStream(nc, ex, "sfo", 1)

        def ffn_emit():
            stQ = contextlib.ExitStack()
            sbQ = mk_sb(stQ)
            psQ = mk_ps(stQ)
            xo_t = [sbQ(f"xo{i}", [128, D], F32, side="right") for i in range(2)]
            yq = sbQ("yq", [128, D], F32, side="right")
            q_s = sbQ("q_s", [128, D], F32, side="right")
            nq_t = [sbQ(f"nq{i}", [128, D], BF16, side="right") for i in range(2)]
            psqt = psQ("psqt", [128, 2048], BF16)
            ld_xq = DmaStream(nc, ex, "lxq", 2)
            nq_hist = []
            qvar_hist = []
            qev_hist = [0, 0]
            for tt in range(8):
                cb = 8 * (tt % 2)

                @on("sync")
                def _(e, tt=tt, q0=qkv_done_v):
                    if tt == 0:
                        wait(e, sv, q0)
                        if "st_l1" in l1q_state:
                            s = l1q_state["st_l1"]
                            s.wait_seq(e, s.seq - 1)
                            s.wait_seq(e, s.seq - 2)
                    if tt >= 2:
                        wait(e, sv, nq_hist[tt - 2])
                    ld_xq.issue(e, xo_t[tt % 2][:], x_own[128 * tt:128 * (tt + 1), :])

                @on("scalar")
                def _(e, tt=tt, cb=cb, evp=qev_hist[tt % 2]):
                    ld_xq.wait_seq(e, tt)
                    if tt >= 2:
                        wait(e, sv, qvar_hist[tt - 2])
                        wait(e, sv, evp)
                    e.activation(nq_t[tt % 2][:], xo_t[tt % 2][:], AF.Square,
                                 accum_out=smalls[:, cb + 1:cb + 2]).then_inc(ss.h, 1)
                    ss.inc()
                sq_n = ss.n

                @on("vector")
                def _(e, tt=tt, sq_n=sq_n, cb=cb):
                    ld_xq.wait_seq(e, tt)
                    e.reduce_sum(smalls[:, cb:cb + 1], xo_t[tt % 2][:], AX)
                    wait(e, ss, sq_n)
                    e.drain()
                    e.tensor_scalar(smalls[:, cb:cb + 1], smalls[:, cb:cb + 1], 1.0 / D, None, ALU.mult)
                    e.drain()
                    e.tensor_tensor(smalls[:, cb + 2:cb + 3], smalls[:, cb:cb + 1],
                                    smalls[:, cb:cb + 1], ALU.mult)
                    e.drain()
                    e.scalar_tensor_tensor(smalls[:, cb + 3:cb + 4], smalls[:, cb + 1:cb + 2],
                                           1.0 / D, smalls[:, cb + 2:cb + 3],
                                           ALU.mult, ALU.subtract).then_inc(sv.h, 1)
                    sv.inc()
                qvar_hist.append(sv.n)

                @on("scalar")
                def _(e, n=sv.n, cb=cb):
                    wait(e, sv, n)
                    e.activation(smalls[:, cb + 4:cb + 5], smalls[:, cb + 3:cb + 4],
                                 AF.Sqrt, bias=cbias[:, 1:2]).then_inc(ss.h, 1)
                    ss.inc()
                sqr_n = ss.n

                @on("vector")
                def _(e, tt=tt, sqr_n=sqr_n, cb=cb):
                    wait(e, ss, sqr_n)
                    e.reciprocal(smalls[:, cb + 5:cb + 6], smalls[:, cb + 4:cb + 5])
                    e.drain()
                    e.tensor_scalar(yq[:], xo_t[tt % 2][:], smalls[:, cb:cb + 1],
                                    smalls[:, cb + 5:cb + 6], ALU.subtract, ALU.mult)
                    e.drain()
                    e.reduce_max(smalls[:, cb + 6:cb + 7], yq[:], AX, apply_absolute_value=True)
                    e.drain()
                    e.reciprocal(smalls[:, cb + 7:cb + 8], smalls[:, cb + 6:cb + 7])
                    e.drain()
                    e.tensor_scalar(smalls[:, cb + 7:cb + 8], smalls[:, cb + 7:cb + 8],
                                    127.0, None, ALU.mult)
                    e.drain()
                    e.reciprocal(smalls[:, cb + 6:cb + 7], smalls[:, cb + 7:cb + 8])
                    e.drain()
                    e.tensor_scalar(q_s[:], yq[:], smalls[:, cb + 7:cb + 8], RND,
                                    ALU.mult, ALU.add)
                    e.drain()
                    e.tensor_scalar(q_s[:], q_s[:], RND, -128.0, ALU.subtract, ALU.max)
                    e.drain()
                    e.tensor_scalar(nq_t[tt % 2][:], q_s[:], 127.0, smalls[:, cb + 6:cb + 7],
                                    ALU.min, ALU.mult).then_inc(sv.h, 1)
                    sv.inc()
                nq_hist.append(sv.n)

                @on("tensor")
                def _(e, tt=tt, n=sv.n, prev=qev_hist[tt % 2]):
                    wait(e, sv, n)
                    if prev:
                        wait(e, sv, prev)
                    for dc in range(16):
                        ins = e.transpose(psqt[:, 128 * dc:128 * (dc + 1)],
                                          nq_t[tt % 2][:, 128 * dc:128 * (dc + 1)], idb[:])
                    ins.then_inc(stn.h, 1)
                    stn.inc()
                tr_n = stn.n

                @on("vector")
                def _(e, tt=tt, tr_n=tr_n):
                    wait(e, stn, tr_n)
                    e.tensor_copy(
                        n_fm[:].rearrange("p (a b) -> p a b", b=TOK)[:, :, 128 * tt:128 * (tt + 1)],
                        psqt[:].rearrange("p (a b) -> p a b", b=128)).then_inc(sv.h, 1)
                    sv.inc()
                qev_hist[tt % 2] = sv.n
                if tt % 2 == 1:
                    yield
            nq_all = sv.n
            if "nq" in tap:
                @on("sync")
                def _(e, n=nq_all):
                    wait(e, sv, n)
                    tbat["nq"].issue(e, tap["nq"][:, :], n_fm[:])
            stQ.close()

            sbF2 = mk_sb(stF)
            psF2 = mk_ps(stF)
            ffn_acc = sbF2("ffn_acc", [128, 16 * 512], F32)
            p_grp = sbF2("p_grp", [128, 8 * 512], BF16)
            gsil = sbF2("gsil", [128, 1024], BF16)
            wstr = [sbF2(f"wstr{i}", [128, 2 * 256], BF16) for i in range(2)]
            wdt = [sbF2(f"wdt{i}", [128, 2 * 256], BF16) for i in range(2)]
            otm = [sbF2("otm0", [128, 1024], F32)]
            ps_gu = psF2("ps_gu", [128, 1024], F32)
            ps_dn = psF2("ps_dn", [128, 1024], F32)
            ld_w = DmaStream(nc, ex, "lw", 2)
            ld_wd = DmaStream(nc, ex, "lwd", 2)
            ld_w_cons = []
            ld_wd_cons = []
            silu_hist = 0
            p_hist = 0
            dn_mm_hist = 0
            acc_ev_hist = 0
            spill_v = 0

            for th in range(2):
                for grp in range(8):
                    for s2 in range(4):
                        hc0 = 128 * (8 * grp + 2 * s2)
                        for gu in range(2):
                            for dh in range(8):
                                @on("sync")
                                def _(e, dh=dh, gu=gu, hc0=hc0,
                                      first=(th == 0 and grp == 0 and s2 == 0 and gu == 0 and dh == 0)):
                                    if first:
                                        wait(e, sv, nq_all)
                                        wait(e, cc["sgu"], cc["sgu"].n)
                                    if ld_w.seq >= 2:
                                        wait(e, stn, ld_w_cons[ld_w.seq - 2])
                                    ld_w.issue(e, wstr[ld_w.seq % 2][:].rearrange("p (a q) -> p a q", q=256),
                                               sgu_out[256 * dh:256 * (dh + 1),
                                                       8192 * gu + hc0:8192 * gu + hc0 + 256]
                                               .rearrange("(a p) q -> p a q", p=128))

                                @on("tensor")
                                def _(e, th=th, gu=gu, dh=dh, seq=ld_w.seq - 1,
                                      sil=silu_hist, pn=p_hist):
                                    ld_w.wait_seq(e, seq)
                                    if dh == 0:
                                        if gu == 0 and pn:
                                            wait(e, sv, pn)
                                        if gu == 1 and sil:
                                            wait(e, ss, sil)
                                    for a in range(2):
                                        dc = 2 * dh + a
                                        for i in range(2):
                                            ins = e.matmul(ps_gu[:, 512 * i:512 * (i + 1)],
                                                           wstr[seq % 2][:, 256 * a + 128 * i:256 * a + 128 * (i + 1)],
                                                           n_fm[:, dc * TOK + 512 * th:dc * TOK + 512 * (th + 1)],
                                                           start=(dc == 0), stop=(dc == 15))
                                    ins.then_inc(stn.h, 1)
                                    stn.inc()
                                ld_w_cons.append(stn.n)
                            gu_mm_n = stn.n
                            if gu == 0:
                                @on("scalar")
                                def _(e, n=gu_mm_n, pn=p_hist):
                                    wait(e, stn, n)
                                    if pn:
                                        wait(e, sv, pn)
                                    e.activation(gsil[:], ps_gu[:, :], AF.Silu).then_inc(ss.h, 1)
                                    ss.inc()
                                silu_hist = ss.n
                            else:
                                @on("vector")
                                def _(e, s2=s2, n=gu_mm_n, sil=silu_hist, dmm=dn_mm_hist):
                                    wait(e, stn, n)
                                    wait(e, ss, sil)
                                    if dmm:
                                        wait(e, stn, dmm)
                                    e.tensor_mul(p_grp[:, 1024 * s2:1024 * (s2 + 1)],
                                                 gsil[:], ps_gu[:, :]).then_inc(sv.h, 1)
                                    sv.inc()
                                p_hist = sv.n
                    p_all_n = p_hist

                    for dmp in range(8):
                        for ih in range(4):
                            @on("sync")
                            def _(e, grp=grp, dmp=dmp, ih=ih, first=(ld_wd.seq == 0)):
                                if first:
                                    wait(e, cc["sd"], cc["sd"].n)
                                if ld_wd.seq >= 2:
                                    wait(e, stn, ld_wd_cons[ld_wd.seq - 2])
                                ld_wd.issue(e, wdt[ld_wd.seq % 2][:].rearrange("p (a q) -> p a q", q=256),
                                            sd_out[1024 * grp + 256 * ih:1024 * grp + 256 * (ih + 1),
                                                   256 * dmp:256 * (dmp + 1)]
                                            .rearrange("(a p) q -> p a q", p=128))

                            @on("tensor")
                            def _(e, dmp=dmp, ih=ih, seq=ld_wd.seq - 1,
                                  pn=p_all_n, accp=acc_ev_hist, spv=spill_v):
                                ld_wd.wait_seq(e, seq)
                                if ih == 0:
                                    wait(e, sv, pn)
                                    if accp:
                                        wait(e, sv, accp)
                                    if dmp == 0 and spv:
                                        wait(e, sv, spv)
                                for a in range(2):
                                    i = 2 * ih + a
                                    for kk in range(2):
                                        ins = e.matmul(ps_dn[:, 512 * kk:512 * (kk + 1)],
                                                       wdt[seq % 2][:, 256 * a + 128 * kk:256 * a + 128 * (kk + 1)],
                                                       p_grp[:, 512 * i:512 * (i + 1)],
                                                       start=(i == 0), stop=(i == 7))
                                ins.then_inc(stn.h, 1)
                                stn.inc()
                            ld_wd_cons.append(stn.n)
                        dn_mm_hist = stn.n

                        @on("vector")
                        def _(e, grp=grp, dmp=dmp, n=dn_mm_hist, spv=spill_v):
                            wait(e, stn, n)
                            if grp == 0 and spv:
                                wait(e, sv, spv)
                            for kk in range(2):
                                dmc = 2 * dmp + kk
                                if grp == 0:
                                    ins = e.tensor_copy(ffn_acc[:, 512 * dmc:512 * (dmc + 1)],
                                                        ps_dn[:, 512 * kk:512 * (kk + 1)])
                                else:
                                    ins = e.tensor_add(ffn_acc[:, 512 * dmc:512 * (dmc + 1)],
                                                       ffn_acc[:, 512 * dmc:512 * (dmc + 1)],
                                                       ps_dn[:, 512 * kk:512 * (kk + 1)])
                            ins.then_inc(sv.h, 1)
                            sv.inc()
                        acc_ev_hist = sv.n
                    yield
                acc_done = sv.n

                spill_ev = 0
                for tq in range(4):
                    for dmg in range(4):
                        @on("tensor")
                        def _(e, tq=tq, dmg=dmg, n=acc_done, sev=spill_ev):
                            wait(e, sv, n)
                            if sev:
                                wait(e, sv, sev)
                            for k in range(4):
                                dmc = 4 * dmg + k
                                ins = e.transpose(ps_dn[:, 128 * k:128 * (k + 1)],
                                                  ffn_acc[:, 512 * dmc + 128 * tq:512 * dmc + 128 * (tq + 1)],
                                                  idf[:])
                            ins.then_inc(stn.h, 1)
                            stn.inc()
                        tr_n = stn.n

                        @on("vector")
                        def _(e, tq=tq, dmg=dmg, tr_n=tr_n, otp=(otm_st.seq - 1)):
                            wait(e, stn, tr_n)
                            if otp >= 0:
                                otm_st.wait_seq(e, otp)
                            e.tensor_copy(
                                otm[0][:].rearrange("p (a b) -> p a b", b=128)[:, 4 * (dmg % 2):4 * (dmg % 2) + 4, :],
                                ps_dn[:, 0:512].rearrange("p (a b) -> p a b", b=128)).then_inc(sv.h, 1)
                            sv.inc()
                        spill_ev = sv.n

                        if dmg % 2 == 1:
                            @on("sync")
                            def _(e, th=th, tq=tq, dmg=dmg, n=spill_ev):
                                wait(e, sv, n)
                                otm_st.issue(e, ffn_sp[512 * th + 128 * tq:512 * th + 128 * (tq + 1),
                                                       1024 * (dmg // 2):1024 * (dmg // 2 + 1)],
                                             otm[0][:])
                    spill_v = sv.n
                    yield
            while True:
                yield

        ffn_gen = ffn_emit()

    l1q_gen = None
    if stage >= 4:
        def l1q_emit():
            stQ1 = contextlib.ExitStack()
            sbQ1 = mk_sb(stQ1)
            psQ1 = mk_ps(stQ1)
            wq_t = [sbQ1(f"l1w{i}", [128, 2048], BF16, side="right") for i in range(2)]
            qf = sbQ1("l1qf", [128, 2048], F32, side="right")
            wh_t = sbQ1("l1wh", [128, 2048], BF16, side="right")
            qtr = sbQ1("l1tr", [128, 2048], BF16, side="right")
            gsc = sbQ1("l1gs", [128, 32], F32, side="right")
            sclb = sbQ1("l1sc", [128, 32], BF16, side="right")
            rsf = sbQ1("l1rs", [128, 32], F32, side="right")
            psq1 = psQ1("psq1", [128, 2048], BF16)
            ld_l1 = DmaStream(nc, ex, "l1l", 2)
            st_l1 = DmaStream(nc, ex, "l1s", 2)
            jobs = [("gu", t) for t in range(16)] + [("p", t) for t in range(16)]
            qdone = []
            trdone = []
            evdone = []
            for j, (kind, t) in enumerate(jobs):
                W = 2048 if kind == "gu" else 1024
                G = W // 64

                @on("gpsimd")
                def _(e, j=j, kind=kind, t=t, W=W):
                    if j >= 2:
                        wait(e, sv, qdone[j - 2])
                    src_t = wgu_sh if kind == "gu" else wp_sh
                    ld_l1.issue(e, wq_t[j % 2][:, :W], src_t[128 * t:128 * (t + 1), :])

                @on("vector")
                def _(e, j=j, W=W, G=G):
                    ld_l1.wait_seq(e, j)
                    wg = wq_t[j % 2][:, :W].rearrange("p (g k) -> p g k", k=64)
                    e.reduce_sum(gsc[:, :G].unsqueeze(2), wg, AX, apply_absolute_value=True)
                    e.drain()
                    e.tensor_scalar(gsc[:, :G], gsc[:, :G], 1.0 / 64.0, None, ALU.mult)
                    e.drain()
                    e.tensor_copy(sclb[:, :G], gsc[:, :G])
                    e.drain()
                    e.tensor_scalar(sclb[:, :G], sclb[:, :G], 1e-8, None, ALU.max)
                    e.drain()
                    e.reciprocal(rsf[:, :G], sclb[:, :G])
                    e.drain()
                    e.tensor_tensor(qf[:, :W].rearrange("p (g k) -> p g k", k=64), wg,
                                    rsf[:, :G].unsqueeze(2).broadcast_to([128, G, 64]), ALU.mult)
                    e.drain()
                    e.tensor_scalar(qf[:, :W], qf[:, :W], RND, RND, ALU.add, ALU.subtract)
                    e.drain()
                    e.tensor_scalar(qf[:, :W], qf[:, :W], -1.0, 1.0, ALU.max, ALU.min)
                    e.drain()
                    e.tensor_tensor(wh_t[:, :W].rearrange("p (g k) -> p g k", k=64),
                                    qf[:, :W].rearrange("p (g k) -> p g k", k=64),
                                    sclb[:, :G].unsqueeze(2).broadcast_to([128, G, 64]),
                                    ALU.mult).then_inc(sv.h, 1)
                    sv.inc()
                qdone.append(sv.n)

                @on("tensor")
                def _(e, j=j, W=W, n=qdone[j], prev=(evdone[j - 2] if j >= 2 else 0)):
                    wait(e, sv, n)
                    if prev:
                        wait(e, sv, prev)
                    for bb in range(W // 128):
                        ins = e.transpose(psq1[:, 128 * bb:128 * (bb + 1)],
                                          wh_t[:, 128 * bb:128 * (bb + 1)], idb[:])
                    ins.then_inc(stn.h, 1)
                    stn.inc()
                trdone.append(stn.n)

                @on("vector")
                def _(e, j=j, W=W, n=trdone[j], stp=(j - 1)):
                    wait(e, stn, n)
                    if stp >= 0:
                        st_l1.wait_seq(e, stp)
                    e.tensor_copy(qtr[:, :W], psq1[:, :W]).then_inc(sv.h, 1)
                    sv.inc()
                evdone.append(sv.n)

                @on("sync")
                def _(e, j=j, kind=kind, t=t, W=W, n=evdone[j]):
                    wait(e, sv, n)
                    if kind == "gu":
                        dst = qgu_in[:, 128 * t:128 * (t + 1)].rearrange("(a p) q -> p a q", p=128)
                    else:
                        dst = qp_in[:, 128 * t:128 * (t + 1)].rearrange("(a p) q -> p a q", p=128)
                    st_l1.issue(e, dst, qtr[:, :W].rearrange("p (a q) -> p a q", q=128))
                if j % 2 == 1:
                    yield
            @on("gpsimd")
            def _(e, nj=len(jobs)):
                st_l1.wait_seq(e, nj - 1)
                st_l1.wait_seq(e, nj - 2)
                e.collective_compute("AllGather", ALU.bypass, replica_groups=RG8,
                                     ins=[qgu_in.tensor.ap().opt()],
                                     outs=[qgu_out.tensor.ap().opt()]).then_inc(cc["qgu"].h, 1)
                cc["qgu"].inc()
                e.collective_compute("AllGather", ALU.bypass, replica_groups=RG8,
                                     ins=[qp_in.tensor.ap().opt()],
                                     outs=[qp_out.tensor.ap().opt()]).then_inc(cc["qp"].h, 1)
                cc["qp"].inc()
            l1q_state["st_l1"] = st_l1
            stQ1.close()
            while True:
                yield

        l1q_gen = l1q_emit()

    @on("sync")
    def _(e):
        wait(e, sv, qkv_done_v)
        for dst, src in [(tri[:], tri_i[:, :]), (dd[:], dd_i[:, :]),
                         (sln[:], sln_i[:, :]), (btab[:], btab_i[:, :])]:
            ld_attc.issue(e, dst, src)

    @on("gpsimd")
    def _(e):
        wait(e, sv, qkv_done_v)
        ld_wc.issue(e, wos_sb[:].rearrange("p (a q) -> p a q", a=4),
                    wos_i[:, :].rearrange("(a p) q -> p a q", p=128))

    stt_hist = 0
    exp_h = 0
    ctx_h = 0
    div_hist = 0
    den_bcast_hist = 0
    odd_dma = []
    wo_evac_hist = [0, 0]
    att_const_waited = [False]

    gens = [g for g in (ffn_gen,) if g is not None]
    if stage >= 4:
        gens = [ffn_gen, l1q_gen]
        import itertools
        def _seq():
            for _ in range(17):
                next(l1q_gen); yield
            while True:
                next(ffn_gen); yield
        _sq = _seq()
        def drain_ffn(k=1):
            for _ in range(k):
                next(_sq)
    else:
        def drain_ffn(k=1):
            if ffn_gen is not None:
                for _ in range(k):
                    next(ffn_gen)

    for qt in range(8):
        odd_seqs = []
        for h in range(8):
            drain_ffn(1 if (qt * 8 + h) % 3 != 2 else 0)
            kh = h // 4
            nblk = 4 * (qt + 1)
            for jtq in range(qt + 1):
                for jl in range(4):
                    jt = 4 * jtq + jl

                    @on("tensor")
                    def _(e, h=h, qt=qt, jt=jt, kh=kh,
                          sttp=stt_hist, divp=div_hist):
                        if jt == 0 and divp:
                            wait(e, sv, divp)
                        if sttp:
                            wait(e, sv, sttp)
                        if h == 0 and qt == 0 and jt == 0:
                            wait(e, sv, qkv_done_v)
                        e.matmul(ps_s[:, :],
                                 KT[64 * kh:64 * (kh + 1), 128 * jt:128 * (jt + 1)],
                                 QT[64 * kh:64 * (kh + 1), (h % 4) * S + 512 * qt:(h % 4) * S + 512 * (qt + 1)],
                                 start=True, stop=True,
                                 tile_position=(64 * kh, 0)).then_inc(stn.h, 1)
                        stn.inc()
                    mm_n = stn.n

                    @on("vector")
                    def _(e, h=h, jl=jl, mm_n=mm_n, expp=exp_h):
                        wait(e, stn, mm_n)
                        if jl == 0 and expp:
                            wait(e, ss, expp)
                        if not att_const_waited[0]:
                            ld_attc.wait_all(e)
                            att_const_waited[0] = True
                        e.scalar_tensor_tensor(T4[:, 512 * jl:512 * (jl + 1)],
                                               dd[:, 512 * jl:512 * (jl + 1)],
                                               sln[:, h:h + 1], ps_s[:, :],
                                               ALU.mult, ALU.add).then_inc(sv.h, 1)
                        sv.inc()
                    stt_hist = sv.n

                @on("scalar")
                def _(e, h=h, dq=qt - jtq, sttn=stt_hist, ctxp=ctx_h):
                    wait(e, sv, sttn)
                    if ctxp:
                        wait(e, stn, ctxp)
                    col = 8 * h + dq
                    e.activation(T4[:], T4[:], AF.Tanh,
                                 bias=btab[:, col:col + 1], scale=1.0 / 240.0)
                    e.drain()
                    e.activation(E4[:], T4[:], AF.Exp,
                                 bias=cbias[:, 2:3], scale=30.0).then_inc(ss.h, 1)
                    ss.inc()
                exp_h = ss.n

                if jtq == qt:
                    @on("vector")
                    def _(e, expn=exp_h):
                        wait(e, ss, expn)
                        for jl in range(4):
                            if jl > 0:
                                e.memset(E4[:, 512 * jl:512 * jl + 128 * jl], 0.0)
                            ins = e.tensor_mul(E4[:, 512 * jl + 128 * jl:512 * jl + 128 * (jl + 1)],
                                               E4[:, 512 * jl + 128 * jl:512 * jl + 128 * (jl + 1)],
                                               tri[:])
                        ins.then_inc(sv.h, 1)
                        sv.inc()
                    e_ready = ("sv", sv.n)
                else:
                    e_ready = ("ss", exp_h)

                for jl in range(4):
                    jt = 4 * jtq + jl

                    @on("tensor")
                    def _(e, jt=jt, jl=jl, kh=kh, nblk=nblk,
                          e_ready=e_ready, divp=div_hist):
                        if jl == 0:
                            if e_ready[0] == "sv":
                                wait(e, sv, e_ready[1])
                            else:
                                wait(e, ss, e_ready[1])
                        if jt == 0 and divp:
                            wait(e, sv, divp)
                        e.matmul(ps_c[0:65, :], VE[:, 130 * jt + 65 * kh:130 * jt + 65 * kh + 65],
                                 E4[:, 512 * jl:512 * (jl + 1)], start=(jt == 0),
                                 stop=(jt == nblk - 1)).then_inc(stn.h, 1)
                        stn.inc()
                ctx_h = stn.n
            ctx_done = stn.n

            @on("vector")
            def _(e, ctx_done=ctx_done, bcp=den_bcast_hist):
                wait(e, stn, ctx_done)
                if bcp:
                    wait(e, stn, bcp)
                e.tensor_copy(dnrb[64:65, 0:512], ps_c[64:65, :])
                e.drain()
                e.reciprocal(dnrb[64:65, 0:512], dnrb[64:65, 0:512]).then_inc(sv.h, 1)
                sv.inc()
            den_n = sv.n

            @on("tensor")
            def _(e, den_n=den_n):
                wait(e, sv, den_n)
                e.matmul(ps_s[0:64, 0:512], ones[64:65, 0:64], dnrb[64:65, 0:512],
                         start=True, stop=True, tile_position=(64, 0)).then_inc(stn.h, 1)
                stn.inc()
            den_bcast_hist = stn.n

            @on("vector")
            def _(e, h=h, qt=qt, bc_n=den_bcast_hist, cdp=(ctx_dma.seq - 1)):
                wait(e, stn, bc_n)
                e.tensor_copy(dnrb[0:64, 0:512], ps_s[0:64, 0:512])
                e.drain()
                if h % 2 == 0:
                    dst = CTX[0:64, (h // 2) * S + 512 * qt:(h // 2) * S + 512 * (qt + 1)]
                else:
                    if cdp >= 0:
                        ctx_dma.wait_seq(e, cdp)
                    dst = tmpdiv[0][0:64, :]
                e.tensor_mul(dst, ps_c[0:64, :], dnrb[0:64, 0:512]).then_inc(sv.h, 1)
                sv.inc()
            div_hist = sv.n

            if h % 2 == 1:
                @on("sync")
                def _(e, h=h, qt=qt, divn=div_hist):
                    wait(e, sv, divn)
                    seq = ctx_dma.issue(
                        e, CTX[64:128, (h // 2) * S + 512 * qt:(h // 2) * S + 512 * (qt + 1)],
                        tmpdiv[0][0:64, :])
                    odd_seqs.append(seq)

        wo_dep_v = div_hist
        for tt in range(4):
            for dmc in range(4):
                wi = 4 * tt + dmc
                wp = wi % 2

                @on("tensor")
                def _(e, qt=qt, tt=tt, dmc=dmc, wp=wp, wo_dep_v=wo_dep_v,
                      odd_seqs=list(odd_seqs), evp=wo_evac_hist[wp],
                      first=(qt == 0 and tt == 0 and dmc == 0)):
                    wait(e, sv, wo_dep_v)
                    if first:
                        ld_wc.wait_all(e)
                    for sq in odd_seqs:
                        ctx_dma.wait_seq(e, sq)
                    if evp:
                        wait(e, sv, evp)
                    for ac in range(4):
                        ins = e.matmul(ps_w[:, 512 * wp:512 * (wp + 1)],
                                       CTX[:, ac * S + 512 * qt + 128 * tt:ac * S + 512 * qt + 128 * (tt + 1)],
                                       wos_sb[:, ac * D + 512 * dmc:ac * D + 512 * (dmc + 1)],
                                       start=(ac == 0), stop=(ac == 3))
                    ins.then_inc(stn.h, 1)
                    stn.inc()
                wo_n = stn.n

                @on("vector")
                def _(e, wp=wp, wo_n=wo_n, stp=(rs_st.seq - 2)):
                    wait(e, stn, wo_n)
                    if stp >= 0:
                        rs_st.wait_seq(e, stp)
                    e.tensor_copy(wo_out[wp][:], ps_w[:, 512 * wp:512 * (wp + 1)]).then_inc(sv.h, 1)
                    sv.inc()
                wo_evac_hist[wp] = sv.n

                @on("sync")
                def _(e, qt=qt, tt=tt, dmc=dmc, wp=wp, evn=wo_evac_hist[wp]):
                    wait(e, sv, evn)
                    rs_st.issue(e, rs_in[512 * qt + 128 * tt:512 * qt + 128 * (tt + 1),
                                         512 * dmc:512 * (dmc + 1)], wo_out[wp][:])

    drain_ffn(40)

    tap_store("ctx", CTX[:])

    @on("gpsimd")
    def _(e):
        rs_st.wait_seq(e, rs_st.seq - 1)
        rs_st.wait_seq(e, rs_st.seq - 2)
        e.collective_compute("ReduceScatter", ALU.add, replica_groups=RG4,
                             ins=[rs_in.tensor.ap().opt()],
                             outs=[rs_out.tensor.ap().opt()]).then_inc(cc["rs"].h, 1)
        cc["rs"].inc()

    if stage >= 3 and "nq" in tap:
        @on("sync")
        def _(e):
            tbat["nq"].wait_all(e)
    if stage >= 3:
        stF.close()
    stC_right_users_v = sv.n
    stC_users_stn = stn.n
    stC.close()
    stQT.close()
    stD = contextlib.ExitStack()
    sbD = mk_sb(stD)
    psD = mk_ps(stD)
    x1 = sbD("x1", [128, 8 * D], F32)
    bob = sbD("bob", [128, D], F32, side="right")
    stB2 = contextlib.ExitStack()
    ps_b2 = mk_ps(stB2)("ps_b2", [128, 512], F32)
    ld_boc = DmaBatch(nc, ex, "lboc")
    ld_xo = DmaBatch(nc, ex, "lxo")

    @on("sync")
    def _(e, n=stC_right_users_v):
        wait(e, sv, n)
        for t in ("ctx", "q", "k", "v"):
            if t in tbat:
                tbat[t].wait_all(e)
        ld_boc.issue(e, bob[0:1, :], bo_i[:, :])
        for tt in range(8):
            ld_xo.issue(e, x1[:, tt * D:(tt + 1) * D], x_own[128 * tt:128 * (tt + 1), :])

    if "x1a" in tap:
        @on("sync")
        def _(e):
            ld_xo.wait_all(e)
            for tt in range(8):
                tbat["x1a"].issue(e, tap["x1a"][128 * tt:128 * (tt + 1), :], x1[:, tt * D:(tt + 1) * D])

    bo_prev_ev = 0
    for c4 in range(4):
        @on("tensor")
        def _(e, c4=c4, prev=bo_prev_ev, nn=stC_right_users_v):
            if c4 == 0:
                ld_boc.wait_all(e)
                wait(e, sv, nn)
            if prev:
                wait(e, sv, prev)
            e.matmul(ps_b2[:, :], ones[0:1, 0:128], bob[0:1, 512 * c4:512 * (c4 + 1)],
                     start=True, stop=True).then_inc(stn.h, 1)
            stn.inc()
        bo_mm_n = stn.n

        @on("vector")
        def _(e, c4=c4, n=bo_mm_n):
            wait(e, stn, n)
            e.tensor_copy(bob[:, 512 * c4:512 * (c4 + 1)], ps_b2[:, :]).then_inc(sv.h, 1)
            sv.inc()
        bo_prev_ev = sv.n
    bo_ev_n = sv.n

    @on("vector")
    def _(e, n=bo_ev_n):
        wait(e, sv, n)
        ld_xo.wait_all(e)
        if "x1a" in tap:
            tbat["x1a"].wait_all(e)
        for tt in range(8):
            e.tensor_add(x1[:, tt * D:(tt + 1) * D], x1[:, tt * D:(tt + 1) * D], bob[:])
        e.engine_nop().then_inc(sv.h, 1)
        sv.inc()
    x1_bo_n = sv.n
    stB2.close()

    if "x1b" in tap:
        @on("sync")
        def _(e, n=x1_bo_n):
            wait(e, sv, n)
            for tt in range(8):
                tbat["x1b"].issue(e, tap["x1b"][128 * tt:128 * (tt + 1), :], x1[:, tt * D:(tt + 1) * D])

    if stage >= 3:
        @on("gpsimd")
        def _(e, n=x1_bo_n):
            wait(e, sv, n)
            if "x1b" in tap:
                tbat["x1b"].wait_all(e)
            otm_st.wait_seq(e, otm_st.seq - 1)
            for tt in range(8):
                e.dma_start(out=x1[:, tt * D:(tt + 1) * D], in_=ffn_sp[128 * tt:128 * (tt + 1), :],
                            accum_op=ALU.add).then_inc(sgp.h, 16)
                sgp.inc(16)
        x1_f_n = sgp.n
    else:
        x1_f_n = 0

    @on("gpsimd")
    def _(e, n=x1_bo_n, fn=x1_f_n):
        wait(e, sv, n)
        if fn:
            wait(e, sgp, fn)
        wait(e, cc["rs"], cc["rs"].n)
        for tt in range(8):
            e.dma_start(out=x1[:, tt * D:(tt + 1) * D], in_=rs_out[128 * tt:128 * (tt + 1), :],
                        accum_op=ALU.add).then_inc(sgp.h, 16)
            sgp.inc(16)
    x1_acc_n = sgp.n

    if "rsout" in tap:
        @on("sync")
        def _(e):
            wait(e, cc["rs"], cc["rs"].n)
            for tt in range(8):
                tbat["rsout"].issue(e, tap["rsout"][128 * tt:128 * (tt + 1), :],
                                    rs_out[128 * tt:128 * (tt + 1), :])

    if "bob" in tap:
        @on("sync")
        def _(e, n=bo_ev_n):
            wait(e, sv, n)
            tbat["bob"].issue(e, tap["bob"][:, :], bob[:])

    if "ffn0" in tap:
        @on("sync")
        def _(e, n=x1_f_n):
            if n:
                wait(e, sgp, n)
            for tt in range(8):
                tbat["ffn0"].issue(e, tap["ffn0"][128 * tt:128 * (tt + 1), :],
                                   ffn_sp[128 * tt:128 * (tt + 1), :])

    if "x1" in tap:
        @on("sync")
        def _(e, n=x1_acc_n):
            wait(e, sgp, n)
            for tt in range(8):
                tbat["x1"].issue(e, tap["x1"][128 * tt:128 * (tt + 1), :], x1[:, tt * D:(tt + 1) * D])

    if stage >= 4:
        import dataclasses
        xsl = dint("xsl", [S, 512], BF16)
        stL = contextlib.ExitStack()
        sbL = mk_sb(stL)
        xn1_fm = sbL("xn1_fm", [128, 16 * TOK], BF16)
        ones_bf = sbL("ones_bf", [128, 1], BF16)

        @on("vector")
        def _(e, n=x1_acc_n):
            wait(e, sgp, n)
            rs_st.wait_seq(e, rs_st.seq - 1)
            rs_st.wait_seq(e, rs_st.seq - 2)
            e.memset(ones_bf[:], 1.0).then_inc(sv.h, 1); sv.inc()

        stLN = contextlib.ExitStack()
        sbLN = mk_sb(stLN)
        psLN = mk_ps(stLN)
        xnt1 = [sbLN(f"xnt1_{i}", [128, D], BF16, side="right") for i in range(2)]
        pstL = [psLN(f"pstL{i}", [128, 2048], BF16) for i in range(2)]
        st_xg = DmaStream(nc, ex, "sxg", 2)
        ln1_nq = []
        ln1_ev = [0, 0]
        for tt in range(8):
            cb = 8 * (tt % 2)

            @on("scalar")
            def _(e, tt=tt, cb=cb, n=x1_acc_n, evp=ln1_ev[tt % 2]):
                if tt == 0:
                    wait(e, sgp, n)
                if tt >= 2:
                    wait(e, sv, ln1_nq[tt - 2])
                    wait(e, sv, evp)
                    st_xg.wait_seq(e, tt - 2)
                e.activation(xnt1[tt % 2][:], x1[:, tt * D:(tt + 1) * D], AF.Square,
                             accum_out=smalls[:, cb + 1:cb + 2]).then_inc(ss.h, 1)
                ss.inc()
            sq_n = ss.n

            @on("vector")
            def _(e, tt=tt, sq_n=sq_n, cb=cb):
                wait(e, ss, sq_n)
                e.reduce_sum(smalls[:, cb:cb + 1], x1[:, tt * D:(tt + 1) * D], AX)
                e.drain()
                e.tensor_scalar(smalls[:, cb:cb + 1], smalls[:, cb:cb + 1], 1.0 / D, None, ALU.mult)
                e.drain()
                e.tensor_tensor(smalls[:, cb + 2:cb + 3], smalls[:, cb:cb + 1],
                                smalls[:, cb:cb + 1], ALU.mult)
                e.drain()
                e.scalar_tensor_tensor(smalls[:, cb + 3:cb + 4], smalls[:, cb + 1:cb + 2],
                                       1.0 / D, smalls[:, cb + 2:cb + 3],
                                       ALU.mult, ALU.subtract).then_inc(sv.h, 1)
                sv.inc()
            var_n = sv.n

            @on("scalar")
            def _(e, var_n=var_n, cb=cb):
                wait(e, sv, var_n)
                e.activation(smalls[:, cb + 4:cb + 5], smalls[:, cb + 3:cb + 4],
                             AF.Sqrt, bias=cbias[:, 0:1]).then_inc(ss.h, 1)
                ss.inc()
            sqr_n = ss.n

            @on("vector")
            def _(e, tt=tt, sqr_n=sqr_n, cb=cb):
                wait(e, ss, sqr_n)
                e.reciprocal(smalls[:, cb + 5:cb + 6], smalls[:, cb + 4:cb + 5])
                e.drain()
                e.tensor_scalar(xnt1[tt % 2][:], x1[:, tt * D:(tt + 1) * D],
                                smalls[:, cb:cb + 1], smalls[:, cb + 5:cb + 6],
                                ALU.subtract, ALU.mult).then_inc(sv.h, 1)
                sv.inc()
            ln1_nq.append(sv.n)

            @on("tensor")
            def _(e, tt=tt, n=sv.n, prev=ln1_ev[tt % 2]):
                wait(e, sv, n)
                if prev:
                    wait(e, sv, prev)
                for dc in range(16):
                    ins = e.transpose(pstL[tt % 2][:, 128 * dc:128 * (dc + 1)],
                                      xnt1[tt % 2][:, 128 * dc:128 * (dc + 1)], idb[:])
                ins.then_inc(stn.h, 1)
                stn.inc()
            tr_n = stn.n

            @on("vector")
            def _(e, tt=tt, tr_n=tr_n):
                wait(e, stn, tr_n)
                e.tensor_copy(
                    xn1_fm[:].rearrange("p (a b) -> p a b", b=TOK)[:, :, 128 * tt:128 * (tt + 1)],
                    pstL[tt % 2][:].rearrange("p (a b) -> p a b", b=128)).then_inc(sv.h, 1)
                sv.inc()
            ln1_ev[tt % 2] = sv.n

            @on("sync")
            def _(e, tt=tt, n=ln1_nq[tt]):
                wait(e, sv, n)
                st_xg.issue(e, xg_in[128 * tt:128 * (tt + 1), :], xnt1[tt % 2][:])
        xn1_fm_done = sv.n

        if "xn1" in tap:
            @on("sync")
            def _(e):
                for j in range(8):
                    st_xg.wait_seq(e, j)
                tbat["xn1"].issue(e, tap["xn1"][:, :], xg_in[:, :])

        @on("gpsimd")
        def _(e):
            for j in range(8):
                st_xg.wait_seq(e, j)
            e.collective_compute("AllGather", ALU.bypass, replica_groups=RG4,
                                 ins=[xg_in.tensor.ap().opt()],
                                 outs=[xg_out.tensor.ap().opt()]).then_inc(cc["xg"].h, 1)
            cc["xg"].inc()

        ld_xsl = DmaBatch(nc, ex, "lxsl")

        @on("sync")
        def _(e):
            wait(e, cc["xg"], cc["xg"].n)
            pid = nc.partition_id(engines=[mybir.EngineType.SP])
            r512 = (pid % 4) * 512
            for k in range(4):
                ap = xg_out[1024 * k:1024 * (k + 1), 0:512]
                ap2 = dataclasses.replace(ap, offset=ap.offset + r512)
                ld_xsl.issue(e, xsl[1024 * k:1024 * (k + 1), :], ap2)
        stLN.close()

        stF1 = contextlib.ExitStack()
        sbF1 = mk_sb(stF1)
        psF1 = mk_ps(stF1)

        def ffn1_emit():
            ffn1_acc = sbF1("ffn1_acc", [128, 16 * 512], F32)
            p1_grp = sbF1("p1_grp", [128, 8 * 512], BF16)
            gsil1 = sbF1("gsil1", [128, 1024], BF16)
            sq_bf = sbF1("sq_bf", [128, 1024], BF16)
            w1str = [sbF1(f"w1s{i}", [128, 4 * 256], BF16) for i in range(2)]
            wp1t = [sbF1(f"wp1{i}", [128, 4 * 256], BF16) for i in range(2)]
            otm1 = sbF1("otm1", [128, 2048], F32)
            stg = sbF1("stg", [128, 512], F32)
            invc = sbF1("invc", [128, 4], F32)
            ps_gu1 = psF1("ps_gu1", [128, 1024], F32)
            ps_dn1 = psF1("ps_dn1", [128, 1024], F32)
            ps_h1 = psF1("ps_h1", [1, 512], F32)
            ld_w1 = DmaStream(nc, ex, "lw1", 2)
            ld_wp1 = DmaStream(nc, ex, "lwp1", 2)
            ot1_st = DmaStream(nc, ex, "so1", 1)
            ld_w1_cons = []
            ld_wp1_cons = []
            silu_h = 0
            p_h = 0
            dn_h = 0
            acc_h = 0
            spl_v = 0
            hsq_mm = 0

            @on("vector")
            def _(e):
                e.memset(stg[:], 0.0).then_inc(sv.h, 1); sv.inc()

            hsq_read_prev = 0
            for th in range(2):
                first_hsq = True
                for grp in range(8):
                    for s2 in range(4):
                        H = 8 * grp + 2 * s2
                        for gu in range(2):
                            HH = H + 64 * gu
                            for dh in range(4):
                                @on("sync")
                                def _(e, HH=HH, dh=dh,
                                      first=(th == 0 and grp == 0 and s2 == 0 and gu == 0 and dh == 0)):
                                    if first:
                                        wait(e, sv, xn1_fm_done)
                                        wait(e, cc["qgu"], cc["qgu"].n)
                                    if ld_w1.seq >= 2:
                                        wait(e, stn, ld_w1_cons[ld_w1.seq - 2])
                                    ld_w1.issue(e, w1str[ld_w1.seq % 2][:].rearrange("p (a q) -> p a q", q=256),
                                                qgu_out[2048 * (HH // 16) + 512 * dh:
                                                        2048 * (HH // 16) + 512 * (dh + 1),
                                                        128 * (HH % 16):128 * (HH % 16) + 256]
                                                .rearrange("(a p) q -> p a q", p=128))

                                @on("tensor")
                                def _(e, th=th, gu=gu, dh=dh, seq=ld_w1.seq - 1,
                                      sil=silu_h, pn=p_h):
                                    ld_w1.wait_seq(e, seq)
                                    if dh == 0:
                                        if gu == 0 and pn:
                                            wait(e, sv, pn)
                                        if gu == 1 and sil:
                                            wait(e, ss, sil)
                                    for a in range(4):
                                        dc = 4 * dh + a
                                        for i in range(2):
                                            ins = e.matmul(ps_gu1[:, 512 * i:512 * (i + 1)],
                                                           w1str[seq % 2][:, 256 * a + 128 * i:256 * a + 128 * (i + 1)],
                                                           xn1_fm[:, dc * TOK + 512 * th:dc * TOK + 512 * (th + 1)],
                                                           start=(dc == 0), stop=(dc == 15))
                                    ins.then_inc(stn.h, 1)
                                    stn.inc()
                                ld_w1_cons.append(stn.n)
                            gu_n = stn.n
                            if gu == 0:
                                @on("scalar")
                                def _(e, n=gu_n, pn=p_h):
                                    wait(e, stn, n)
                                    if pn:
                                        wait(e, sv, pn)
                                    e.activation(gsil1[:], ps_gu1[:, :], AF.Silu).then_inc(ss.h, 1)
                                    ss.inc()
                                silu_h = ss.n
                            else:
                                @on("vector")
                                def _(e, s2=s2, n=gu_n, sil=silu_h, dmm=dn_h):
                                    wait(e, stn, n)
                                    wait(e, ss, sil)
                                    if dmm:
                                        wait(e, stn, dmm)
                                    e.tensor_mul(p1_grp[:, 1024 * s2:1024 * (s2 + 1)],
                                                 gsil1[:], ps_gu1[:, :]).then_inc(sv.h, 1)
                                    sv.inc()
                                p_h = sv.n

                                @on("scalar")
                                def _(e, s2=s2, n=p_h, hs=hsq_mm):
                                    wait(e, sv, n)
                                    if hs:
                                        wait(e, stn, hs)
                                    e.activation(sq_bf[:], p1_grp[:, 1024 * s2:1024 * (s2 + 1)],
                                                 AF.Square).then_inc(ss.h, 1)
                                    ss.inc()
                                sq_n2 = ss.n

                                @on("tensor")
                                def _(e, n=sq_n2, first=first_hsq, last=(grp == 7 and s2 == 3),
                                      hrp=hsq_read_prev):
                                    wait(e, ss, n)
                                    if first and hrp:
                                        wait(e, sv, hrp)
                                    for i in range(2):
                                        ins = e.matmul(ps_h1[0:1, :], ones_bf[:],
                                                       sq_bf[:, 512 * i:512 * (i + 1)],
                                                       start=(first and i == 0),
                                                       stop=(last and i == 1)).then_inc(stn.h, 1)
                                        stn.inc()
                                hsq_mm = stn.n
                                first_hsq = False
                    p1_all = p_h

                    for dmp in range(8):
                        for ih in range(2):
                            @on("sync")
                            def _(e, grp=grp, dmp=dmp, ih=ih, first=(ld_wp1.seq == 0)):
                                if first:
                                    wait(e, cc["qp"], cc["qp"].n)
                                if ld_wp1.seq >= 2:
                                    wait(e, stn, ld_wp1_cons[ld_wp1.seq - 2])
                                ld_wp1.issue(e, wp1t[ld_wp1.seq % 2][:].rearrange("p (a q) -> p a q", q=256),
                                             qp_out[1024 * grp + 512 * ih:1024 * grp + 512 * (ih + 1),
                                                    256 * dmp:256 * (dmp + 1)]
                                             .rearrange("(a p) q -> p a q", p=128))

                            @on("tensor")
                            def _(e, dmp=dmp, ih=ih, seq=ld_wp1.seq - 1,
                                  pn=p1_all, accp=acc_h, spv=spl_v):
                                ld_wp1.wait_seq(e, seq)
                                if ih == 0:
                                    wait(e, sv, pn)
                                    if accp:
                                        wait(e, sv, accp)
                                    if dmp == 0 and spv:
                                        wait(e, sv, spv)
                                for a in range(4):
                                    i = 4 * ih + a
                                    for kk in range(2):
                                        ins = e.matmul(ps_dn1[:, 512 * kk:512 * (kk + 1)],
                                                       wp1t[seq % 2][:, 256 * a + 128 * kk:256 * a + 128 * (kk + 1)],
                                                       p1_grp[:, 512 * i:512 * (i + 1)],
                                                       start=(i == 0), stop=(i == 7))
                                ins.then_inc(stn.h, 1)
                                stn.inc()
                            ld_wp1_cons.append(stn.n)
                        dn_h = stn.n

                        @on("vector")
                        def _(e, grp=grp, dmp=dmp, n=dn_h, spv=spl_v):
                            wait(e, stn, n)
                            if grp == 0 and spv:
                                wait(e, sv, spv)
                            for kk in range(2):
                                dmc = 2 * dmp + kk
                                if grp == 0:
                                    ins = e.tensor_copy(ffn1_acc[:, 512 * dmc:512 * (dmc + 1)],
                                                        ps_dn1[:, 512 * kk:512 * (kk + 1)])
                                else:
                                    ins = e.tensor_add(ffn1_acc[:, 512 * dmc:512 * (dmc + 1)],
                                                       ffn1_acc[:, 512 * dmc:512 * (dmc + 1)],
                                                       ps_dn1[:, 512 * kk:512 * (kk + 1)])
                            ins.then_inc(sv.h, 1)
                            sv.inc()
                        acc_h = sv.n
                    yield
                acc_done = sv.n

                @on("vector")
                def _(e, n=hsq_mm):
                    wait(e, stn, n)
                    e.tensor_copy(stg[0:1, :], ps_h1[0:1, :])
                    e.drain()
                    e.tensor_scalar(stg[0:1, :], stg[0:1, :], 1.0 / 8192.0, None, ALU.mult).then_inc(sv.h, 1)
                    sv.inc()
                mh = sv.n
                hsq_read_prev = mh

                @on("scalar")
                def _(e, n=mh):
                    wait(e, sv, n)
                    e.activation(stg[0:1, :], stg[0:1, :], AF.Sqrt,
                                 bias=cbias[0:1, 3:4]).then_inc(ss.h, 1)
                    ss.inc()
                sh = ss.n

                @on("vector")
                def _(e, n=sh):
                    wait(e, ss, n)
                    e.reciprocal(stg[0:1, :], stg[0:1, :]).then_inc(sv.h, 1)
                    sv.inc()
                ih = sv.n

                @on("tensor")
                def _(e, n=ih, dnn=dn_h):
                    wait(e, sv, n)
                    wait(e, stn, dnn)
                    for k in range(4):
                        ins = e.transpose(ps_dn1[:, 128 * k:128 * (k + 1)],
                                          stg[:, 128 * k:128 * (k + 1)], idf[:])
                    ins.then_inc(stn.h, 1)
                    stn.inc()
                itr = stn.n

                @on("vector")
                def _(e, n=itr):
                    wait(e, stn, n)
                    for k in range(4):
                        ins = e.tensor_copy(invc[:, k:k + 1], ps_dn1[:, 128 * k:128 * k + 1])
                    ins.then_inc(sv.h, 1)
                    sv.inc()
                inv_done = sv.n

                spill_ev = 0
                for tq in range(4):
                    for dmg in range(4):
                        @on("tensor")
                        def _(e, tq=tq, dmg=dmg, n=acc_done, iv=inv_done, sev=spill_ev):
                            wait(e, sv, n)
                            wait(e, sv, iv)
                            if sev:
                                wait(e, sv, sev)
                            for k in range(4):
                                dmc = 4 * dmg + k
                                ins = e.transpose(ps_dn1[:, 128 * k:128 * (k + 1)],
                                                  ffn1_acc[:, 512 * dmc + 128 * tq:512 * dmc + 128 * (tq + 1)],
                                                  idf[:])
                            ins.then_inc(stn.h, 1)
                            stn.inc()
                        tr_n2 = stn.n

                        @on("vector")
                        def _(e, th=th, tq=tq, dmg=dmg, tr_n2=tr_n2, otp=(ot1_st.seq - 1)):
                            wait(e, stn, tr_n2)
                            if otp >= 0:
                                ot1_st.wait_seq(e, otp)
                            e.tensor_scalar(
                                otm1[:].rearrange("p (a b) -> p a b", b=128)[:, 4 * dmg:4 * dmg + 4, :],
                                ps_dn1[:, 0:512].rearrange("p (a b) -> p a b", b=128),
                                invc[:, tq:tq + 1], None, ALU.mult).then_inc(sv.h, 1)
                            sv.inc()
                        spill_ev = sv.n
                    spl_v = sv.n

                    @on("sync")
                    def _(e, th=th, tq=tq, n=spl_v, fwv=x1_f_n):
                        wait(e, sv, n)
                        wait(e, sgp, fwv)
                        ot1_st.issue(e, ffn_sp[512 * th + 128 * tq:512 * th + 128 * (tq + 1), :],
                                     otm1[:])
                    yield
            ffn1_state["ot1_st"] = ot1_st
            while True:
                yield

        ffn1_state = {}

        def gram_emit():
            stG = contextlib.ExitStack()
            sbG = mk_sb(stG)
            psG = mk_ps(stG)
            G_sb = sbG("G_sb", [128, 2048], F32, side="right")
            E1 = sbG("Eg", [128, 2048], BF16, side="right")
            qtrP = sbG("qtrP", [128, 2048], BF16, side="right")
            xslt = [sbG(f"xsl{i}", [128, 512], BF16, side="right") for i in range(2)]
            xgt = [sbG(f"xgt{i}", [128, 1024], BF16, side="right") for i in range(2)]
            pG = psG("pG", [128, 1024], F32)
            ptb = psG("ptb", [128, 1024], BF16)
            ld_xg2 = DmaStream(nc, ex, "lxg2", 2)
            ld_xs2 = DmaStream(nc, ex, "lxs2", 2)
            st_pt = DmaStream(nc, ex, "spt", 1)
            gmm_hist = []
            ev_h = 0
            exp_h = 0
            ptr_h = 0
            ptev_h = 0
            for ib in range(4):
                for jh in range(2):
                    for ct in range(32):
                        @on("sync")
                        def _(e, ib=ib, jh=jh, ct=ct, first=(ld_xs2.seq == 0)):
                            if first:
                                ld_xsl.wait_all(e)
                            if ld_xs2.seq >= 2:
                                wait(e, stn, gmm_hist[ld_xs2.seq - 2])
                            ld_xs2.issue(e, xslt[ld_xs2.seq % 2][:], xsl[128 * ct:128 * (ct + 1), :])
                            ld_xg2.issue(e, xgt[ld_xg2.seq % 2][:],
                                         xg_out[128 * ct:128 * (ct + 1), 1024 * jh:1024 * (jh + 1)])

                        @on("tensor")
                        def _(e, ib=ib, ct=ct, seq=ld_xs2.seq - 1, evh=ev_h):
                            ld_xs2.wait_seq(e, seq)
                            ld_xg2.wait_seq(e, seq)
                            if ct == 0 and evh:
                                wait(e, sv, evh)
                            for k in range(2):
                                ins = e.matmul(pG[:, 512 * k:512 * (k + 1)],
                                               xslt[seq % 2][:, 128 * ib:128 * (ib + 1)],
                                               xgt[seq % 2][:, 512 * k:512 * (k + 1)],
                                               start=(ct == 0), stop=(ct == 31))
                            ins.then_inc(stn.h, 1)
                            stn.inc()
                        gmm_hist.append(stn.n)
                    g_n = stn.n

                    @on("vector")
                    def _(e, jh=jh, n=g_n, exph=exp_h):
                        wait(e, stn, n)
                        if jh == 0 and exph:
                            wait(e, ss, exph)
                        e.tensor_copy(G_sb[:, 1024 * jh:1024 * (jh + 1)], pG[:, :]).then_inc(sv.h, 1)
                        sv.inc()
                    ev_h = sv.n
                    yield

                @on("vector")
                def _(e, n=ev_h):
                    wait(e, sv, n)
                    e.reduce_max(smalls[:, 32:33], G_sb[:], AX)
                    e.drain()
                    e.tensor_scalar(smalls[:, 33:34], smalls[:, 32:33], -1.0, None,
                                    ALU.mult).then_inc(sv.h, 1)
                    sv.inc()
                m_n = sv.n

                @on("scalar")
                def _(e, n=m_n, ptrh=ptr_h):
                    wait(e, sv, n)
                    if ptrh:
                        wait(e, stn, ptrh)
                    e.activation(E1[:], G_sb[:], AF.Exp, bias=smalls[:, 33:34],
                                 accum_out=smalls[:, 34:35]).then_inc(ss.h, 1)
                    ss.inc()
                exp_h = ss.n

                @on("vector")
                def _(e, n=exp_h):
                    wait(e, ss, n)
                    e.reciprocal(smalls[:, 35:36], smalls[:, 34:35])
                    e.drain()
                    e.tensor_scalar(E1[:], E1[:], smalls[:, 35:36], None, ALU.mult).then_inc(sv.h, 1)
                    sv.inc()
                p_n = sv.n

                for half in range(2):
                    @on("tensor")
                    def _(e, half=half, n=p_n, ptevh=ptev_h):
                        wait(e, sv, n)
                        if ptevh:
                            wait(e, sv, ptevh)
                        for k in range(8):
                            jc = 8 * half + k
                            ins = e.transpose(ptb[:, 128 * k:128 * (k + 1)],
                                              E1[:, 128 * jc:128 * (jc + 1)], idb[:])
                        ins.then_inc(stn.h, 1)
                        stn.inc()
                    ptr_h = stn.n

                    @on("vector")
                    def _(e, half=half, n=ptr_h, stp=(st_pt.seq - 1)):
                        wait(e, stn, n)
                        if stp >= 0 and half == 0:
                            st_pt.wait_seq(e, stp)
                        e.tensor_copy(qtrP[:, 1024 * half:1024 * (half + 1)], ptb[:]).then_inc(sv.h, 1)
                        sv.inc()
                    ptev_h = sv.n

                @on("sync")
                def _(e, ib=ib, n=ptev_h):
                    wait(e, sv, n)
                    st_pt.issue(e, pt_in[2048 * ib:2048 * (ib + 1), :].rearrange("(a p) q -> p a q", p=128),
                                qtrP[:].rearrange("p (a q) -> p a q", q=128))
                yield

            @on("gpsimd")
            def _(e, ns=st_pt.seq):
                st_pt.wait_seq(e, ns - 1)
                e.collective_compute("AllGather", ALU.bypass, replica_groups=RG4,
                                     ins=[pt_in.tensor.ap().opt()],
                                     outs=[pt_out.tensor.ap().opt()]).then_inc(cc["pt"].h, 1)
                cc["pt"].inc()
            stG.close()

            sbX = mk_sb(stF1)
            psX = mk_ps(stF1)
            ptt = [sbX(f"ptt{i}", [128, 256], BF16) for i in range(2)]
            cxe = sbX("cxe", [128, 512], F32)
            ps_cx = psX("ps_cx", [128, 512], F32)
            ps_tr1 = psX("ps_tr1", [128, 512], F32)
            ld_pt2 = DmaStream(nc, ex, "lpt2", 2)
            ld_pt2_cons = []
            cx_ev = 0
            cx_tr = 0
            cx_add = 0
            for I in range(16):
                for cc2 in range(2):
                    for jp in range(8):
                        @on("sync")
                        def _(e, I=I, jp=jp, first=(ld_pt2.seq == 0)):
                            if first:
                                wait(e, cc["pt"], cc["pt"].n)
                            if ld_pt2.seq >= 2:
                                wait(e, stn, ld_pt2_cons[ld_pt2.seq - 2])
                            base = 8192 * (I // 4) + (16 * (I % 4) + 2 * jp) * 128
                            ld_pt2.issue(e, ptt[ld_pt2.seq % 2][:].rearrange("p (a q) -> p a q", q=128),
                                         pt_out[base:base + 256, :].rearrange("(a p) q -> p a q", p=128))

                        @on("tensor")
                        def _(e, I=I, cc2=cc2, jp=jp, seq=ld_pt2.seq - 1, cxv=cx_ev):
                            ld_pt2.wait_seq(e, seq)
                            if jp == 0 and cxv:
                                wait(e, sv, cxv)
                            for k in range(2):
                                jc = 2 * jp + k
                                ins = e.matmul(ps_cx[:, :],
                                               ptt[seq % 2][:, 128 * k:128 * (k + 1)],
                                               xn1_fm[:, jc * TOK + 512 * cc2:jc * TOK + 512 * (cc2 + 1)],
                                               start=(jp == 0 and k == 0), stop=(jp == 7 and k == 1))
                            ins.then_inc(stn.h, 1)
                            stn.inc()
                        ld_pt2_cons.append(stn.n)
                    cmm = stn.n

                    @on("vector")
                    def _(e, n=cmm, ctr=cx_tr):
                        wait(e, stn, n)
                        if ctr:
                            wait(e, stn, ctr)
                        e.tensor_copy(cxe[:], ps_cx[:, :]).then_inc(sv.h, 1)
                        sv.inc()
                    cx_ev = sv.n

                    @on("tensor")
                    def _(e, n=cx_ev, ca=cx_add):
                        wait(e, sv, n)
                        if ca:
                            wait(e, sv, ca)
                        for k in range(4):
                            ins = e.transpose(ps_tr1[:, 128 * k:128 * (k + 1)],
                                              cxe[:, 128 * k:128 * (k + 1)], idf[:])
                        ins.then_inc(stn.h, 1)
                        stn.inc()
                    cx_tr = stn.n

                    @on("vector")
                    def _(e, I=I, cc2=cc2, n=cx_tr):
                        wait(e, stn, n)
                        for k in range(4):
                            tt = 4 * cc2 + k
                            ins = e.tensor_add(x1[:, tt * D + 128 * I:tt * D + 128 * (I + 1)],
                                               x1[:, tt * D + 128 * I:tt * D + 128 * (I + 1)],
                                               ps_tr1[:, 128 * k:128 * (k + 1)])
                        ins.then_inc(sv.h, 1)
                        sv.inc()
                    cx_add = sv.n
                    yield
            gram_state["cx_done"] = cx_add
            while True:
                yield

        gram_state = {}
        g1 = ffn1_emit()
        g2 = gram_emit()
        next(g1)
        for _ in range(80):
            next(g2)
            next(g1)
        for _ in range(40):
            next(g1)
        for _ in range(40):
            next(g2)

        ffn1_last_v = sv.n

        @on("gpsimd")
        def _(e, n=ffn1_last_v):
            wait(e, sv, n)
            ot1 = ffn1_state["ot1_st"]
            ot1.wait_seq(e, ot1.seq - 1)
            for tt in range(8):
                e.dma_start(out=x1[:, tt * D:(tt + 1) * D], in_=ffn_sp[128 * tt:128 * (tt + 1), :],
                            accum_op=ALU.add).then_inc(sgp.h, 16)
                sgp.inc(16)
        fin_n = sgp.n

        @on("sync")
        def _(e, n=fin_n):
            wait(e, sgp, n)
            for tt in range(8):
                st_out.issue(e, out_own[128 * tt:128 * (tt + 1), :], x1[:, tt * D:(tt + 1) * D])
            st_out.wait_all(e)
            for tb in tbat.values():
                tb.wait_all(e)
        stF1.close()
        stL.close()
        stD.close()
        ex.close()
        return nc

    if stage <= 3:
        @on("sync")
        def _(e, n=x1_acc_n):
            wait(e, sgp, n)
            for tt in range(8):
                st_out.issue(e, out_own[128 * tt:128 * (tt + 1), :], x1[:, tt * D:(tt + 1) * D])
            st_out.wait_all(e)
            for tb in tbat.values():
                tb.wait_all(e)
        stD.close()
        ex.close()
        return nc

    stD.close()
    ex.close()
    return nc


def kernel_stage(inputs, stage=1, taps=()):
    nc = build_nc(stage=stage, taps=taps)
    in_maps = [host_inputs(inputs, c) for c in range(8)]
    res = run_bass_kernel_spmd(nc, in_maps, core_ids=list(range(8)))
    return res


def kernel(**inputs):
    nc = build_nc(stage=4)
    in_maps = [host_inputs(inputs, c) for c in range(8)]
    res = run_bass_kernel_spmd(nc, in_maps, core_ids=list(range(8)))
    out = np.empty((2, 4096, 2048), np.float32)
    for c in range(8):
        b, r = c // 4, c % 4
        out[b, 1024 * r:1024 * (r + 1)] = res.results[c]["out_own"]
    return out


# version 06
# speedup vs baseline: 1.4821x; 1.0113x over previous
import contextlib
import numpy as np
import ml_dtypes

import concourse.bass as bass
import concourse.mybir as mybir
from concourse.bass_utils import run_bass_kernel_spmd

F32 = mybir.dt.float32
BF16 = mybir.dt.bfloat16
AF = mybir.ActivationFunctionType
ALU = mybir.AluOpType
AX = mybir.AxisListType.X
BF = ml_dtypes.bfloat16

B, S, D = 2, 4096, 2048
DH = 64
TOK = 1024
RG4 = [[0, 1, 2, 3], [4, 5, 6, 7]]
RG8 = [[0, 1, 2, 3, 4, 5, 6, 7]]
RND = 12582912.0
SWAP = [i ^ 1 for i in range(32)]


def host_tables():
    t = {}
    inv_freq = (np.float32(1.0) /
                (np.float32(10000.0) ** (np.arange(0, DH, 2, dtype=np.float32) / np.float32(DH)))).astype(np.float32)
    pos = np.arange(S, dtype=np.float32)
    freqs = (pos[:, None] * inv_freq[None, :]).astype(np.float32)
    cos = np.cos(freqs).astype(np.float32)
    sin = np.sin(freqs).astype(np.float32)
    cos_dup = np.repeat(cos, 2, axis=1)
    sin_sgn = np.empty((S, DH), np.float32)
    sin_sgn[:, 0::2] = -sin
    sin_sgn[:, 1::2] = sin
    t["cos_t"] = np.ascontiguousarray(np.tile(cos_dup.T, (2, 1)))
    t["sin_t"] = np.ascontiguousarray(np.tile(sin_sgn.T, (2, 1)))
    p = np.arange(128)
    t["tri"] = (p[None, :] >= p[:, None]).astype(np.float32)
    f = np.arange(512, dtype=np.float32)
    ddb = f[None, :] - np.arange(128, dtype=np.float32)[:, None]
    t["dd"] = np.concatenate([ddb - 128.0 * jl for jl in range(4)], axis=1).astype(np.float32)
    t["ident_bf"] = np.eye(128, dtype=np.float32).astype(BF)
    t["ident_f"] = np.eye(128, dtype=np.float32)
    return t


_TABLES = host_tables()


def host_inputs(inp, core):
    b, r = core // 4, core % 4
    x = np.asarray(inp["L0_attn_hidden_states"])
    Wq = np.asarray(inp["L0_attn_Wq"]); Wk = np.asarray(inp["L0_attn_Wk"])
    Wv = np.asarray(inp["L0_attn_Wv"]); Wo = np.asarray(inp["L0_attn_Wo"])
    bo = np.asarray(inp["L0_attn_bo"])
    Wg = np.asarray(inp["L0_ffn_W_g"]); Wu = np.asarray(inp["L0_ffn_W_u"]); Wd = np.asarray(inp["L0_ffn_W_d"])
    Wgu1 = np.asarray(inp["L1_ffn_w_gate_up"]); Wp1 = np.asarray(inp["L1_ffn_w_proj"])

    d = dict(_TABLES)
    d["x_full"] = np.ascontiguousarray(x[b])
    d["x_own"] = np.ascontiguousarray(x[b, 1024 * r:1024 * (r + 1)])
    qcols = []
    for t in range(4):
        for hh in (t, t + 4):
            g = 8 * r + hh
            qcols.extend(range(64 * g, 64 * g + 64))
    d["wqs"] = np.ascontiguousarray(Wq[:, qcols])
    d["wks"] = np.ascontiguousarray(Wk[:, 128 * r:128 * (r + 1)])
    d["wvs"] = np.ascontiguousarray(Wv[:, 128 * r:128 * (r + 1)])
    d["wos"] = np.ascontiguousarray(Wo[512 * r:512 * (r + 1), :])
    d["bo"] = np.ascontiguousarray(np.broadcast_to(bo[None, :], (1, D)))
    hs = np.arange(1, 33, dtype=np.float32) * np.float32(8.0) / np.float32(32.0)
    slopes = (np.float32(2.0) ** (-hs)).astype(np.float32)
    sl = slopes[8 * r:8 * r + 8].astype(np.float64)
    d["slopes_neg"] = np.ascontiguousarray(
        np.tile(-sl[None, :].astype(np.float32), (128, 1)))
    tab = np.zeros((128, 64), np.float64)
    for h in range(8):
        for dq in range(8):
            tab[:, 8 * h + dq] = -sl[h] * 512.0 * dq / 240.0
    d["bias_tab"] = np.ascontiguousarray(tab.astype(np.float32))
    d["wg_sh"] = np.ascontiguousarray(
        np.concatenate([Wg.T[256 * core:256 * (core + 1), :],
                        Wu.T[256 * core:256 * (core + 1), :]], axis=1))
    d["wd_sh"] = np.ascontiguousarray(Wd.T[1024 * core:1024 * (core + 1), :])
    d["wgu_sh"] = np.ascontiguousarray(Wgu1[2048 * core:2048 * (core + 1), :])
    d["wp_sh"] = np.ascontiguousarray(Wp1[:, 1024 * core:1024 * (core + 1)])
    return d


class Ctr:
    def __init__(self, h):
        self.h = h
        self.n = 0

    def inc(self, k=1):
        self.n += k
        return self.n


def wait(e, c, n=None):
    n = c.n if n is None else n
    assert n <= c.n, f"wait on future inc: {n} > {c.n}"
    if n > 0:
        e.wait_ge(c.h, n)


class DmaStream:

    def __init__(self, nc, ex, name, nbuf):
        self.sems = [Ctr(ex.enter_context(nc.semaphore(f"{name}{i}"))) for i in range(nbuf)]
        self.nbuf = nbuf
        self.seq = 0

    def issue(self, e, out, in_, **kw):
        c = self.sems[self.seq % self.nbuf]
        e.dma_start(out=out, in_=in_, **kw).then_inc(c.h, 16)
        c.inc(16)
        self.seq += 1
        return self.seq - 1

    def wait_seq(self, e, seq):
        assert 0 <= seq < self.seq
        c = self.sems[seq % self.nbuf]
        e.wait_ge(c.h, 16 * (seq // self.nbuf + 1))


class DmaBatch:

    def __init__(self, nc, ex, name):
        self.c = Ctr(ex.enter_context(nc.semaphore(name)))

    def issue(self, e, out, in_, **kw):
        e.dma_start(out=out, in_=in_, **kw).then_inc(self.c.h, 16)
        self.c.inc(16)

    def wait_all(self, e):
        if self.c.n:
            e.wait_ge(self.c.h, self.c.n)


def build_nc(stage=99, taps=()):
    nc = bass.Bass("TRN2", target_bir_lowering=False, debug=False)
    ex = contextlib.ExitStack()
    taps = set(taps)

    def din(name, shape, dt=F32):
        return nc.dram_tensor(name, shape, dt, kind="ExternalInput").ap()

    def dout(name, shape, dt=F32):
        return nc.dram_tensor(name, shape, dt, kind="ExternalOutput").ap()

    def dint(name, shape, dt=F32):
        return nc.dram_tensor(name, shape, dt, kind="Internal").ap()

    def mk_sb(stk):
        def f(name, shape, dt, side="left"):
            return stk.enter_context(nc.sbuf_tensor(name, shape, dt, side=side))
        return f

    def mk_ps(stk):
        def f(name, shape, dt=F32):
            return stk.enter_context(nc.psum_tensor(name, shape, dt))
        return f

    sb = mk_sb(ex)

    def sem(name):
        return Ctr(ex.enter_context(nc.semaphore(name)))

    x_full = din("x_full", [S, D])
    x_own = din("x_own", [TOK, D])
    wqs_i = din("wqs", [D, 512]); wks_i = din("wks", [D, 128]); wvs_i = din("wvs", [D, 128])
    wos_i = din("wos", [512, D]); bo_i = din("bo", [1, D])
    wg_sh = din("wg_sh", [256, 16384]); wd_sh = din("wd_sh", [1024, 2048])
    wgu_sh = din("wgu_sh", [2048, 2048]); wp_sh = din("wp_sh", [2048, 1024])
    cos_i = din("cos_t", [128, S]); sin_i = din("sin_t", [128, S])
    tri_i = din("tri", [128, 128]); dd_i = din("dd", [128, 2048])
    idb_i = din("ident_bf", [128, 128], BF16); idf_i = din("ident_f", [128, 128])
    sln_i = din("slopes_neg", [128, 8]); btab_i = din("bias_tab", [128, 64])

    out_own = dout("out_own", [TOK, D])
    tap = {}
    _tapspec = {
        "q": ([128, 4 * S], BF16), "k": ([128, S], BF16), "v": ([128, 32 * 130], BF16),
        "ctx": ([128, 4 * S], BF16), "x1": ([TOK, D], F32),
        "xn_own": ([128, 8 * D], BF16), "nq": ([128, 16 * TOK], BF16),
        "xn1": ([TOK, D], BF16), "gram": ([512, D], F32), "pown": ([128, 4 * D], BF16),
        "ffn0": ([TOK, D], F32),
        "rsout": ([TOK, D], F32), "bob": ([128, D], F32),
        "x1a": ([TOK, D], F32), "x1b": ([TOK, D], F32),
    }
    for t in taps:
        tap[t] = dout("tap_" + t, *_tapspec[t])

    sgu_in = dint("sgu_in", [256, 16384], BF16); sgu_out = dint("sgu_out", [2048, 16384], BF16)
    sd_in = dint("sd_in", [1024, 2048], BF16); sd_out = dint("sd_out", [8192, 2048], BF16)
    qgu_in = dint("qgu_in", [2048, 2048], BF16); qgu_out = dint("qgu_out", [16384, 2048], BF16)
    qp_in = dint("qp_in", [1024, 2048], BF16); qp_out = dint("qp_out", [8192, 2048], BF16)
    rs_in = dint("rs_in", [S, D]); rs_out = dint("rs_out", [TOK, D])
    ffn_sp = dint("ffn_sp", [TOK, D])
    xg_in = dint("xg_in", [TOK, D], BF16); xg_out = dint("xg_out", [S, D], BF16)
    a2a_in = dint("a2a_in", [S, 512], BF16); a2a_out = dint("a2a_out", [S, 512], BF16)
    pt_in = dint("pt_in", [8192, 128], BF16); pt_out = dint("pt_out", [32768, 128], BF16)

    sv = sem("sv")
    ss = sem("ss")
    stn = sem("stn")
    sgp = sem("sgp")

    ld_const = DmaBatch(nc, ex, "ldc")
    tbat = {t: DmaBatch(nc, ex, "tb_" + t) for t in taps}
    ld_wc = DmaBatch(nc, ex, "ldw")
    st_out = DmaBatch(nc, ex, "sto")
    cc = {nm: sem("cc_" + nm) for nm in
          ("sgu", "sd", "qgu", "qp", "rs", "xg", "a2a", "pt")}

    KT = sb("KT", [128, S], BF16)
    VE = sb("VE", [128, 32 * 130], BF16)
    idb = sb("idb", [128, 128], BF16)
    idf = sb("idf", [128, 128], F32)
    ones = sb("ones", [128, 128], F32)
    smalls = sb("smalls", [128, 64], F32)
    cbias = sb("cbias", [128, 8], F32)
    stQT = contextlib.ExitStack()
    QT = mk_sb(stQT)("QT", [128, 4 * S], BF16, side="right")

    stW = contextlib.ExitStack()
    sbW = mk_sb(stW)
    wq_sb = sbW("wq_sb", [128, 16 * 512], BF16, side="right")
    wk_sb = sbW("wk_sb", [128, 16 * 128], BF16, side="right")
    wv_sb = sbW("wv_sb", [128, 16 * 128], BF16, side="right")

    blk = ex.enter_context(nc.Block())

    def on(engine):
        def deco(fn):
            getattr(blk, engine)(fn)
        return deco

    @on("sync")
    def _(e):
        ld_const.issue(e, idb[:], idb_i[:, :])
        ld_const.issue(e, idf[:], idf_i[:, :])

    @on("gpsimd")
    def _(e):
        for dst, src in [
            (wq_sb[:].rearrange("p (a q) -> p a q", a=16), wqs_i[:, :].rearrange("(a p) q -> p a q", p=128)),
            (wk_sb[:].rearrange("p (a q) -> p a q", a=16), wks_i[:, :].rearrange("(a p) q -> p a q", p=128)),
            (wv_sb[:].rearrange("p (a q) -> p a q", a=16), wvs_i[:, :].rearrange("(a p) q -> p a q", p=128)),
        ]:
            ld_wc.issue(e, dst, src)

    @on("vector")
    def _(e):
        e.memset(ones[:], 1.0)
        e.memset(VE[:].rearrange("p (a b) -> p a b", b=65)[:, :, 64:65], 1.0)
        for c, val in [(0, 1e-5), (1, 1e-8), (2, -30.0), (3, 1.1920929e-07)]:
            e.memset(cbias[:, c:c + 1], val)
        e.engine_nop().then_inc(sv.h, 1); sv.inc()
    consts_v = sv.n

    if stage >= 3:
        st0 = contextlib.ExitStack()
        sb0 = mk_sb(st0)
        sgn = [sb0(f"sgn{i}", [128, 8192], BF16, side="right") for i in range(2)]
        ld_sgn = DmaStream(nc, ex, "lsgn", 2)
        st_sgn = DmaStream(nc, ex, "ssgn", 2)
        jobs = [("gu", tt, hh) for tt in range(2) for hh in range(2)] + \
               [("d", tt, 0) for tt in range(8)]
        sgn_done = []
        for j, (kind, tt, hh) in enumerate(jobs):
            @on("gpsimd")
            def _(e, j=j, kind=kind, tt=tt, hh=hh):
                if j >= 2:
                    wait(e, ss, sgn_done[j - 2])
                    st_sgn.wait_seq(e, j - 2)
                if kind == "gu":
                    ld_sgn.issue(e, sgn[j % 2][:], wg_sh[128 * tt:128 * (tt + 1), 8192 * hh:8192 * (hh + 1)])
                else:
                    ld_sgn.issue(e, sgn[j % 2][:, :2048], wd_sh[128 * tt:128 * (tt + 1), :])

            @on("scalar")
            def _(e, j=j, kind=kind):
                ld_sgn.wait_seq(e, j)
                if j >= 2:
                    st_sgn.wait_seq(e, j - 2)
                if kind == "gu":
                    e.activation(sgn[j % 2][:], sgn[j % 2][:], AF.Sign).then_inc(ss.h, 1)
                else:
                    e.activation(sgn[j % 2][:, :2048], sgn[j % 2][:, :2048], AF.Sign).then_inc(ss.h, 1)
                ss.inc()
            sgn_done.append(ss.n)

            @on("gpsimd")
            def _(e, j=j, kind=kind, tt=tt, hh=hh):
                wait(e, ss, sgn_done[j])
                if kind == "gu":
                    st_sgn.issue(e, sgu_in[128 * tt:128 * (tt + 1), 8192 * hh:8192 * (hh + 1)], sgn[j % 2][:])
                else:
                    st_sgn.issue(e, sd_in[128 * tt:128 * (tt + 1), :], sgn[j % 2][:, :2048])

        @on("gpsimd")
        def _(e):
            for j in range(len(jobs)):
                st_sgn.wait_seq(e, j)
            e.collective_compute("AllGather", ALU.bypass, replica_groups=RG8,
                                 ins=[sgu_in.tensor.ap().opt()],
                                 outs=[sgu_out.tensor.ap().opt()]).then_inc(cc["sgu"].h, 1)
            cc["sgu"].inc()
            e.collective_compute("AllGather", ALU.bypass, replica_groups=RG8,
                                 ins=[sd_in.tensor.ap().opt()],
                                 outs=[sd_out.tensor.ap().opt()]).then_inc(cc["sd"].h, 1)
            cc["sd"].inc()
        st0.close()

    st1 = contextlib.ExitStack()
    sb1 = mk_sb(st1)
    ps1 = mk_ps(st1)
    xt = [sb1(f"xt{i}", [128, D], F32, side="right") for i in range(2)]
    xnt = [sb1(f"xnt{i}", [128, D], BF16, side="right") for i in range(2)]
    xnf = sb1("xnf", [128, 16 * 512], BF16, side="right")
    cs_sl = sb1("cs_sl", [128, 512], F32, side="right")
    sn_sl = sb1("sn_sl", [128, 512], F32, side="right")
    qshu = sb1("qshu", [128, 512], F32, side="right")
    qtmp = sb1("qtmp", [128, 512], F32, side="right")
    pst = [ps1(f"pst{i}", [128, 2048], BF16) for i in range(2)]
    psq = ps1("psq", [128, 512], F32)
    psk = ps1("psk", [128, 512], F32)
    psv = ps1("psv", [128, 128], F32)

    ld_x = DmaStream(nc, ex, "lx", 2)
    ld_cs = DmaStream(nc, ex, "lcs", 2)

    if stage >= 3:
        @on("sync")
        def _(e):
            st_sgn.wait_seq(e, st_sgn.seq - 1)
            st_sgn.wait_seq(e, st_sgn.seq - 2)

    xn_hist = []
    var_hist = []
    evac_done = [0, 0]
    rope_done = 0
    v_last_evac = 0

    for ch in range(8):
        t0 = 512 * ch

        @on("sync")
        def _(e, t0=t0, rope_prev=rope_done):
            if rope_prev:
                wait(e, sv, rope_prev)
            ld_cs.issue(e, cs_sl[:], cos_i[:, t0:t0 + 512])
            ld_cs.issue(e, sn_sl[:], sin_i[:, t0:t0 + 512])
        cs_seq = ld_cs.seq - 1

        for tt in range(4):
            gi = 4 * ch + tt
            cb = 8 * (gi % 2)

            @on("sync")
            def _(e, tt=tt, t0=t0, gi=gi):
                if gi >= 2:
                    wait(e, sv, xn_hist[gi - 2])
                ld_x.issue(e, xt[tt % 2][:], x_full[t0 + 128 * tt:t0 + 128 * (tt + 1), :])

            @on("scalar")
            def _(e, tt=tt, gi=gi, cb=cb, evp=evac_done[tt % 2]):
                ld_x.wait_seq(e, gi)
                if gi >= 2:
                    wait(e, sv, var_hist[gi - 2])
                    wait(e, sv, evp)
                e.activation(xnt[tt % 2][:], xt[tt % 2][:], AF.Square,
                             accum_out=smalls[:, cb + 1:cb + 2]).then_inc(ss.h, 1)
                ss.inc()
            sq_n = ss.n

            @on("vector")
            def _(e, tt=tt, gi=gi, sq_n=sq_n, cb=cb):
                ld_x.wait_seq(e, gi)
                e.reduce_sum(smalls[:, cb:cb + 1], xt[tt % 2][:], AX)
                wait(e, ss, sq_n)
                e.drain()
                e.tensor_scalar(smalls[:, cb:cb + 1], smalls[:, cb:cb + 1], 1.0 / D, None, ALU.mult)
                e.drain()
                e.tensor_tensor(smalls[:, cb + 2:cb + 3], smalls[:, cb:cb + 1],
                                smalls[:, cb:cb + 1], ALU.mult)
                e.drain()
                e.scalar_tensor_tensor(smalls[:, cb + 3:cb + 4], smalls[:, cb + 1:cb + 2], 1.0 / D,
                                       smalls[:, cb + 2:cb + 3], ALU.mult, ALU.subtract).then_inc(sv.h, 1)
                sv.inc()
            var_hist.append(sv.n)

            @on("scalar")
            def _(e, var_n=sv.n, cb=cb, first=(gi == 0)):
                wait(e, sv, var_n)
                if first:
                    wait(e, sv, consts_v)
                e.activation(smalls[:, cb + 4:cb + 5], smalls[:, cb + 3:cb + 4],
                             AF.Sqrt, bias=cbias[:, 0:1]).then_inc(ss.h, 1)
                ss.inc()
            sqr_n = ss.n

            @on("vector")
            def _(e, tt=tt, sqr_n=sqr_n, cb=cb):
                wait(e, ss, sqr_n)
                e.reciprocal(smalls[:, cb + 5:cb + 6], smalls[:, cb + 4:cb + 5])
                e.drain()
                e.tensor_scalar(xnt[tt % 2][:], xt[tt % 2][:], smalls[:, cb:cb + 1],
                                smalls[:, cb + 5:cb + 6], ALU.subtract, ALU.mult).then_inc(sv.h, 1)
                sv.inc()
            xn_hist.append(sv.n)

            @on("tensor")
            def _(e, tt=tt, xn_n=sv.n, prev_ev=evac_done[tt % 2], first=(gi == 0)):
                wait(e, sv, xn_n)
                if first:
                    ld_const.wait_all(e)
                if prev_ev:
                    wait(e, sv, prev_ev)
                for dc in range(16):
                    ins = e.transpose(pst[tt % 2][:, 128 * dc:128 * (dc + 1)],
                                      xnt[tt % 2][:, 128 * dc:128 * (dc + 1)], idb[:])
                ins.then_inc(stn.h, 1)
                stn.inc()
            tr_n = stn.n

            @on("vector")
            def _(e, tt=tt, tr_n=tr_n):
                wait(e, stn, tr_n)
                e.tensor_copy(
                    xnf[:].rearrange("p (a b) -> p a b", b=512)[:, :, 128 * tt:128 * (tt + 1)],
                    pst[tt % 2][:].rearrange("p (a b) -> p a b", b=128)).then_inc(sv.h, 1)
                sv.inc()
            evac_done[tt % 2] = sv.n

        xnf_n = sv.n

        for pr in range(4):
            @on("tensor")
            def _(e, pr=pr, ch=ch, xnf_n=xnf_n, rope_prev=rope_done):
                wait(e, sv, xnf_n)
                if rope_prev:
                    wait(e, sv, rope_prev)
                if pr == 0 and ch == 0:
                    ld_wc.wait_all(e)
                for dc in range(16):
                    ins = e.matmul(psq[:, :], wq_sb[:, dc * 512 + pr * 128:dc * 512 + (pr + 1) * 128],
                                   xnf[:, dc * 512:(dc + 1) * 512], start=(dc == 0), stop=(dc == 15))
                ins.then_inc(stn.h, 1)
                stn.inc()
            qmm_n = stn.n

            @on("vector")
            def _(e, pr=pr, ch=ch, qmm_n=qmm_n, cs_seq=cs_seq):
                wait(e, stn, qmm_n)
                ld_cs.wait_seq(e, cs_seq - 1)
                ld_cs.wait_seq(e, cs_seq)
                dst = QT[:, pr * S + 512 * ch: pr * S + 512 * (ch + 1)]
                e.stream_shuffle(qshu[:], psq[:, :], SWAP)
                e.tensor_tensor(qtmp[:], psq[:, :], cs_sl[:], ALU.mult)
                e.drain()
                e.tensor_tensor(qshu[:], qshu[:], sn_sl[:], ALU.mult)
                e.drain()
                e.tensor_add(dst, qtmp[:], qshu[:]).then_inc(sv.h, 1)
                sv.inc()
            rope_done = sv.n

        @on("tensor")
        def _(e, xnf_n=xnf_n, rope_prev=rope_done):
            wait(e, sv, rope_prev)
            for dc in range(16):
                ins = e.matmul(psk[:, :], wk_sb[:, dc * 128:(dc + 1) * 128],
                               xnf[:, dc * 512:(dc + 1) * 512], start=(dc == 0), stop=(dc == 15))
            ins.then_inc(stn.h, 1)
            stn.inc()
        kmm_n = stn.n

        @on("vector")
        def _(e, ch=ch, kmm_n=kmm_n):
            wait(e, stn, kmm_n)
            dst = KT[:, 512 * ch:512 * (ch + 1)]
            e.stream_shuffle(qshu[:], psk[:, :], SWAP)
            e.tensor_tensor(qtmp[:], psk[:, :], cs_sl[:], ALU.mult)
            e.drain()
            e.tensor_tensor(qshu[:], qshu[:], sn_sl[:], ALU.mult)
            e.drain()
            e.tensor_add(dst, qtmp[:], qshu[:]).then_inc(sv.h, 1)
            sv.inc()
        rope_done = sv.n

        for tt in range(4):
            @on("tensor")
            def _(e, tt=tt, xnf_n=xnf_n, vprev=v_last_evac):
                if vprev:
                    wait(e, sv, vprev)
                wait(e, sv, xnf_n)
                for dc in range(16):
                    ins = e.matmul(psv[:, :], xnf[:, dc * 512 + 128 * tt:dc * 512 + 128 * (tt + 1)],
                                   wv_sb[:, dc * 128:(dc + 1) * 128], start=(dc == 0), stop=(dc == 15))
                ins.then_inc(stn.h, 1)
                stn.inc()
            vmm_n = stn.n

            @on("vector")
            def _(e, ch=ch, tt=tt, vmm_n=vmm_n):
                wait(e, stn, vmm_n)
                jt = 4 * ch + tt
                e.tensor_copy(VE[:, jt * 130:jt * 130 + 64], psv[:, 0:64])
                e.tensor_copy(VE[:, jt * 130 + 65:jt * 130 + 129], psv[:, 64:128]).then_inc(sv.h, 1)
                sv.inc()
            v_last_evac = sv.n

    st1.close()
    stW.close()

    def tap_store(name, src_ap):
        if name in tap:
            @on("sync")
            def _(e):
                wait(e, sv, sv.n)
                wait(e, ss, ss.n)
                wait(e, stn, stn.n)
                tbat[name].issue(e, tap[name][:, :], src_ap)

    tap_store("q", QT[:])
    tap_store("k", KT[:])
    tap_store("v", VE[:])

    if stage <= 1:
        z = sb("zz", [128, D], F32)

        @on("vector")
        def _(e):
            e.memset(z[:], 0.0).then_inc(sv.h, 1); sv.inc()
        zn = sv.n

        @on("sync")
        def _(e):
            wait(e, sv, zn)
            for tt in range(8):
                st_out.issue(e, out_own[128 * tt:128 * (tt + 1), :], z[:])
            st_out.wait_all(e)
        stQT.close()
        ex.close()
        return nc

    stC = contextlib.ExitStack()
    sbC = mk_sb(stC)
    psC = mk_ps(stC)
    CTX = sbC("CTX", [128, 4 * S], BF16, side="right")
    T4 = sbC("T4", [128, 2048], F32)
    E4 = sbC("E4", [128, 2048], BF16, side="right")
    dnrb = sbC("dnrb", [128, 512], F32, side="right")
    tmpdiv = [sbC("tdv0", [128, 512], BF16, side="right")]
    tri = sbC("tri_s", [128, 128], F32, side="right")
    dd = sbC("dd_s", [128, 2048], F32, side="right")
    sln = sbC("sln_s", [128, 8], F32, side="right")
    btab = sbC("btab_s", [128, 64], F32, side="right")
    wos_sb = sbC("wos_sb", [128, 4 * D], BF16, side="right")
    wo_out = [sbC(f"woo{i}", [128, 512], F32, side="right") for i in range(2)]
    ps_s = psC("ps_s", [128, 512], F32)
    ps_c = psC("ps_c", [128, 512], F32)
    ps_w = psC("ps_w", [128, 1024], F32)

    ld_attc = DmaBatch(nc, ex, "ldac")
    ctx_dma = DmaStream(nc, ex, "cdx", 2)
    rs_st = DmaStream(nc, ex, "rst", 2)

    qkv_done_v = sv.n

    ffn_gen = None
    otm_st = None
    l1q_state = {}
    if stage >= 3:
        stF = contextlib.ExitStack()
        sbF = mk_sb(stF)
        n_fm = sbF("n_fm", [128, 16 * TOK], BF16)
        otm_st = DmaStream(nc, ex, "sfo", 1)

        def ffn_emit():
            stQ = contextlib.ExitStack()
            sbQ = mk_sb(stQ)
            psQ = mk_ps(stQ)
            xo_t = [sbQ(f"xo{i}", [128, D], F32, side="right") for i in range(2)]
            yq = sbQ("yq", [128, D], F32, side="right")
            q_s = sbQ("q_s", [128, D], F32, side="right")
            nq_t = [sbQ(f"nq{i}", [128, D], BF16, side="right") for i in range(2)]
            psqt = psQ("psqt", [128, 2048], BF16)
            ld_xq = DmaStream(nc, ex, "lxq", 2)
            nq_hist = []
            qvar_hist = []
            qev_hist = [0, 0]
            for tt in range(8):
                cb = 8 * (tt % 2)

                @on("sync")
                def _(e, tt=tt, q0=qkv_done_v):
                    if tt == 0:
                        wait(e, sv, q0)
                        if "st_l1" in l1q_state:
                            s = l1q_state["st_l1"]
                            s.wait_seq(e, s.seq - 1)
                            s.wait_seq(e, s.seq - 2)
                    if tt >= 2:
                        wait(e, sv, nq_hist[tt - 2])
                    ld_xq.issue(e, xo_t[tt % 2][:], x_own[128 * tt:128 * (tt + 1), :])

                @on("scalar")
                def _(e, tt=tt, cb=cb, evp=qev_hist[tt % 2]):
                    ld_xq.wait_seq(e, tt)
                    if tt >= 2:
                        wait(e, sv, qvar_hist[tt - 2])
                        wait(e, sv, evp)
                    e.activation(nq_t[tt % 2][:], xo_t[tt % 2][:], AF.Square,
                                 accum_out=smalls[:, cb + 1:cb + 2]).then_inc(ss.h, 1)
                    ss.inc()
                sq_n = ss.n

                @on("vector")
                def _(e, tt=tt, sq_n=sq_n, cb=cb):
                    ld_xq.wait_seq(e, tt)
                    e.reduce_sum(smalls[:, cb:cb + 1], xo_t[tt % 2][:], AX)
                    wait(e, ss, sq_n)
                    e.drain()
                    e.tensor_scalar(smalls[:, cb:cb + 1], smalls[:, cb:cb + 1], 1.0 / D, None, ALU.mult)
                    e.drain()
                    e.tensor_tensor(smalls[:, cb + 2:cb + 3], smalls[:, cb:cb + 1],
                                    smalls[:, cb:cb + 1], ALU.mult)
                    e.drain()
                    e.scalar_tensor_tensor(smalls[:, cb + 3:cb + 4], smalls[:, cb + 1:cb + 2],
                                           1.0 / D, smalls[:, cb + 2:cb + 3],
                                           ALU.mult, ALU.subtract).then_inc(sv.h, 1)
                    sv.inc()
                qvar_hist.append(sv.n)

                @on("scalar")
                def _(e, n=sv.n, cb=cb):
                    wait(e, sv, n)
                    e.activation(smalls[:, cb + 4:cb + 5], smalls[:, cb + 3:cb + 4],
                                 AF.Sqrt, bias=cbias[:, 1:2]).then_inc(ss.h, 1)
                    ss.inc()
                sqr_n = ss.n

                @on("vector")
                def _(e, tt=tt, sqr_n=sqr_n, cb=cb):
                    wait(e, ss, sqr_n)
                    e.reciprocal(smalls[:, cb + 5:cb + 6], smalls[:, cb + 4:cb + 5])
                    e.drain()
                    e.tensor_scalar(yq[:], xo_t[tt % 2][:], smalls[:, cb:cb + 1],
                                    smalls[:, cb + 5:cb + 6], ALU.subtract, ALU.mult)
                    e.drain()
                    e.reduce_max(smalls[:, cb + 6:cb + 7], yq[:], AX, apply_absolute_value=True)
                    e.drain()
                    e.reciprocal(smalls[:, cb + 7:cb + 8], smalls[:, cb + 6:cb + 7])
                    e.drain()
                    e.tensor_scalar(smalls[:, cb + 7:cb + 8], smalls[:, cb + 7:cb + 8],
                                    127.0, None, ALU.mult)
                    e.drain()
                    e.reciprocal(smalls[:, cb + 6:cb + 7], smalls[:, cb + 7:cb + 8])
                    e.drain()
                    e.tensor_scalar(q_s[:], yq[:], smalls[:, cb + 7:cb + 8], RND,
                                    ALU.mult, ALU.add)
                    e.drain()
                    e.tensor_scalar(q_s[:], q_s[:], RND, -128.0, ALU.subtract, ALU.max)
                    e.drain()
                    e.tensor_scalar(nq_t[tt % 2][:], q_s[:], 127.0, smalls[:, cb + 6:cb + 7],
                                    ALU.min, ALU.mult).then_inc(sv.h, 1)
                    sv.inc()
                nq_hist.append(sv.n)

                @on("tensor")
                def _(e, tt=tt, n=sv.n, prev=qev_hist[tt % 2]):
                    wait(e, sv, n)
                    if prev:
                        wait(e, sv, prev)
                    for dc in range(16):
                        ins = e.transpose(psqt[:, 128 * dc:128 * (dc + 1)],
                                          nq_t[tt % 2][:, 128 * dc:128 * (dc + 1)], idb[:])
                    ins.then_inc(stn.h, 1)
                    stn.inc()
                tr_n = stn.n

                @on("vector")
                def _(e, tt=tt, tr_n=tr_n):
                    wait(e, stn, tr_n)
                    e.tensor_copy(
                        n_fm[:].rearrange("p (a b) -> p a b", b=TOK)[:, :, 128 * tt:128 * (tt + 1)],
                        psqt[:].rearrange("p (a b) -> p a b", b=128)).then_inc(sv.h, 1)
                    sv.inc()
                qev_hist[tt % 2] = sv.n
                if tt % 2 == 1:
                    yield
            nq_all = sv.n
            if "nq" in tap:
                @on("sync")
                def _(e, n=nq_all):
                    wait(e, sv, n)
                    tbat["nq"].issue(e, tap["nq"][:, :], n_fm[:])
            stQ.close()

            sbF2 = mk_sb(stF)
            psF2 = mk_ps(stF)
            ffn_acc = sbF2("ffn_acc", [128, 16 * 512], F32)
            p_grp = sbF2("p_grp", [128, 8 * 512], BF16)
            gsil = sbF2("gsil", [128, 1024], BF16)
            wstr = [sbF2(f"wstr{i}", [128, 2 * 256], BF16) for i in range(2)]
            wdt = [sbF2(f"wdt{i}", [128, 2 * 256], BF16) for i in range(2)]
            otm = [sbF2("otm0", [128, 1024], F32)]
            ps_gu = psF2("ps_gu", [128, 1024], F32)
            ps_dn = psF2("ps_dn", [128, 1024], F32)
            ld_w = DmaStream(nc, ex, "lw", 2)
            ld_wd = DmaStream(nc, ex, "lwd", 2)
            ld_w_cons = []
            ld_wd_cons = []
            silu_hist = 0
            p_hist = 0
            dn_mm_hist = 0
            acc_ev_hist = 0
            spill_v = 0

            for th in range(2):
                for grp in range(8):
                    for s2 in range(4):
                        hc0 = 128 * (8 * grp + 2 * s2)
                        for gu in range(2):
                            for dh in range(8):
                                @on("sync")
                                def _(e, dh=dh, gu=gu, hc0=hc0,
                                      first=(th == 0 and grp == 0 and s2 == 0 and gu == 0 and dh == 0)):
                                    if first:
                                        wait(e, sv, nq_all)
                                        wait(e, cc["sgu"], cc["sgu"].n)
                                    if ld_w.seq >= 2:
                                        wait(e, stn, ld_w_cons[ld_w.seq - 2])
                                    ld_w.issue(e, wstr[ld_w.seq % 2][:].rearrange("p (a q) -> p a q", q=256),
                                               sgu_out[256 * dh:256 * (dh + 1),
                                                       8192 * gu + hc0:8192 * gu + hc0 + 256]
                                               .rearrange("(a p) q -> p a q", p=128))

                                @on("tensor")
                                def _(e, th=th, gu=gu, dh=dh, seq=ld_w.seq - 1,
                                      sil=silu_hist, pn=p_hist):
                                    ld_w.wait_seq(e, seq)
                                    if dh == 0:
                                        if gu == 0 and pn:
                                            wait(e, sv, pn)
                                        if gu == 1 and sil:
                                            wait(e, ss, sil)
                                    for a in range(2):
                                        dc = 2 * dh + a
                                        for i in range(2):
                                            ins = e.matmul(ps_gu[:, 512 * i:512 * (i + 1)],
                                                           wstr[seq % 2][:, 256 * a + 128 * i:256 * a + 128 * (i + 1)],
                                                           n_fm[:, dc * TOK + 512 * th:dc * TOK + 512 * (th + 1)],
                                                           start=(dc == 0), stop=(dc == 15))
                                    ins.then_inc(stn.h, 1)
                                    stn.inc()
                                ld_w_cons.append(stn.n)
                            gu_mm_n = stn.n
                            if gu == 0:
                                @on("scalar")
                                def _(e, n=gu_mm_n, pn=p_hist):
                                    wait(e, stn, n)
                                    if pn:
                                        wait(e, sv, pn)
                                    e.activation(gsil[:], ps_gu[:, :], AF.Silu).then_inc(ss.h, 1)
                                    ss.inc()
                                silu_hist = ss.n
                            else:
                                @on("vector")
                                def _(e, s2=s2, n=gu_mm_n, sil=silu_hist, dmm=dn_mm_hist):
                                    wait(e, stn, n)
                                    wait(e, ss, sil)
                                    if dmm:
                                        wait(e, stn, dmm)
                                    e.tensor_mul(p_grp[:, 1024 * s2:1024 * (s2 + 1)],
                                                 gsil[:], ps_gu[:, :]).then_inc(sv.h, 1)
                                    sv.inc()
                                p_hist = sv.n
                    p_all_n = p_hist

                    for dmp in range(8):
                        for ih in range(4):
                            @on("sync")
                            def _(e, grp=grp, dmp=dmp, ih=ih, first=(ld_wd.seq == 0)):
                                if first:
                                    wait(e, cc["sd"], cc["sd"].n)
                                if ld_wd.seq >= 2:
                                    wait(e, stn, ld_wd_cons[ld_wd.seq - 2])
                                ld_wd.issue(e, wdt[ld_wd.seq % 2][:].rearrange("p (a q) -> p a q", q=256),
                                            sd_out[1024 * grp + 256 * ih:1024 * grp + 256 * (ih + 1),
                                                   256 * dmp:256 * (dmp + 1)]
                                            .rearrange("(a p) q -> p a q", p=128))

                            @on("tensor")
                            def _(e, dmp=dmp, ih=ih, seq=ld_wd.seq - 1,
                                  pn=p_all_n, accp=acc_ev_hist, spv=spill_v):
                                ld_wd.wait_seq(e, seq)
                                if ih == 0:
                                    wait(e, sv, pn)
                                    if accp:
                                        wait(e, sv, accp)
                                    if dmp == 0 and spv:
                                        wait(e, sv, spv)
                                for a in range(2):
                                    i = 2 * ih + a
                                    for kk in range(2):
                                        ins = e.matmul(ps_dn[:, 512 * kk:512 * (kk + 1)],
                                                       wdt[seq % 2][:, 256 * a + 128 * kk:256 * a + 128 * (kk + 1)],
                                                       p_grp[:, 512 * i:512 * (i + 1)],
                                                       start=(i == 0), stop=(i == 7))
                                ins.then_inc(stn.h, 1)
                                stn.inc()
                            ld_wd_cons.append(stn.n)
                        dn_mm_hist = stn.n

                        @on("vector")
                        def _(e, grp=grp, dmp=dmp, n=dn_mm_hist, spv=spill_v):
                            wait(e, stn, n)
                            if grp == 0 and spv:
                                wait(e, sv, spv)
                            for kk in range(2):
                                dmc = 2 * dmp + kk
                                if grp == 0:
                                    ins = e.tensor_copy(ffn_acc[:, 512 * dmc:512 * (dmc + 1)],
                                                        ps_dn[:, 512 * kk:512 * (kk + 1)])
                                else:
                                    ins = e.tensor_add(ffn_acc[:, 512 * dmc:512 * (dmc + 1)],
                                                       ffn_acc[:, 512 * dmc:512 * (dmc + 1)],
                                                       ps_dn[:, 512 * kk:512 * (kk + 1)])
                            ins.then_inc(sv.h, 1)
                            sv.inc()
                        acc_ev_hist = sv.n
                    yield
                acc_done = sv.n

                spill_ev = 0
                for tq in range(4):
                    for dmg in range(4):
                        @on("tensor")
                        def _(e, tq=tq, dmg=dmg, n=acc_done, sev=spill_ev):
                            wait(e, sv, n)
                            if sev:
                                wait(e, sv, sev)
                            for k in range(4):
                                dmc = 4 * dmg + k
                                ins = e.transpose(ps_dn[:, 128 * k:128 * (k + 1)],
                                                  ffn_acc[:, 512 * dmc + 128 * tq:512 * dmc + 128 * (tq + 1)],
                                                  idf[:])
                            ins.then_inc(stn.h, 1)
                            stn.inc()
                        tr_n = stn.n

                        @on("vector")
                        def _(e, tq=tq, dmg=dmg, tr_n=tr_n, otp=(otm_st.seq - 1)):
                            wait(e, stn, tr_n)
                            if otp >= 0:
                                otm_st.wait_seq(e, otp)
                            e.tensor_copy(
                                otm[0][:].rearrange("p (a b) -> p a b", b=128)[:, 4 * (dmg % 2):4 * (dmg % 2) + 4, :],
                                ps_dn[:, 0:512].rearrange("p (a b) -> p a b", b=128)).then_inc(sv.h, 1)
                            sv.inc()
                        spill_ev = sv.n

                        if dmg % 2 == 1:
                            @on("sync")
                            def _(e, th=th, tq=tq, dmg=dmg, n=spill_ev):
                                wait(e, sv, n)
                                otm_st.issue(e, ffn_sp[512 * th + 128 * tq:512 * th + 128 * (tq + 1),
                                                       1024 * (dmg // 2):1024 * (dmg // 2 + 1)],
                                             otm[0][:])
                    spill_v = sv.n
                    yield
            while True:
                yield

        ffn_gen = ffn_emit()

    l1q_gen = None
    if stage >= 4:
        def l1q_emit():
            stQ1 = contextlib.ExitStack()
            sbQ1 = mk_sb(stQ1)
            psQ1 = mk_ps(stQ1)
            wq_t = [sbQ1(f"l1w{i}", [128, 2048], BF16, side="right") for i in range(2)]
            qf = sbQ1("l1qf", [128, 2048], F32, side="right")
            wh_t = sbQ1("l1wh", [128, 2048], BF16, side="right")
            qtr = sbQ1("l1tr", [128, 2048], BF16, side="right")
            gsc = sbQ1("l1gs", [128, 32], F32, side="right")
            sclb = sbQ1("l1sc", [128, 32], BF16, side="right")
            rsf = sbQ1("l1rs", [128, 32], F32, side="right")
            psq1 = psQ1("psq1", [128, 2048], BF16)
            ld_l1 = DmaStream(nc, ex, "l1l", 2)
            st_l1 = DmaStream(nc, ex, "l1s", 2)
            jobs = [("gu", t) for t in range(16)] + [("p", t) for t in range(16)]
            qdone = []
            trdone = []
            evdone = []
            for j, (kind, t) in enumerate(jobs):
                W = 2048 if kind == "gu" else 1024
                G = W // 64

                @on("gpsimd")
                def _(e, j=j, kind=kind, t=t, W=W):
                    if j >= 2:
                        wait(e, sv, qdone[j - 2])
                    src_t = wgu_sh if kind == "gu" else wp_sh
                    ld_l1.issue(e, wq_t[j % 2][:, :W], src_t[128 * t:128 * (t + 1), :])

                @on("vector")
                def _(e, j=j, W=W, G=G):
                    ld_l1.wait_seq(e, j)
                    wg = wq_t[j % 2][:, :W].rearrange("p (g k) -> p g k", k=64)
                    e.reduce_sum(gsc[:, :G].unsqueeze(2), wg, AX, apply_absolute_value=True)
                    e.drain()
                    e.tensor_scalar(gsc[:, :G], gsc[:, :G], 1.0 / 64.0, None, ALU.mult)
                    e.drain()
                    e.tensor_copy(sclb[:, :G], gsc[:, :G])
                    e.drain()
                    e.tensor_scalar(sclb[:, :G], sclb[:, :G], 1e-8, None, ALU.max)
                    e.drain()
                    e.reciprocal(rsf[:, :G], sclb[:, :G])
                    e.drain()
                    e.tensor_tensor(qf[:, :W].rearrange("p (g k) -> p g k", k=64), wg,
                                    rsf[:, :G].unsqueeze(2).broadcast_to([128, G, 64]), ALU.mult)
                    e.drain()
                    e.tensor_scalar(qf[:, :W], qf[:, :W], RND, RND, ALU.add, ALU.subtract)
                    e.drain()
                    e.tensor_scalar(qf[:, :W], qf[:, :W], -1.0, 1.0, ALU.max, ALU.min)
                    e.drain()
                    e.tensor_tensor(wh_t[:, :W].rearrange("p (g k) -> p g k", k=64),
                                    qf[:, :W].rearrange("p (g k) -> p g k", k=64),
                                    sclb[:, :G].unsqueeze(2).broadcast_to([128, G, 64]),
                                    ALU.mult).then_inc(sv.h, 1)
                    sv.inc()
                qdone.append(sv.n)

                @on("tensor")
                def _(e, j=j, W=W, n=qdone[j], prev=(evdone[j - 2] if j >= 2 else 0)):
                    wait(e, sv, n)
                    if prev:
                        wait(e, sv, prev)
                    for bb in range(W // 128):
                        ins = e.transpose(psq1[:, 128 * bb:128 * (bb + 1)],
                                          wh_t[:, 128 * bb:128 * (bb + 1)], idb[:])
                    ins.then_inc(stn.h, 1)
                    stn.inc()
                trdone.append(stn.n)

                @on("vector")
                def _(e, j=j, W=W, n=trdone[j], stp=(j - 1)):
                    wait(e, stn, n)
                    if stp >= 0:
                        st_l1.wait_seq(e, stp)
                    e.tensor_copy(qtr[:, :W], psq1[:, :W]).then_inc(sv.h, 1)
                    sv.inc()
                evdone.append(sv.n)

                @on("sync")
                def _(e, j=j, kind=kind, t=t, W=W, n=evdone[j]):
                    wait(e, sv, n)
                    if kind == "gu":
                        dst = qgu_in[:, 128 * t:128 * (t + 1)].rearrange("(a p) q -> p a q", p=128)
                    else:
                        dst = qp_in[:, 128 * t:128 * (t + 1)].rearrange("(a p) q -> p a q", p=128)
                    st_l1.issue(e, dst, qtr[:, :W].rearrange("p (a q) -> p a q", q=128))
                if j % 2 == 1:
                    yield
            @on("gpsimd")
            def _(e, nj=len(jobs)):
                st_l1.wait_seq(e, nj - 1)
                st_l1.wait_seq(e, nj - 2)
                e.collective_compute("AllGather", ALU.bypass, replica_groups=RG8,
                                     ins=[qgu_in.tensor.ap().opt()],
                                     outs=[qgu_out.tensor.ap().opt()]).then_inc(cc["qgu"].h, 1)
                cc["qgu"].inc()
                e.collective_compute("AllGather", ALU.bypass, replica_groups=RG8,
                                     ins=[qp_in.tensor.ap().opt()],
                                     outs=[qp_out.tensor.ap().opt()]).then_inc(cc["qp"].h, 1)
                cc["qp"].inc()
            l1q_state["st_l1"] = st_l1
            stQ1.close()
            while True:
                yield

        l1q_gen = l1q_emit()

    @on("sync")
    def _(e):
        wait(e, sv, qkv_done_v)
        for dst, src in [(tri[:], tri_i[:, :]), (dd[:], dd_i[:, :]),
                         (sln[:], sln_i[:, :]), (btab[:], btab_i[:, :])]:
            ld_attc.issue(e, dst, src)

    @on("gpsimd")
    def _(e):
        wait(e, sv, qkv_done_v)
        ld_wc.issue(e, wos_sb[:].rearrange("p (a q) -> p a q", a=4),
                    wos_i[:, :].rearrange("(a p) q -> p a q", p=128))

    stt_hist = 0
    exp_h = 0
    ctx_h = 0
    div_hist = 0
    den_bcast_hist = 0
    odd_dma = []
    wo_evac_hist = [0, 0]
    att_const_waited = [False]

    gens = [g for g in (ffn_gen,) if g is not None]
    if stage >= 4:
        gens = [ffn_gen, l1q_gen]
        import itertools
        def _seq():
            for _ in range(17):
                next(l1q_gen); yield
            while True:
                next(ffn_gen); yield
        _sq = _seq()
        def drain_ffn(k=1):
            for _ in range(k):
                next(_sq)
    else:
        def drain_ffn(k=1):
            if ffn_gen is not None:
                for _ in range(k):
                    next(ffn_gen)

    for qt in range(8):
        odd_seqs = []
        for h in range(8):
            drain_ffn(0 if qt * 8 + h < 24 else 1)
            kh = h // 4
            nblk = 4 * (qt + 1)
            for jtq in range(qt + 1):
                for jl in range(4):
                    jt = 4 * jtq + jl

                    @on("tensor")
                    def _(e, h=h, qt=qt, jt=jt, kh=kh,
                          sttp=stt_hist, divp=div_hist):
                        if jt == 0 and divp:
                            wait(e, sv, divp)
                        if sttp:
                            wait(e, sv, sttp)
                        if h == 0 and qt == 0 and jt == 0:
                            wait(e, sv, qkv_done_v)
                        e.matmul(ps_s[:, :],
                                 KT[64 * kh:64 * (kh + 1), 128 * jt:128 * (jt + 1)],
                                 QT[64 * kh:64 * (kh + 1), (h % 4) * S + 512 * qt:(h % 4) * S + 512 * (qt + 1)],
                                 start=True, stop=True,
                                 tile_position=(64 * kh, 0)).then_inc(stn.h, 1)
                        stn.inc()
                    mm_n = stn.n

                    @on("vector")
                    def _(e, h=h, jl=jl, mm_n=mm_n, expp=exp_h):
                        wait(e, stn, mm_n)
                        if jl == 0 and expp:
                            wait(e, ss, expp)
                        if not att_const_waited[0]:
                            ld_attc.wait_all(e)
                            att_const_waited[0] = True
                        e.scalar_tensor_tensor(T4[:, 512 * jl:512 * (jl + 1)],
                                               dd[:, 512 * jl:512 * (jl + 1)],
                                               sln[:, h:h + 1], ps_s[:, :],
                                               ALU.mult, ALU.add).then_inc(sv.h, 1)
                        sv.inc()
                    stt_hist = sv.n

                @on("scalar")
                def _(e, h=h, dq=qt - jtq, sttn=stt_hist, ctxp=ctx_h):
                    wait(e, sv, sttn)
                    if ctxp:
                        wait(e, stn, ctxp)
                    col = 8 * h + dq
                    e.activation(T4[:], T4[:], AF.Tanh,
                                 bias=btab[:, col:col + 1], scale=1.0 / 240.0)
                    e.drain()
                    e.activation(E4[:], T4[:], AF.Exp,
                                 bias=cbias[:, 2:3], scale=30.0).then_inc(ss.h, 1)
                    ss.inc()
                exp_h = ss.n

                if jtq == qt:
                    @on("vector")
                    def _(e, expn=exp_h):
                        wait(e, ss, expn)
                        for jl in range(4):
                            if jl > 0:
                                e.memset(E4[:, 512 * jl:512 * jl + 128 * jl], 0.0)
                            ins = e.tensor_mul(E4[:, 512 * jl + 128 * jl:512 * jl + 128 * (jl + 1)],
                                               E4[:, 512 * jl + 128 * jl:512 * jl + 128 * (jl + 1)],
                                               tri[:])
                        ins.then_inc(sv.h, 1)
                        sv.inc()
                    e_ready = ("sv", sv.n)
                else:
                    e_ready = ("ss", exp_h)

                for jl in range(4):
                    jt = 4 * jtq + jl

                    @on("tensor")
                    def _(e, jt=jt, jl=jl, kh=kh, nblk=nblk,
                          e_ready=e_ready, divp=div_hist):
                        if jl == 0:
                            if e_ready[0] == "sv":
                                wait(e, sv, e_ready[1])
                            else:
                                wait(e, ss, e_ready[1])
                        if jt == 0 and divp:
                            wait(e, sv, divp)
                        e.matmul(ps_c[0:65, :], VE[:, 130 * jt + 65 * kh:130 * jt + 65 * kh + 65],
                                 E4[:, 512 * jl:512 * (jl + 1)], start=(jt == 0),
                                 stop=(jt == nblk - 1)).then_inc(stn.h, 1)
                        stn.inc()
                ctx_h = stn.n
            ctx_done = stn.n

            @on("vector")
            def _(e, ctx_done=ctx_done, bcp=den_bcast_hist):
                wait(e, stn, ctx_done)
                if bcp:
                    wait(e, stn, bcp)
                e.tensor_copy(dnrb[64:65, 0:512], ps_c[64:65, :])
                e.drain()
                e.reciprocal(dnrb[64:65, 0:512], dnrb[64:65, 0:512]).then_inc(sv.h, 1)
                sv.inc()
            den_n = sv.n

            @on("tensor")
            def _(e, den_n=den_n):
                wait(e, sv, den_n)
                e.matmul(ps_s[0:64, 0:512], ones[64:65, 0:64], dnrb[64:65, 0:512],
                         start=True, stop=True, tile_position=(64, 0)).then_inc(stn.h, 1)
                stn.inc()
            den_bcast_hist = stn.n

            @on("vector")
            def _(e, h=h, qt=qt, bc_n=den_bcast_hist, cdp=(ctx_dma.seq - 1)):
                wait(e, stn, bc_n)
                e.tensor_copy(dnrb[0:64, 0:512], ps_s[0:64, 0:512])
                e.drain()
                if h % 2 == 0:
                    dst = CTX[0:64, (h // 2) * S + 512 * qt:(h // 2) * S + 512 * (qt + 1)]
                else:
                    if cdp >= 0:
                        ctx_dma.wait_seq(e, cdp)
                    dst = tmpdiv[0][0:64, :]
                e.tensor_mul(dst, ps_c[0:64, :], dnrb[0:64, 0:512]).then_inc(sv.h, 1)
                sv.inc()
            div_hist = sv.n

            if h % 2 == 1:
                @on("sync")
                def _(e, h=h, qt=qt, divn=div_hist):
                    wait(e, sv, divn)
                    seq = ctx_dma.issue(
                        e, CTX[64:128, (h // 2) * S + 512 * qt:(h // 2) * S + 512 * (qt + 1)],
                        tmpdiv[0][0:64, :])
                    odd_seqs.append(seq)

        wo_dep_v = div_hist
        for tt in range(4):
            for dmc in range(4):
                wi = 4 * tt + dmc
                wp = wi % 2

                @on("tensor")
                def _(e, qt=qt, tt=tt, dmc=dmc, wp=wp, wo_dep_v=wo_dep_v,
                      odd_seqs=list(odd_seqs), evp=wo_evac_hist[wp],
                      first=(qt == 0 and tt == 0 and dmc == 0)):
                    wait(e, sv, wo_dep_v)
                    if first:
                        ld_wc.wait_all(e)
                    for sq in odd_seqs:
                        ctx_dma.wait_seq(e, sq)
                    if evp:
                        wait(e, sv, evp)
                    for ac in range(4):
                        ins = e.matmul(ps_w[:, 512 * wp:512 * (wp + 1)],
                                       CTX[:, ac * S + 512 * qt + 128 * tt:ac * S + 512 * qt + 128 * (tt + 1)],
                                       wos_sb[:, ac * D + 512 * dmc:ac * D + 512 * (dmc + 1)],
                                       start=(ac == 0), stop=(ac == 3))
                    ins.then_inc(stn.h, 1)
                    stn.inc()
                wo_n = stn.n

                @on("vector")
                def _(e, wp=wp, wo_n=wo_n, stp=(rs_st.seq - 2)):
                    wait(e, stn, wo_n)
                    if stp >= 0:
                        rs_st.wait_seq(e, stp)
                    e.tensor_copy(wo_out[wp][:], ps_w[:, 512 * wp:512 * (wp + 1)]).then_inc(sv.h, 1)
                    sv.inc()
                wo_evac_hist[wp] = sv.n

                @on("sync")
                def _(e, qt=qt, tt=tt, dmc=dmc, wp=wp, evn=wo_evac_hist[wp]):
                    wait(e, sv, evn)
                    rs_st.issue(e, rs_in[512 * qt + 128 * tt:512 * qt + 128 * (tt + 1),
                                         512 * dmc:512 * (dmc + 1)], wo_out[wp][:])

    drain_ffn(40)

    tap_store("ctx", CTX[:])

    @on("gpsimd")
    def _(e):
        rs_st.wait_seq(e, rs_st.seq - 1)
        rs_st.wait_seq(e, rs_st.seq - 2)
        e.collective_compute("ReduceScatter", ALU.add, replica_groups=RG4,
                             ins=[rs_in.tensor.ap().opt()],
                             outs=[rs_out.tensor.ap().opt()]).then_inc(cc["rs"].h, 1)
        cc["rs"].inc()

    if stage >= 3 and "nq" in tap:
        @on("sync")
        def _(e):
            tbat["nq"].wait_all(e)
    if stage >= 3:
        stF.close()
    stC_right_users_v = sv.n
    stC_users_stn = stn.n
    stC.close()
    stQT.close()
    stD = contextlib.ExitStack()
    sbD = mk_sb(stD)
    psD = mk_ps(stD)
    x1 = sbD("x1", [128, 8 * D], F32)
    bob = sbD("bob", [128, D], F32, side="right")
    stB2 = contextlib.ExitStack()
    ps_b2 = mk_ps(stB2)("ps_b2", [128, 512], F32)
    ld_boc = DmaBatch(nc, ex, "lboc")
    ld_xo = DmaBatch(nc, ex, "lxo")

    @on("sync")
    def _(e, n=stC_right_users_v):
        wait(e, sv, n)
        for t in ("ctx", "q", "k", "v"):
            if t in tbat:
                tbat[t].wait_all(e)
        ld_boc.issue(e, bob[0:1, :], bo_i[:, :])
        for tt in range(8):
            ld_xo.issue(e, x1[:, tt * D:(tt + 1) * D], x_own[128 * tt:128 * (tt + 1), :])

    if "x1a" in tap:
        @on("sync")
        def _(e):
            ld_xo.wait_all(e)
            for tt in range(8):
                tbat["x1a"].issue(e, tap["x1a"][128 * tt:128 * (tt + 1), :], x1[:, tt * D:(tt + 1) * D])

    bo_prev_ev = 0
    for c4 in range(4):
        @on("tensor")
        def _(e, c4=c4, prev=bo_prev_ev, nn=stC_right_users_v):
            if c4 == 0:
                ld_boc.wait_all(e)
                wait(e, sv, nn)
            if prev:
                wait(e, sv, prev)
            e.matmul(ps_b2[:, :], ones[0:1, 0:128], bob[0:1, 512 * c4:512 * (c4 + 1)],
                     start=True, stop=True).then_inc(stn.h, 1)
            stn.inc()
        bo_mm_n = stn.n

        @on("vector")
        def _(e, c4=c4, n=bo_mm_n):
            wait(e, stn, n)
            e.tensor_copy(bob[:, 512 * c4:512 * (c4 + 1)], ps_b2[:, :]).then_inc(sv.h, 1)
            sv.inc()
        bo_prev_ev = sv.n
    bo_ev_n = sv.n

    @on("vector")
    def _(e, n=bo_ev_n):
        wait(e, sv, n)
        ld_xo.wait_all(e)
        if "x1a" in tap:
            tbat["x1a"].wait_all(e)
        for tt in range(8):
            e.tensor_add(x1[:, tt * D:(tt + 1) * D], x1[:, tt * D:(tt + 1) * D], bob[:])
        e.engine_nop().then_inc(sv.h, 1)
        sv.inc()
    x1_bo_n = sv.n
    stB2.close()

    if "x1b" in tap:
        @on("sync")
        def _(e, n=x1_bo_n):
            wait(e, sv, n)
            for tt in range(8):
                tbat["x1b"].issue(e, tap["x1b"][128 * tt:128 * (tt + 1), :], x1[:, tt * D:(tt + 1) * D])

    if stage >= 3:
        @on("gpsimd")
        def _(e, n=x1_bo_n):
            wait(e, sv, n)
            if "x1b" in tap:
                tbat["x1b"].wait_all(e)
            otm_st.wait_seq(e, otm_st.seq - 1)
            for tt in range(8):
                e.dma_start(out=x1[:, tt * D:(tt + 1) * D], in_=ffn_sp[128 * tt:128 * (tt + 1), :],
                            accum_op=ALU.add).then_inc(sgp.h, 16)
                sgp.inc(16)
        x1_f_n = sgp.n
    else:
        x1_f_n = 0

    @on("gpsimd")
    def _(e, n=x1_bo_n, fn=x1_f_n):
        wait(e, sv, n)
        if fn:
            wait(e, sgp, fn)
        wait(e, cc["rs"], cc["rs"].n)
        for tt in range(8):
            e.dma_start(out=x1[:, tt * D:(tt + 1) * D], in_=rs_out[128 * tt:128 * (tt + 1), :],
                        accum_op=ALU.add).then_inc(sgp.h, 16)
            sgp.inc(16)
    x1_acc_n = sgp.n

    if "rsout" in tap:
        @on("sync")
        def _(e):
            wait(e, cc["rs"], cc["rs"].n)
            for tt in range(8):
                tbat["rsout"].issue(e, tap["rsout"][128 * tt:128 * (tt + 1), :],
                                    rs_out[128 * tt:128 * (tt + 1), :])

    if "bob" in tap:
        @on("sync")
        def _(e, n=bo_ev_n):
            wait(e, sv, n)
            tbat["bob"].issue(e, tap["bob"][:, :], bob[:])

    if "ffn0" in tap:
        @on("sync")
        def _(e, n=x1_f_n):
            if n:
                wait(e, sgp, n)
            for tt in range(8):
                tbat["ffn0"].issue(e, tap["ffn0"][128 * tt:128 * (tt + 1), :],
                                   ffn_sp[128 * tt:128 * (tt + 1), :])

    if "x1" in tap:
        @on("sync")
        def _(e, n=x1_acc_n):
            wait(e, sgp, n)
            for tt in range(8):
                tbat["x1"].issue(e, tap["x1"][128 * tt:128 * (tt + 1), :], x1[:, tt * D:(tt + 1) * D])

    if stage >= 4:
        import dataclasses
        xsl = dint("xsl", [S, 512], BF16)
        stL = contextlib.ExitStack()
        sbL = mk_sb(stL)
        xn1_fm = sbL("xn1_fm", [128, 16 * TOK], BF16)
        ones_bf = sbL("ones_bf", [128, 1], BF16)

        @on("vector")
        def _(e, n=x1_acc_n):
            wait(e, sgp, n)
            rs_st.wait_seq(e, rs_st.seq - 1)
            rs_st.wait_seq(e, rs_st.seq - 2)
            e.memset(ones_bf[:], 1.0).then_inc(sv.h, 1); sv.inc()

        stLN = contextlib.ExitStack()
        sbLN = mk_sb(stLN)
        psLN = mk_ps(stLN)
        xnt1 = [sbLN(f"xnt1_{i}", [128, D], BF16, side="right") for i in range(2)]
        pstL = [psLN(f"pstL{i}", [128, 2048], BF16) for i in range(2)]
        st_xg = DmaStream(nc, ex, "sxg", 2)
        ln1_nq = []
        ln1_ev = [0, 0]
        for tt in range(8):
            cb = 8 * (tt % 2)

            @on("scalar")
            def _(e, tt=tt, cb=cb, n=x1_acc_n, evp=ln1_ev[tt % 2]):
                if tt == 0:
                    wait(e, sgp, n)
                if tt >= 2:
                    wait(e, sv, ln1_nq[tt - 2])
                    wait(e, sv, evp)
                    st_xg.wait_seq(e, tt - 2)
                e.activation(xnt1[tt % 2][:], x1[:, tt * D:(tt + 1) * D], AF.Square,
                             accum_out=smalls[:, cb + 1:cb + 2]).then_inc(ss.h, 1)
                ss.inc()
            sq_n = ss.n

            @on("vector")
            def _(e, tt=tt, sq_n=sq_n, cb=cb):
                wait(e, ss, sq_n)
                e.reduce_sum(smalls[:, cb:cb + 1], x1[:, tt * D:(tt + 1) * D], AX)
                e.drain()
                e.tensor_scalar(smalls[:, cb:cb + 1], smalls[:, cb:cb + 1], 1.0 / D, None, ALU.mult)
                e.drain()
                e.tensor_tensor(smalls[:, cb + 2:cb + 3], smalls[:, cb:cb + 1],
                                smalls[:, cb:cb + 1], ALU.mult)
                e.drain()
                e.scalar_tensor_tensor(smalls[:, cb + 3:cb + 4], smalls[:, cb + 1:cb + 2],
                                       1.0 / D, smalls[:, cb + 2:cb + 3],
                                       ALU.mult, ALU.subtract).then_inc(sv.h, 1)
                sv.inc()
            var_n = sv.n

            @on("scalar")
            def _(e, var_n=var_n, cb=cb):
                wait(e, sv, var_n)
                e.activation(smalls[:, cb + 4:cb + 5], smalls[:, cb + 3:cb + 4],
                             AF.Sqrt, bias=cbias[:, 0:1]).then_inc(ss.h, 1)
                ss.inc()
            sqr_n = ss.n

            @on("vector")
            def _(e, tt=tt, sqr_n=sqr_n, cb=cb):
                wait(e, ss, sqr_n)
                e.reciprocal(smalls[:, cb + 5:cb + 6], smalls[:, cb + 4:cb + 5])
                e.drain()
                e.tensor_scalar(xnt1[tt % 2][:], x1[:, tt * D:(tt + 1) * D],
                                smalls[:, cb:cb + 1], smalls[:, cb + 5:cb + 6],
                                ALU.subtract, ALU.mult).then_inc(sv.h, 1)
                sv.inc()
            ln1_nq.append(sv.n)

            @on("tensor")
            def _(e, tt=tt, n=sv.n, prev=ln1_ev[tt % 2]):
                wait(e, sv, n)
                if prev:
                    wait(e, sv, prev)
                for dc in range(16):
                    ins = e.transpose(pstL[tt % 2][:, 128 * dc:128 * (dc + 1)],
                                      xnt1[tt % 2][:, 128 * dc:128 * (dc + 1)], idb[:])
                ins.then_inc(stn.h, 1)
                stn.inc()
            tr_n = stn.n

            @on("vector")
            def _(e, tt=tt, tr_n=tr_n):
                wait(e, stn, tr_n)
                e.tensor_copy(
                    xn1_fm[:].rearrange("p (a b) -> p a b", b=TOK)[:, :, 128 * tt:128 * (tt + 1)],
                    pstL[tt % 2][:].rearrange("p (a b) -> p a b", b=128)).then_inc(sv.h, 1)
                sv.inc()
            ln1_ev[tt % 2] = sv.n

            @on("sync")
            def _(e, tt=tt, n=ln1_nq[tt]):
                wait(e, sv, n)
                st_xg.issue(e, xg_in[128 * tt:128 * (tt + 1), :], xnt1[tt % 2][:])
        xn1_fm_done = sv.n

        if "xn1" in tap:
            @on("sync")
            def _(e):
                for j in range(8):
                    st_xg.wait_seq(e, j)
                tbat["xn1"].issue(e, tap["xn1"][:, :], xg_in[:, :])

        @on("gpsimd")
        def _(e):
            for j in range(8):
                st_xg.wait_seq(e, j)
            e.collective_compute("AllGather", ALU.bypass, replica_groups=RG4,
                                 ins=[xg_in.tensor.ap().opt()],
                                 outs=[xg_out.tensor.ap().opt()]).then_inc(cc["xg"].h, 1)
            cc["xg"].inc()

        ld_xsl = DmaBatch(nc, ex, "lxsl")

        @on("sync")
        def _(e):
            wait(e, cc["xg"], cc["xg"].n)
            pid = nc.partition_id(engines=[mybir.EngineType.SP])
            r512 = (pid % 4) * 512
            for k in range(4):
                ap = xg_out[1024 * k:1024 * (k + 1), 0:512]
                ap2 = dataclasses.replace(ap, offset=ap.offset + r512)
                ld_xsl.issue(e, xsl[1024 * k:1024 * (k + 1), :], ap2)
        stLN.close()

        stF1 = contextlib.ExitStack()
        sbF1 = mk_sb(stF1)
        psF1 = mk_ps(stF1)

        def ffn1_emit():
            ffn1_acc = sbF1("ffn1_acc", [128, 16 * 512], F32)
            p1_grp = sbF1("p1_grp", [128, 8 * 512], BF16)
            gsil1 = sbF1("gsil1", [128, 1024], BF16)
            sq_bf = sbF1("sq_bf", [128, 1024], BF16)
            w1str = [sbF1(f"w1s{i}", [128, 4 * 256], BF16) for i in range(2)]
            wp1t = [sbF1(f"wp1{i}", [128, 4 * 256], BF16) for i in range(2)]
            otm1 = sbF1("otm1", [128, 2048], F32)
            stg = sbF1("stg", [128, 512], F32)
            invc = sbF1("invc", [128, 4], F32)
            ps_gu1 = psF1("ps_gu1", [128, 1024], F32)
            ps_dn1 = psF1("ps_dn1", [128, 1024], F32)
            ps_h1 = psF1("ps_h1", [1, 512], F32)
            ld_w1 = DmaStream(nc, ex, "lw1", 2)
            ld_wp1 = DmaStream(nc, ex, "lwp1", 2)
            ot1_st = DmaStream(nc, ex, "so1", 1)
            ld_w1_cons = []
            ld_wp1_cons = []
            silu_h = 0
            p_h = 0
            dn_h = 0
            acc_h = 0
            spl_v = 0
            hsq_mm = 0

            @on("vector")
            def _(e):
                e.memset(stg[:], 0.0).then_inc(sv.h, 1); sv.inc()

            hsq_read_prev = 0
            for th in range(2):
                first_hsq = True
                for grp in range(8):
                    for s2 in range(4):
                        H = 8 * grp + 2 * s2
                        for gu in range(2):
                            HH = H + 64 * gu
                            for dh in range(4):
                                @on("sync")
                                def _(e, HH=HH, dh=dh,
                                      first=(th == 0 and grp == 0 and s2 == 0 and gu == 0 and dh == 0)):
                                    if first:
                                        wait(e, sv, xn1_fm_done)
                                        wait(e, cc["qgu"], cc["qgu"].n)
                                    if ld_w1.seq >= 2:
                                        wait(e, stn, ld_w1_cons[ld_w1.seq - 2])
                                    ld_w1.issue(e, w1str[ld_w1.seq % 2][:].rearrange("p (a q) -> p a q", q=256),
                                                qgu_out[2048 * (HH // 16) + 512 * dh:
                                                        2048 * (HH // 16) + 512 * (dh + 1),
                                                        128 * (HH % 16):128 * (HH % 16) + 256]
                                                .rearrange("(a p) q -> p a q", p=128))

                                @on("tensor")
                                def _(e, th=th, gu=gu, dh=dh, seq=ld_w1.seq - 1,
                                      sil=silu_h, pn=p_h):
                                    ld_w1.wait_seq(e, seq)
                                    if dh == 0:
                                        if gu == 0 and pn:
                                            wait(e, sv, pn)
                                        if gu == 1 and sil:
                                            wait(e, ss, sil)
                                    for a in range(4):
                                        dc = 4 * dh + a
                                        for i in range(2):
                                            ins = e.matmul(ps_gu1[:, 512 * i:512 * (i + 1)],
                                                           w1str[seq % 2][:, 256 * a + 128 * i:256 * a + 128 * (i + 1)],
                                                           xn1_fm[:, dc * TOK + 512 * th:dc * TOK + 512 * (th + 1)],
                                                           start=(dc == 0), stop=(dc == 15))
                                    ins.then_inc(stn.h, 1)
                                    stn.inc()
                                ld_w1_cons.append(stn.n)
                            gu_n = stn.n
                            if gu == 0:
                                @on("scalar")
                                def _(e, n=gu_n, pn=p_h):
                                    wait(e, stn, n)
                                    if pn:
                                        wait(e, sv, pn)
                                    e.activation(gsil1[:], ps_gu1[:, :], AF.Silu).then_inc(ss.h, 1)
                                    ss.inc()
                                silu_h = ss.n
                            else:
                                @on("vector")
                                def _(e, s2=s2, n=gu_n, sil=silu_h, dmm=dn_h):
                                    wait(e, stn, n)
                                    wait(e, ss, sil)
                                    if dmm:
                                        wait(e, stn, dmm)
                                    e.tensor_mul(p1_grp[:, 1024 * s2:1024 * (s2 + 1)],
                                                 gsil1[:], ps_gu1[:, :]).then_inc(sv.h, 1)
                                    sv.inc()
                                p_h = sv.n

                                @on("scalar")
                                def _(e, s2=s2, n=p_h, hs=hsq_mm):
                                    wait(e, sv, n)
                                    if hs:
                                        wait(e, stn, hs)
                                    e.activation(sq_bf[:], p1_grp[:, 1024 * s2:1024 * (s2 + 1)],
                                                 AF.Square).then_inc(ss.h, 1)
                                    ss.inc()
                                sq_n2 = ss.n

                                @on("tensor")
                                def _(e, n=sq_n2, first=first_hsq, last=(grp == 7 and s2 == 3),
                                      hrp=hsq_read_prev):
                                    wait(e, ss, n)
                                    if first and hrp:
                                        wait(e, sv, hrp)
                                    for i in range(2):
                                        ins = e.matmul(ps_h1[0:1, :], ones_bf[:],
                                                       sq_bf[:, 512 * i:512 * (i + 1)],
                                                       start=(first and i == 0),
                                                       stop=(last and i == 1)).then_inc(stn.h, 1)
                                        stn.inc()
                                hsq_mm = stn.n
                                first_hsq = False
                    p1_all = p_h

                    for dmp in range(8):
                        for ih in range(2):
                            @on("sync")
                            def _(e, grp=grp, dmp=dmp, ih=ih, first=(ld_wp1.seq == 0)):
                                if first:
                                    wait(e, cc["qp"], cc["qp"].n)
                                if ld_wp1.seq >= 2:
                                    wait(e, stn, ld_wp1_cons[ld_wp1.seq - 2])
                                ld_wp1.issue(e, wp1t[ld_wp1.seq % 2][:].rearrange("p (a q) -> p a q", q=256),
                                             qp_out[1024 * grp + 512 * ih:1024 * grp + 512 * (ih + 1),
                                                    256 * dmp:256 * (dmp + 1)]
                                             .rearrange("(a p) q -> p a q", p=128))

                            @on("tensor")
                            def _(e, dmp=dmp, ih=ih, seq=ld_wp1.seq - 1,
                                  pn=p1_all, accp=acc_h, spv=spl_v):
                                ld_wp1.wait_seq(e, seq)
                                if ih == 0:
                                    wait(e, sv, pn)
                                    if accp:
                                        wait(e, sv, accp)
                                    if dmp == 0 and spv:
                                        wait(e, sv, spv)
                                for a in range(4):
                                    i = 4 * ih + a
                                    for kk in range(2):
                                        ins = e.matmul(ps_dn1[:, 512 * kk:512 * (kk + 1)],
                                                       wp1t[seq % 2][:, 256 * a + 128 * kk:256 * a + 128 * (kk + 1)],
                                                       p1_grp[:, 512 * i:512 * (i + 1)],
                                                       start=(i == 0), stop=(i == 7))
                                ins.then_inc(stn.h, 1)
                                stn.inc()
                            ld_wp1_cons.append(stn.n)
                        dn_h = stn.n

                        @on("vector")
                        def _(e, grp=grp, dmp=dmp, n=dn_h, spv=spl_v):
                            wait(e, stn, n)
                            if grp == 0 and spv:
                                wait(e, sv, spv)
                            for kk in range(2):
                                dmc = 2 * dmp + kk
                                if grp == 0:
                                    ins = e.tensor_copy(ffn1_acc[:, 512 * dmc:512 * (dmc + 1)],
                                                        ps_dn1[:, 512 * kk:512 * (kk + 1)])
                                else:
                                    ins = e.tensor_add(ffn1_acc[:, 512 * dmc:512 * (dmc + 1)],
                                                       ffn1_acc[:, 512 * dmc:512 * (dmc + 1)],
                                                       ps_dn1[:, 512 * kk:512 * (kk + 1)])
                            ins.then_inc(sv.h, 1)
                            sv.inc()
                        acc_h = sv.n
                    yield
                acc_done = sv.n

                @on("vector")
                def _(e, n=hsq_mm):
                    wait(e, stn, n)
                    e.tensor_copy(stg[0:1, :], ps_h1[0:1, :])
                    e.drain()
                    e.tensor_scalar(stg[0:1, :], stg[0:1, :], 1.0 / 8192.0, None, ALU.mult).then_inc(sv.h, 1)
                    sv.inc()
                mh = sv.n
                hsq_read_prev = mh

                @on("scalar")
                def _(e, n=mh):
                    wait(e, sv, n)
                    e.activation(stg[0:1, :], stg[0:1, :], AF.Sqrt,
                                 bias=cbias[0:1, 3:4]).then_inc(ss.h, 1)
                    ss.inc()
                sh = ss.n

                @on("vector")
                def _(e, n=sh):
                    wait(e, ss, n)
                    e.reciprocal(stg[0:1, :], stg[0:1, :]).then_inc(sv.h, 1)
                    sv.inc()
                ih = sv.n

                @on("tensor")
                def _(e, n=ih, dnn=dn_h):
                    wait(e, sv, n)
                    wait(e, stn, dnn)
                    for k in range(4):
                        ins = e.transpose(ps_dn1[:, 128 * k:128 * (k + 1)],
                                          stg[:, 128 * k:128 * (k + 1)], idf[:])
                    ins.then_inc(stn.h, 1)
                    stn.inc()
                itr = stn.n

                @on("vector")
                def _(e, n=itr):
                    wait(e, stn, n)
                    for k in range(4):
                        ins = e.tensor_copy(invc[:, k:k + 1], ps_dn1[:, 128 * k:128 * k + 1])
                    ins.then_inc(sv.h, 1)
                    sv.inc()
                inv_done = sv.n

                spill_ev = 0
                for tq in range(4):
                    for dmg in range(4):
                        @on("tensor")
                        def _(e, tq=tq, dmg=dmg, n=acc_done, iv=inv_done, sev=spill_ev):
                            wait(e, sv, n)
                            wait(e, sv, iv)
                            if sev:
                                wait(e, sv, sev)
                            for k in range(4):
                                dmc = 4 * dmg + k
                                ins = e.transpose(ps_dn1[:, 128 * k:128 * (k + 1)],
                                                  ffn1_acc[:, 512 * dmc + 128 * tq:512 * dmc + 128 * (tq + 1)],
                                                  idf[:])
                            ins.then_inc(stn.h, 1)
                            stn.inc()
                        tr_n2 = stn.n

                        @on("vector")
                        def _(e, th=th, tq=tq, dmg=dmg, tr_n2=tr_n2, otp=(ot1_st.seq - 1)):
                            wait(e, stn, tr_n2)
                            if otp >= 0:
                                ot1_st.wait_seq(e, otp)
                            e.tensor_scalar(
                                otm1[:].rearrange("p (a b) -> p a b", b=128)[:, 4 * dmg:4 * dmg + 4, :],
                                ps_dn1[:, 0:512].rearrange("p (a b) -> p a b", b=128),
                                invc[:, tq:tq + 1], None, ALU.mult).then_inc(sv.h, 1)
                            sv.inc()
                        spill_ev = sv.n
                    spl_v = sv.n

                    @on("sync")
                    def _(e, th=th, tq=tq, n=spl_v, fwv=x1_f_n):
                        wait(e, sv, n)
                        wait(e, sgp, fwv)
                        ot1_st.issue(e, ffn_sp[512 * th + 128 * tq:512 * th + 128 * (tq + 1), :],
                                     otm1[:])
                    yield
            ffn1_state["ot1_st"] = ot1_st
            while True:
                yield

        ffn1_state = {}

        def gram_emit():
            stG = contextlib.ExitStack()
            sbG = mk_sb(stG)
            psG = mk_ps(stG)
            G_sb = sbG("G_sb", [128, 2048], F32, side="right")
            E1 = sbG("Eg", [128, 2048], BF16, side="right")
            qtrP = sbG("qtrP", [128, 2048], BF16, side="right")
            xslt = [sbG(f"xsl{i}", [128, 512], BF16, side="right") for i in range(2)]
            xgt = [sbG(f"xgt{i}", [128, 1024], BF16, side="right") for i in range(2)]
            pG = psG("pG", [128, 1024], F32)
            ptb = psG("ptb", [128, 1024], BF16)
            ld_xg2 = DmaStream(nc, ex, "lxg2", 2)
            ld_xs2 = DmaStream(nc, ex, "lxs2", 2)
            st_pt = DmaStream(nc, ex, "spt", 1)
            gmm_hist = []
            ev_h = 0
            exp_h = 0
            ptr_h = 0
            ptev_h = 0
            for ib in range(4):
                for jh in range(2):
                    for ct in range(32):
                        @on("sync")
                        def _(e, ib=ib, jh=jh, ct=ct, first=(ld_xs2.seq == 0)):
                            if first:
                                ld_xsl.wait_all(e)
                            if ld_xs2.seq >= 2:
                                wait(e, stn, gmm_hist[ld_xs2.seq - 2])
                            ld_xs2.issue(e, xslt[ld_xs2.seq % 2][:], xsl[128 * ct:128 * (ct + 1), :])
                            ld_xg2.issue(e, xgt[ld_xg2.seq % 2][:],
                                         xg_out[128 * ct:128 * (ct + 1), 1024 * jh:1024 * (jh + 1)])

                        @on("tensor")
                        def _(e, ib=ib, ct=ct, seq=ld_xs2.seq - 1, evh=ev_h):
                            ld_xs2.wait_seq(e, seq)
                            ld_xg2.wait_seq(e, seq)
                            if ct == 0 and evh:
                                wait(e, sv, evh)
                            for k in range(2):
                                ins = e.matmul(pG[:, 512 * k:512 * (k + 1)],
                                               xslt[seq % 2][:, 128 * ib:128 * (ib + 1)],
                                               xgt[seq % 2][:, 512 * k:512 * (k + 1)],
                                               start=(ct == 0), stop=(ct == 31))
                            ins.then_inc(stn.h, 1)
                            stn.inc()
                        gmm_hist.append(stn.n)
                    g_n = stn.n

                    @on("vector")
                    def _(e, jh=jh, n=g_n, exph=exp_h):
                        wait(e, stn, n)
                        if jh == 0 and exph:
                            wait(e, ss, exph)
                        e.tensor_copy(G_sb[:, 1024 * jh:1024 * (jh + 1)], pG[:, :]).then_inc(sv.h, 1)
                        sv.inc()
                    ev_h = sv.n
                    yield

                @on("vector")
                def _(e, n=ev_h):
                    wait(e, sv, n)
                    e.reduce_max(smalls[:, 32:33], G_sb[:], AX)
                    e.drain()
                    e.tensor_scalar(smalls[:, 33:34], smalls[:, 32:33], -1.0, None,
                                    ALU.mult).then_inc(sv.h, 1)
                    sv.inc()
                m_n = sv.n

                @on("scalar")
                def _(e, n=m_n, ptrh=ptr_h):
                    wait(e, sv, n)
                    if ptrh:
                        wait(e, stn, ptrh)
                    e.activation(E1[:], G_sb[:], AF.Exp, bias=smalls[:, 33:34],
                                 accum_out=smalls[:, 34:35]).then_inc(ss.h, 1)
                    ss.inc()
                exp_h = ss.n

                @on("vector")
                def _(e, n=exp_h):
                    wait(e, ss, n)
                    e.reciprocal(smalls[:, 35:36], smalls[:, 34:35])
                    e.drain()
                    e.tensor_scalar(E1[:], E1[:], smalls[:, 35:36], None, ALU.mult).then_inc(sv.h, 1)
                    sv.inc()
                p_n = sv.n

                for half in range(2):
                    @on("tensor")
                    def _(e, half=half, n=p_n, ptevh=ptev_h):
                        wait(e, sv, n)
                        if ptevh:
                            wait(e, sv, ptevh)
                        for k in range(8):
                            jc = 8 * half + k
                            ins = e.transpose(ptb[:, 128 * k:128 * (k + 1)],
                                              E1[:, 128 * jc:128 * (jc + 1)], idb[:])
                        ins.then_inc(stn.h, 1)
                        stn.inc()
                    ptr_h = stn.n

                    @on("vector")
                    def _(e, half=half, n=ptr_h, stp=(st_pt.seq - 1)):
                        wait(e, stn, n)
                        if stp >= 0 and half == 0:
                            st_pt.wait_seq(e, stp)
                        e.tensor_copy(qtrP[:, 1024 * half:1024 * (half + 1)], ptb[:]).then_inc(sv.h, 1)
                        sv.inc()
                    ptev_h = sv.n

                @on("sync")
                def _(e, ib=ib, n=ptev_h):
                    wait(e, sv, n)
                    st_pt.issue(e, pt_in[2048 * ib:2048 * (ib + 1), :].rearrange("(a p) q -> p a q", p=128),
                                qtrP[:].rearrange("p (a q) -> p a q", q=128))
                yield

            @on("gpsimd")
            def _(e, ns=st_pt.seq):
                st_pt.wait_seq(e, ns - 1)
                e.collective_compute("AllGather", ALU.bypass, replica_groups=RG4,
                                     ins=[pt_in.tensor.ap().opt()],
                                     outs=[pt_out.tensor.ap().opt()]).then_inc(cc["pt"].h, 1)
                cc["pt"].inc()
            stG.close()

            sbX = mk_sb(stF1)
            psX = mk_ps(stF1)
            ptt = [sbX(f"ptt{i}", [128, 256], BF16) for i in range(2)]
            cxe = sbX("cxe", [128, 512], F32)
            ps_cx = psX("ps_cx", [128, 512], F32)
            ps_tr1 = psX("ps_tr1", [128, 512], F32)
            ld_pt2 = DmaStream(nc, ex, "lpt2", 2)
            ld_pt2_cons = []
            cx_ev = 0
            cx_tr = 0
            cx_add = 0
            for I in range(16):
                for cc2 in range(2):
                    for jp in range(8):
                        @on("sync")
                        def _(e, I=I, jp=jp, first=(ld_pt2.seq == 0)):
                            if first:
                                wait(e, cc["pt"], cc["pt"].n)
                            if ld_pt2.seq >= 2:
                                wait(e, stn, ld_pt2_cons[ld_pt2.seq - 2])
                            base = 8192 * (I // 4) + (16 * (I % 4) + 2 * jp) * 128
                            ld_pt2.issue(e, ptt[ld_pt2.seq % 2][:].rearrange("p (a q) -> p a q", q=128),
                                         pt_out[base:base + 256, :].rearrange("(a p) q -> p a q", p=128))

                        @on("tensor")
                        def _(e, I=I, cc2=cc2, jp=jp, seq=ld_pt2.seq - 1, cxv=cx_ev):
                            ld_pt2.wait_seq(e, seq)
                            if jp == 0 and cxv:
                                wait(e, sv, cxv)
                            for k in range(2):
                                jc = 2 * jp + k
                                ins = e.matmul(ps_cx[:, :],
                                               ptt[seq % 2][:, 128 * k:128 * (k + 1)],
                                               xn1_fm[:, jc * TOK + 512 * cc2:jc * TOK + 512 * (cc2 + 1)],
                                               start=(jp == 0 and k == 0), stop=(jp == 7 and k == 1))
                            ins.then_inc(stn.h, 1)
                            stn.inc()
                        ld_pt2_cons.append(stn.n)
                    cmm = stn.n

                    @on("vector")
                    def _(e, n=cmm, ctr=cx_tr):
                        wait(e, stn, n)
                        if ctr:
                            wait(e, stn, ctr)
                        e.tensor_copy(cxe[:], ps_cx[:, :]).then_inc(sv.h, 1)
                        sv.inc()
                    cx_ev = sv.n

                    @on("tensor")
                    def _(e, n=cx_ev, ca=cx_add):
                        wait(e, sv, n)
                        if ca:
                            wait(e, sv, ca)
                        for k in range(4):
                            ins = e.transpose(ps_tr1[:, 128 * k:128 * (k + 1)],
                                              cxe[:, 128 * k:128 * (k + 1)], idf[:])
                        ins.then_inc(stn.h, 1)
                        stn.inc()
                    cx_tr = stn.n

                    @on("vector")
                    def _(e, I=I, cc2=cc2, n=cx_tr):
                        wait(e, stn, n)
                        for k in range(4):
                            tt = 4 * cc2 + k
                            ins = e.tensor_add(x1[:, tt * D + 128 * I:tt * D + 128 * (I + 1)],
                                               x1[:, tt * D + 128 * I:tt * D + 128 * (I + 1)],
                                               ps_tr1[:, 128 * k:128 * (k + 1)])
                        ins.then_inc(sv.h, 1)
                        sv.inc()
                    cx_add = sv.n
                    yield
            gram_state["cx_done"] = cx_add
            while True:
                yield

        gram_state = {}
        g1 = ffn1_emit()
        g2 = gram_emit()
        next(g1)
        for _ in range(80):
            next(g2)
            next(g1)
        for _ in range(40):
            next(g1)
        for _ in range(40):
            next(g2)

        ffn1_last_v = sv.n

        @on("gpsimd")
        def _(e, n=ffn1_last_v):
            wait(e, sv, n)
            ot1 = ffn1_state["ot1_st"]
            ot1.wait_seq(e, ot1.seq - 1)
            for tt in range(8):
                e.dma_start(out=x1[:, tt * D:(tt + 1) * D], in_=ffn_sp[128 * tt:128 * (tt + 1), :],
                            accum_op=ALU.add).then_inc(sgp.h, 16)
                sgp.inc(16)
        fin_n = sgp.n

        @on("sync")
        def _(e, n=fin_n):
            wait(e, sgp, n)
            for tt in range(8):
                st_out.issue(e, out_own[128 * tt:128 * (tt + 1), :], x1[:, tt * D:(tt + 1) * D])
            st_out.wait_all(e)
            for tb in tbat.values():
                tb.wait_all(e)
        stF1.close()
        stL.close()
        stD.close()
        ex.close()
        return nc

    if stage <= 3:
        @on("sync")
        def _(e, n=x1_acc_n):
            wait(e, sgp, n)
            for tt in range(8):
                st_out.issue(e, out_own[128 * tt:128 * (tt + 1), :], x1[:, tt * D:(tt + 1) * D])
            st_out.wait_all(e)
            for tb in tbat.values():
                tb.wait_all(e)
        stD.close()
        ex.close()
        return nc

    stD.close()
    ex.close()
    return nc


def kernel_stage(inputs, stage=1, taps=()):
    nc = build_nc(stage=stage, taps=taps)
    in_maps = [host_inputs(inputs, c) for c in range(8)]
    res = run_bass_kernel_spmd(nc, in_maps, core_ids=list(range(8)))
    return res


def kernel(**inputs):
    nc = build_nc(stage=4)
    in_maps = [host_inputs(inputs, c) for c in range(8)]
    res = run_bass_kernel_spmd(nc, in_maps, core_ids=list(range(8)))
    out = np.empty((2, 4096, 2048), np.float32)
    for c in range(8):
        b, r = c // 4, c % 4
        out[b, 1024 * r:1024 * (r + 1)] = res.results[c]["out_own"]
    return out
